# version 04
# baseline (speedup 1.0000x reference)
import numpy as np
import ml_dtypes
import concourse.bass as bass
import concourse.mybir as mybir
from concourse.bass_utils import run_bass_kernel_spmd
from concourse.alu_op_type import AluOpType as ALU

F32 = mybir.dt.float32
BF16 = mybir.dt.bfloat16
I32 = mybir.dt.int32
AF = mybir.ActivationFunctionType
AX = mybir.AxisListType.X

D = 2048
NT = 2048
NC = 16
TT = 512
NTT = NT // TT
H = 16
DH = 128
NE = 32
EI = 768
HARM = 64
GI = 5504
GOB = GI // 128
EPS = 1e-5
BF = ml_dtypes.bfloat16
SCALE = float(DH ** -0.5)



class Tok:
    __slots__ = ("eng", "idx", "step")
    def __init__(self, eng, idx, step):
        self.eng, self.idx, self.step = eng, idx, step


class _Eng:
    def __init__(self, name):
        self.name = name
        self.ops = []
        self.need_inc = set()
        self.sem = None


class Prog:
    def __init__(self, nc):
        self.nc = nc
        self.e = {n: _Eng(n) for n in ("pe", "dve", "act", "pool", "sync")}
        self.pending = {}
        self.consumed = set()

    def fence(self):
        self.pending = {n: Tok(n, len(st.ops) - 1, 16 if st.ops[-1][2] else 1)
                        for n, st in self.e.items() if st.ops}
        self.consumed = set()

    def op(self, eng, fn, waits=(), dma=False, strict=()):
        st = self.e[eng]
        ws = []
        for w in waits:
            if w is None:
                continue
            if isinstance(w, (list, tuple)):
                ws += [x for x in w if x is not None]
            else:
                ws.append(w)
        sws = [w for w in (strict if isinstance(strict, (list, tuple)) else [strict])
               if w is not None]
        if self.pending and eng not in self.consumed:
            ws += [t for n, t in self.pending.items() if n != eng]
            self.consumed.add(eng)
        for w in ws + sws:
            self.e[w.eng].need_inc.add(w.idx)
        st.ops.append((fn, ws, dma, sws))
        return Tok(eng, len(st.ops) - 1, 16 if dma else 1)

    def dma(self, queue, dst, src, waits=()):
        eng = {"sync": "sync", "act": "act", "pool": "pool"}[queue]
        return self.op(eng, lambda e, dst=dst, src=src: e.dma_start(dst, src),
                       waits=waits, dma=True)

    def chain(self, eng, fns, waits=()):
        t = None
        for i, fn in enumerate(fns):
            t = self.op(eng, fn, waits=(waits if i == 0 else ()),
                        strict=([t] if t is not None else ()))
        return t

    def realize(self):
        nc = self.nc
        ctxs = []
        for n, st in self.e.items():
            if st.ops:
                cm = nc.semaphore(f"sem_{n}")
                st.sem = cm.__enter__()
                ctxs.append(cm)
        cum = {}
        for n, st in self.e.items():
            for i, (fn, ws, dma, sws) in enumerate(st.ops):
                if dma:
                    st.need_inc.add(i)
            vals, c = {}, 0
            for i, (fn, ws, dma, sws) in enumerate(st.ops):
                if i in st.need_inc:
                    c += 16 if dma else 1
                vals[i] = c
            cum[n] = vals
        engmap = {"pe": "tensor", "dve": "vector", "act": "scalar",
                  "pool": "gpsimd", "sync": "sync"}
        blk = nc.Block()
        blk.__enter__()
        for n, st in self.e.items():
            if not st.ops:
                continue
            def make(st=st):
                def body(e):
                    for i, (fn, ws, dma, sws) in enumerate(st.ops):
                        wvals = {}
                        for w in ws:
                            if w.eng == st.name:
                                continue
                            v = cum[w.eng][w.idx]
                            if wvals.get(w.eng, -1) < v:
                                wvals[w.eng] = v
                        for w in sws:
                            v = cum[w.eng][w.idx]
                            if wvals.get(w.eng, -1) < v:
                                wvals[w.eng] = v
                        for we, wv in wvals.items():
                            e.wait_ge(self.e[we].sem, wv)
                        inst = fn(e)
                        if i in st.need_inc:
                            inst.then_inc(st.sem, 16 if dma else 1)
                return body
            getattr(blk, engmap[n])(make())
        blk.__exit__(None, None, None)
        return ctxs


class Arena:
    def __init__(self, nc, kb=206):
        self.bytes = kb * 1024
        self.cm = nc.sbuf_tensor("arena", [128, self.bytes], mybir.dt.uint8)
        self.th = self.cm.__enter__()
        self.views = {F32: self.th.bitcast(F32).ap(),
                      BF16: self.th.bitcast(BF16).ap(),
                      I32: self.th.bitcast(I32).ap()}
        self.off = 0
        self.marks = []
        self.peak = 0

    def alloc(self, cols, dtype, parts=128, name=None):
        size = {F32: 4, BF16: 2, I32: 4}[dtype]
        align = 512 if cols * size >= 512 else 64
        self.off = (self.off + align - 1) // align * align
        self.log = getattr(self, "log", [])
        self.log.append((self.off, cols * size, name))
        assert self.off + cols * size <= self.bytes, \
            f"arena overflow: need {self.off + cols*size}, have {self.bytes}"
        v = self.views[dtype][0:parts, self.off // size: self.off // size + cols]
        self.off += cols * size
        self.peak = max(self.peak, self.off)
        return v

    def push(self):
        self.marks.append(self.off)

    def pop(self):
        self.off = self.marks.pop()


class Ctx:
    def __init__(self, nc):
        self.nc = nc
        self.p = Prog(nc)
        self.ar = Arena(nc)
        self._pc = []
        self.pb = []
        self.pbbf = []
        for i in range(8):
            cm = nc.psum_tensor(f"pb{i}", [128, 512], F32)
            h = cm.__enter__()
            self.pb.append(h)
            self.pbbf.append(h.bitcast(BF16))
            self._pc.append(cm)
        self.dram = {}
        self.tok = {}
        self.c = {}
        self.bankw = [None] * 8
        self.bankr = [dict() for _ in range(8)]

    def pk(self, i, parts=128, cols=512):
        return self.pb[i][0:parts, 0:cols]

    def pkbf(self, i, parts=128, cols=512):
        return self.pbbf[i][0:parts, 0:cols]

    def war(self, b):
        return list(self.bankr[b].values())

    def wrote(self, b, tok):
        self.bankw[b] = tok
        self.bankr[b] = {}

    def raw(self, b):
        return [self.bankw[b]]

    def read(self, b, tok):
        self.bankr[b][tok.eng] = tok


def declare_dram(C, specs, outputs=()):
    for name, shape, dt_ in specs:
        kind = "ExternalOutput" if name in outputs else "ExternalInput"
        C.dram[name] = C.nc.dram_tensor(name, list(shape), dt_, kind=kind).ap()



def stage_inputs(inputs):
    S = {}
    x = np.asarray(inputs["L0_attn_x"])
    S["x_fm"] = [np.ascontiguousarray(x[b].T) for b in range(8)]
    S["gamma_t"] = np.ascontiguousarray(
        np.asarray(inputs["L0_attn_gamma"]).reshape(NC, 128).T)
    S["wq"] = np.ascontiguousarray(inputs["L0_attn_Wq"])
    S["wkv"] = np.ascontiguousarray(inputs["L0_attn_Wkv"])
    S["wo"] = np.ascontiguousarray(inputs["L0_attn_Wo"])
    S["gwt"] = np.ascontiguousarray(np.asarray(inputs["L0_ffn_gate_weight"]).T)
    S["wgu_t"] = np.ascontiguousarray(
        np.asarray(inputs["L0_ffn_gate_up_proj"]).transpose(0, 2, 1))
    S["wd_t"] = np.ascontiguousarray(
        np.asarray(inputs["L0_ffn_down_proj"]).transpose(0, 2, 1))
    for nm in ("q", "k", "v", "o"):
        S[f"basis_{nm}_T"] = np.ascontiguousarray(np.asarray(inputs[f"L1_attn_basis_{nm}"]).T)
        S[f"phase_{nm}_T"] = np.ascontiguousarray(np.asarray(inputs[f"L1_attn_phase_{nm}"]).T)
        S[f"amp_{nm}_T"] = np.ascontiguousarray(np.asarray(inputs[f"L1_attn_amp_{nm}"]).T)
    S["lut"] = np.ascontiguousarray(np.asarray(inputs["L1_ffn_lut"]).reshape(1, 16))
    gt = np.asarray(inputs["L1_ffn_gate_idx"]).T
    ut = np.asarray(inputs["L1_ffn_up_idx"]).T
    dt_ = np.asarray(inputs["L1_ffn_down_idx"]).T
    def tile_hw(m, nr, ncb):
        return np.ascontiguousarray(
            m.reshape(nr, 128, ncb, 128).transpose(2, 0, 1, 3)).astype(np.int32)
    S["gidx"] = tile_hw(gt, NC, GOB)
    S["uidx"] = tile_hw(ut, NC, GOB)
    S["didx"] = tile_hw(dt_, GOB, NC)
    S["ident_bf"] = np.eye(128, dtype=BF)
    S["ident_f"] = np.eye(128, dtype=np.float32)
    cb = np.zeros((128, 128), np.float32)
    for p in range(128):
        cb[p, :p] = -1e30
    S["causal_bias"] = cb
    inv = 1.0 / (10000.0 ** (np.arange(0, DH, 2, dtype=np.float32) / DH))
    S["invfreq2"] = np.concatenate([inv, inv]).reshape(128, 1).astype(np.float32)
    S["iota_row"] = np.arange(NT, dtype=np.float32).reshape(1, NT)
    S["invfreq_row"] = np.concatenate([inv, inv]).reshape(1, 128).astype(np.float32)
    return S


DRAM_CONST = [
    ("gamma_t", (128, NC), F32), ("ident_bf", (128, 128), BF16),
    ("ident_f", (128, 128), F32), ("causal_bias", (128, 128), F32),
    ("invfreq2", (128, 1), F32), ("lut", (1, 16), F32), ("iota_row", (1, NT), F32),
    ("invfreq_row", (1, 128), F32),
]



def emit_prologue(C):
    p, ar, c = C.p, C.ar, C.c
    c["ones_col_bf"] = ar.alloc(1, BF16)
    c["ones_row_bf"] = ar.alloc(128, BF16, parts=1)
    c["ones_row_f"] = ar.alloc(128, F32, parts=1)
    c["eps1"] = ar.alloc(1, F32, parts=1)
    p.op("dve", lambda e: e.memset(c["ones_col_bf"], 1.0))
    p.op("dve", lambda e: e.memset(c["ones_row_bf"], 1.0))
    p.op("dve", lambda e: e.memset(c["ones_row_f"], 1.0))
    tm = p.op("dve", lambda e: e.memset(c["eps1"], EPS))
    C.tok["c_dve"] = [tm]
    loads = []
    for name, cols, dt_ in (("ident_bf", 128, BF16), ("ident_f", 128, F32),
                            ("causal", 128, F32), ("gamma", NC, F32),
                            ("invfreq2", 1, F32)):
        src = {"causal": "causal_bias", "gamma": "gamma_t"}.get(name, name)
        if src in C.dram:
            c[name] = ar.alloc(cols, dt_)
            loads.append(p.dma("sync", c[name], C.dram[src]))
    if "lut" in C.dram:
        c["lut_row"] = ar.alloc(16, F32, parts=1)
        tl = p.dma("sync", c["lut_row"], C.dram["lut"])
        c["lutcol"] = ar.alloc(16, F32)
        mm = p.op("pe", lambda e: e.matmul(C.pk(7, 128, 16), c["ones_row_f"], c["lut_row"],
                                           start=True, stop=True),
                  waits=[tl, tm] + C.war(7))
        C.wrote(7, mm)
        tc = p.op("dve", lambda e: e.tensor_copy(c["lutcol"], C.pk(7, 128, 16)),
                  waits=C.raw(7))
        C.read(7, tc)
        C.tok["lutcol"] = tc
        loads.append(tl)
    C.tok["c_loads"] = loads
    C.tok["consts"] = [tm] + loads


def emit_ln(C, xin, T, dst, sc, deps, double_gamma=False, pb=(4, 5), src_bf=False):
    p, c = C.p, C.c
    bA, bB = pb

    def stats(src, sdeps, is_bf):
        s1, s2 = C.pk(bA, 1, T), C.pk(bB, 1, T)
        warA, warB = C.war(bA), C.war(bB)
        mmtok = [None, None]
        for ci in range(NC):
            sl = ci % 2
            war_sl = [mmtok[1]] if ci >= 2 else []
            if is_bf:
                xb = src(ci)
                tc_ = None
            else:
                tc_ = p.op("act", lambda e, ci=ci, sl=sl: e.activation(
                    sc["xb2"][sl], src(ci), AF.Copy), waits=sdeps(ci) + war_sl)
                xb = None
            ts_ = p.op("act", lambda e, ci=ci, sl=sl: e.activation(
                sc["sq2"][sl], src(ci), AF.Square),
                waits=(sdeps(ci) + war_sl) if is_bf else [])
            m1 = p.op("pe", lambda e, ci=ci, sl=sl, xb=xb: e.matmul(
                s1, c["ones_col_bf"], xb if xb is not None else sc["xb2"][sl],
                start=(ci == 0), stop=(ci == NC - 1)),
                waits=([ts_] if is_bf else [tc_]) + ((warA + C.tok["c_dve"]) if ci == 0 else []))
            m2 = p.op("pe", lambda e, ci=ci, sl=sl: e.matmul(
                s2, c["ones_col_bf"], sc["sq2"][sl],
                start=(ci == 0), stop=(ci == NC - 1)),
                waits=[ts_] + (warB if ci == 0 else []))
            mmtok = [m1, m2]
        C.wrote(bA, mmtok[0])
        C.wrote(bB, mmtok[1])
        mu, e2, var, rstd = sc["st"]
        t1 = p.op("dve", lambda e: e.tensor_scalar(mu, s1, -1.0 / D, None, ALU.mult),
                  waits=C.raw(bA) + C.raw(bB))
        p.op("dve", lambda e: e.tensor_scalar(e2, s2, 1.0 / D, None, ALU.mult))
        p.op("dve", lambda e: e.tensor_tensor(var, mu, mu, ALU.mult))
        t4 = p.op("dve", lambda e: e.tensor_tensor(var, e2, var, ALU.subtract))
        C.read(bA, t4)
        C.read(bB, t4)
        t5 = p.op("act", lambda e: e.activation(var, var, AF.Ln, bias=c["eps1"][0:1, 0:1]),
                  waits=[t4] + C.tok["c_dve"])
        t6 = p.op("act", lambda e: e.activation(rstd, var, AF.Exp, scale=-0.5))
        b1 = p.op("pe", lambda e: e.matmul(C.pk(bA, 128, T), c["ones_row_f"], mu,
                                           start=True, stop=True),
                  waits=[t6] + C.war(bA))
        C.wrote(bA, b1)
        b2 = p.op("pe", lambda e: e.matmul(C.pk(bB, 128, T), c["ones_row_f"], rstd,
                                           start=True, stop=True),
                  waits=[t6] + C.war(bB))
        C.wrote(bB, b2)
        return C.pk(bA, 128, T), C.pk(bB, 128, T), [b1, b2]

    out = []
    if not double_gamma:
        mu_b, r_b, bc = stats(xin, deps, src_bf)
        last = None
        for ci in range(NC):
            p.op("dve", lambda e, ci=ci: e.tensor_tensor(sc["tmpf"], xin(ci), mu_b, ALU.add),
                 waits=deps(ci) + bc)
            last = p.op("dve", lambda e, ci=ci: e.tensor_tensor(dst(ci), sc["tmpf"], r_b, ALU.mult))
            out.append(last)
        C.read(bA, last)
        C.read(bB, last)
        return out
    ln1 = sc["ln1"]
    mu_b, r_b, bc = stats(xin, deps, src_bf)
    l1t = []
    for ci in range(NC):
        p.op("dve", lambda e, ci=ci: e.tensor_tensor(sc["tmpf"], xin(ci), mu_b, ALU.add),
             waits=deps(ci) + bc)
        l1t.append(p.op("dve", lambda e, ci=ci: e.tensor_tensor(ln1(ci), sc["tmpf"], r_b, ALU.mult)))
    C.read(bA, l1t[-1])
    C.read(bB, l1t[-1])
    mu2, r2, bc2 = stats(ln1, lambda ci: [l1t[ci]], True)
    last = None
    for ci in range(NC):
        p.op("dve", lambda e, ci=ci: e.tensor_tensor(sc["tmpf"], ln1(ci), mu2, ALU.add),
             waits=[l1t[ci]] + bc2)
        p.op("dve", lambda e, ci=ci: e.tensor_tensor(sc["tmpf"], sc["tmpf"], r2, ALU.mult))
        last = p.op("dve", lambda e, ci=ci: e.tensor_scalar(
            dst(ci), sc["tmpf"], c["gamma"][:, ci:ci + 1], None, ALU.mult))
        out.append(last)
    C.read(bA, last)
    C.read(bB, last)
    return out


def alloc_ln_scratch(C, T, double=False):
    ar = C.ar
    sc = {
        "xb2": [ar.alloc(T, BF16) for _ in range(2)],
        "sq2": [ar.alloc(T, BF16) for _ in range(2)],
        "tmpf": ar.alloc(T, F32),
    }
    strow = ar.alloc(4 * T, F32, parts=1)
    sc["st"] = [strow[0:1, i * T:(i + 1) * T] for i in range(4)]
    sc["strow"] = strow
    if double:
        ln1b = ar.alloc(NC * T, BF16)
        sc["ln1"] = lambda ci: ln1b[:, ci * T:(ci + 1) * T]
    return sc


def emit_rotary_tables(C):
    p, ar, c = C.p, C.ar, C.c
    c["cos_t"] = ar.alloc(NT, F32)
    c["sinsgn_t"] = ar.alloc(NT, F32)
    ar.push()
    fr = ar.alloc(NT, F32, name="fr")
    red = ar.alloc(NT, F32, name="red")
    red2 = ar.alloc(NT, F32, name="red2")
    n32 = ar.alloc(NT, I32, name="n32")
    nf = ar.alloc(NT, F32, name="nf")
    sgn = ar.alloc(1, F32)
    iorow = ar.alloc(NT, F32, parts=1)
    ifrow = ar.alloc(128, F32, parts=1)
    tir = p.dma("sync", iorow, C.dram["iota_row"])
    tif = p.dma("sync", ifrow, C.dram["invfreq_row"])
    pos_toks = []
    for j4 in range(4):
        mmi = p.op("pe", lambda e, j4=j4: e.matmul(
            C.pk(6), ifrow, iorow[0:1, j4 * 512:(j4 + 1) * 512],
            start=True, stop=True),
            waits=[tir, tif] + C.war(6))
        C.wrote(6, mmi)
        tpc = p.op("dve", lambda e, j4=j4: e.tensor_copy(
            fr[:, j4 * 512:(j4 + 1) * 512], C.pk(6)), waits=C.raw(6))
        C.read(6, tpc)
        pos_toks.append(tpc)
    tfr = pos_toks[-1]
    TWO_PI = 2.0 * np.pi
    PI_HI = np.float32(TWO_PI)
    PI_LO = float(np.float64(TWO_PI) - np.float64(PI_HI))

    def reduce_to(dst, shift, war=()):
        p.op("dve", lambda e: e.tensor_scalar(red, fr, 1.0 / TWO_PI, shift / TWO_PI,
                                              ALU.mult, ALU.add), waits=war)
        p.op("dve", lambda e: e.tensor_copy(n32, red))
        p.op("dve", lambda e: e.tensor_copy(nf, n32))
        p.op("dve", lambda e: e.scalar_tensor_tensor(red2, nf, -float(PI_HI), fr,
                                                     ALU.mult, ALU.add))
        if shift:
            p.op("dve", lambda e: e.tensor_scalar(red2, red2, shift, None, ALU.add))
        p.op("dve", lambda e: e.scalar_tensor_tensor(dst, nf, -PI_LO, red2,
                                                     ALU.mult, ALU.add))
        p.op("dve", lambda e: e.tensor_scalar(nf, dst, float(np.pi), float(TWO_PI),
                                              ALU.is_gt, ALU.mult))
        return p.op("dve", lambda e: e.tensor_tensor(dst, dst, nf, ALU.subtract))

    t1 = reduce_to(red, 0.0)
    tsin = p.op("act", lambda e: e.activation(c["sinsgn_t"], red, AF.Sin), waits=[t1])
    t2 = reduce_to(red2, float(np.pi / 2), war=[tsin])
    tcos = p.op("act", lambda e: e.activation(c["cos_t"], red2, AF.Sin), waits=[t2])
    p.op("dve", lambda e: e.memset(sgn[0:64, :], -1.0))
    p.op("dve", lambda e: e.memset(sgn[64:128, :], 1.0))
    tss = p.op("dve", lambda e: e.tensor_scalar(c["sinsgn_t"], c["sinsgn_t"], sgn[:, 0:1],
                                                None, ALU.mult), waits=[tsin])
    C.tok["rotary"] = [tss, tcos]
    C.tok["rotary_temps_done"] = [tss, tcos, tfr]
    ar.pop()



def emit_phaseA(C, a1=True, xin_name="x_fm", xout_name="x1", pump=None):
    p, ar, c = C.p, C.ar, C.c
    x_in = C.dram[xin_name]
    x1_out = C.dram[xout_name]
    pump = pump or (lambda: None)
    ar.push()
    kfull = ar.alloc(NT, BF16, name="kfull")
    vtm = ar.alloc(NT, BF16, name="vtm")
    kstat = ar.alloc(8, F32, parts=1, name="kstat")
    emit_rotary_tables(C)
    sc = alloc_ln_scratch(C, TT, double=True)
    x_tile = ar.alloc(NC * TT, F32, name="x_tile")
    xn_tile = ar.alloc(NC * TT, BF16, name="xn_tile")
    wkvf = ar.alloc(2 * 256, F32)
    wkv_full = ar.alloc(NC * 256, BF16)
    wstream = [ar.alloc(NC * 128, F32) for _ in range(2)]
    wstream_bf = [ar.alloc(NC * 128, BF16) for _ in range(2)]
    vbf = ar.alloc(TT, BF16)
    qf = ar.alloc(TT, F32)
    qsh = ar.alloc(TT, F32)
    qsq = ar.alloc(TT, BF16)
    ptil = ar.alloc(NC * TT, BF16, name="ptil")
    lacc = ar.alloc(TT, F32, name="lacc")
    laccb = ar.alloc(TT, BF16)
    rl = ar.alloc(TT, F32, parts=1)
    rls = ar.alloc(TT, F32)
    ocat = ar.alloc(H * TT, BF16, name="ocat")
    qh2 = [ar.alloc(TT, BF16) for _ in range(2)]
    negbb2 = [ar.alloc(TT, BF16, parts=1) for _ in range(2)]
    nqrow = [sc["strow"][0:1, 2 * TT:3 * TT], sc["strow"][0:1, 3 * TT:4 * TT]]
    kf = ar.alloc(TT, F32)
    ksh = ar.alloc(TT, F32)
    youts = [ar.alloc(TT, F32) for _ in range(2)]

    xin = lambda ci: x_tile[:, ci * TT:(ci + 1) * TT]
    xnf = lambda ci: xn_tile[:, ci * TT:(ci + 1) * TT]
    C.dbg = dict(kfull=kfull, vtm=vtm, kstat=kstat, x_tile=x_tile, xn_tile=xn_tile,
                 ptil=ptil, lacc=lacc, ocat=ocat)

    twkv = []
    for ci in range(NC):
        sl = ci % 2
        war = [twkv[ci - 2]] if ci >= 2 else list(C.tok["rotary_temps_done"])
        tl = p.dma("sync", wkvf[:, sl * 256:(sl + 1) * 256],
                   C.dram["wkv"][ci * 128:(ci + 1) * 128, :], waits=war)
        tc_ = p.op("dve", lambda e, ci=ci, sl=sl: e.tensor_copy(
            wkv_full[:, ci * 256:(ci + 1) * 256], wkvf[:, sl * 256:(sl + 1) * 256]),
            waits=[tl])
        twkv.append(tc_)

    prev_x_readers = list(C.tok["rotary_temps_done"])
    ksq_toks = []
    kv_ready = []
    for ti in range(NTT):
        tx = p.dma("sync", x_tile.rearrange("p (c t) -> p c t", c=NC),
                   x_in[:, ti * TT:(ti + 1) * TT].rearrange("(c p) t -> p c t", p=128),
                   waits=prev_x_readers)
        xn_toks = emit_ln(C, xin, TT, xnf, sc, lambda ci: [tx], double_gamma=True)
        prev_x_readers = [xn_toks[-1]]
        mk = mv = None
        for ci in range(NC):
            mk = p.op("pe", lambda e, ci=ci: e.matmul(
                C.pk(0), wkv_full[:, ci * 256:ci * 256 + 128], xnf(ci),
                start=(ci == 0), stop=(ci == NC - 1)),
                waits=[xn_toks[ci], twkv[ci]] + (C.war(0) if ci == 0 else []))
            mv = p.op("pe", lambda e, ci=ci: e.matmul(
                C.pk(1), wkv_full[:, ci * 256 + 128:ci * 256 + 256], xnf(ci),
                start=(ci == 0), stop=(ci == NC - 1)),
                waits=(C.war(1) if ci == 0 else []))
        C.wrote(0, mk)
        C.wrote(1, mv)
        csl = slice(ti * TT, (ti + 1) * TT)
        t1 = p.op("dve", lambda e: e.tensor_copy(kf, C.pk(0)), waits=C.raw(0) + C.tok["rotary"])
        C.read(0, t1)
        p.op("dve", lambda e: e.tensor_copy(ksh[0:64, :], kf[64:128, :]))
        p.op("dve", lambda e: e.tensor_copy(ksh[64:128, :], kf[0:64, :]))
        p.op("dve", lambda e, csl=csl: e.tensor_tensor(kf, kf, c["cos_t"][:, csl], ALU.mult))
        p.op("dve", lambda e, csl=csl: e.tensor_tensor(ksh, ksh, c["sinsgn_t"][:, csl], ALU.mult))
        t6 = p.op("dve", lambda e, csl=csl: e.tensor_tensor(kfull[:, csl], kf, ksh, ALU.add))
        kv_ready.append(t6)
        t7 = p.op("act", lambda e, csl=csl: e.activation(qsq, kfull[:, csl], AF.Square), waits=[t6])
        m3 = p.op("pe", lambda e: e.matmul(C.pk(2, 1, TT), c["ones_col_bf"], qsq,
                                           start=True, stop=True),
                  waits=[t7] + C.war(2))
        C.wrote(2, m3)
        t8 = p.op("dve", lambda e, ti=ti: e.reduce_max(kstat[0:1, ti:ti + 1],
                                                       C.pk(2, 1, TT), axis=AX),
                  waits=C.raw(2))
        C.read(2, t8)
        ksq_toks.append(t8)
        tv = p.op("dve", lambda e: e.tensor_copy(vbf, C.pk(1)), waits=C.raw(1))
        C.read(1, tv)
        for j in range(4):
            kc = ti * 4 + j
            mtr = p.op("pe", lambda e, j=j: e.transpose(
                C.pkbf(3, 128, 128), vbf[:, j * 128:(j + 1) * 128], c["ident_bf"]),
                waits=[tv] + C.war(3) + C.tok["c_loads"])
            C.wrote(3, mtr)
            tve = p.op("dve", lambda e, kc=kc: e.tensor_copy(
                vtm[:, kc * 128:(kc + 1) * 128], C.pkbf(3, 128, 128)),
                waits=C.raw(3))
            C.read(3, tve)
            kv_ready.append(tve)
    tkm = p.op("dve", lambda e: e.reduce_max(kstat[0:1, 4:5], kstat[0:1, 0:4], axis=AX),
               waits=ksq_toks)
    tkm2a = p.op("act", lambda e: e.activation(kstat[0:1, 4:5], kstat[0:1, 4:5], AF.Sqrt),
                 waits=[tkm])
    tkm2 = p.op("dve", lambda e: e.tensor_scalar(kstat[0:1, 5:6], kstat[0:1, 4:5], -1.0,
                                                 None, ALU.mult), waits=[tkm2a])

    if not a1:
        ar.pop()
        return [tkm2]
    first_pe_kv_wait = kv_ready
    x1_stores = []
    ydma = [None, None]
    for qt in range(NTT):
        tx = p.dma("sync", x_tile.rearrange("p (c t) -> p c t", c=NC),
                   x_in[:, qt * TT:(qt + 1) * TT].rearrange("(c p) t -> p c t", p=128),
                   waits=prev_x_readers)
        xn_toks = emit_ln(C, xin, TT, xnf, sc, lambda ci: [tx], double_gamma=True)
        nkc = (qt + 1) * 4
        qsl = slice(qt * TT, (qt + 1) * TT)
        ocat_toks = [None] * H

        def qgen(h, slot):
            wb = h % 2
            qbank = 6 + slot
            twl = p.dma("sync", wstream[wb].rearrange("p (c n) -> p c n", c=NC),
                        C.dram["wq"][:, h * DH:(h + 1) * DH].rearrange("(c p) n -> p c n", p=128),
                        waits=[C.tok.get(("A_ws", wb))])
            twc = p.op("dve", lambda e, wb=wb: e.tensor_copy(wstream_bf[wb], wstream[wb]),
                       waits=[twl])
            mq = None
            for ci in range(NC):
                mq = p.op("pe", lambda e, ci=ci, wb=wb, qbank=qbank: e.matmul(
                    C.pk(qbank), wstream_bf[wb][:, ci * 128:(ci + 1) * 128], xnf(ci),
                    start=(ci == 0), stop=(ci == NC - 1)),
                    waits=[twc, xn_toks[ci]] + (C.war(qbank) if ci == 0 else [])
                          + (first_pe_kv_wait if (qt == 0 and h == 0 and ci == 0) else []))
            C.wrote(qbank, mq)
            C.tok[("A_ws", wb)] = mq
            t1 = p.op("dve", lambda e, qbank=qbank: e.tensor_copy(qf, C.pk(qbank)),
                      waits=C.raw(qbank))
            C.read(qbank, t1)
            p.op("dve", lambda e: e.tensor_copy(qsh[0:64, :], qf[64:128, :]))
            p.op("dve", lambda e: e.tensor_copy(qsh[64:128, :], qf[0:64, :]))
            p.op("dve", lambda e, qsl=qsl: e.tensor_tensor(qf, qf, c["cos_t"][:, qsl], ALU.mult))
            p.op("dve", lambda e, qsl=qsl: e.tensor_tensor(qsh, qsh, c["sinsgn_t"][:, qsl], ALU.mult))
            tq = p.op("dve", lambda e, slot=slot: e.tensor_tensor(qh2[slot], qf, qsh, ALU.add),
                      waits=[C.tok.get(("A_qh_pe", slot)), C.tok.get(("A_qh_act", slot))])
            t2 = p.op("act", lambda e, slot=slot: e.activation(qsq, qh2[slot], AF.Square),
                      waits=[tq, C.tok.get(("A_qsq_pe",))])
            C.tok[("A_qh_act", slot)] = t2
            m2 = p.op("pe", lambda e, qbank=qbank: e.matmul(C.pk(qbank, 1, TT),
                                                            c["ones_col_bf"], qsq,
                                                            start=True, stop=True),
                      waits=[t2] + C.war(qbank))
            C.wrote(qbank, m2)
            C.tok[("A_qsq_pe",)] = m2
            t3 = p.op("act", lambda e, slot=slot, qbank=qbank: e.activation(
                nqrow[slot], C.pk(qbank, 1, TT), AF.Sqrt), waits=C.raw(qbank))
            C.read(qbank, t3)
            t4 = p.op("dve", lambda e, slot=slot: e.tensor_scalar(
                nqrow[slot], nqrow[slot], kstat[0:1, 5:6], None, ALU.mult),
                waits=[t3, tkm2])
            t4b = p.op("dve", lambda e, slot=slot: e.tensor_copy(negbb2[slot], nqrow[slot]),
                       strict=[t4])
            return dict(tq=tq, t4b=t4b, slot=slot)

        qd = qgen(0, 0)
        for h in range(H):
            slot = qd["slot"]
            qh = qh2[slot]
            negbb = negbb2[slot]
            tq = qd["tq"]
            t4b = qd["t4b"]
            tl0 = p.op("dve", lambda e: e.memset(lacc, 0.0))
            stoks = []
            SBANKS = (0, 1, 2, 4)
            s_pe = []
            for kc in range(nkc):
                sb = SBANKS[kc % 4]
                war = C.war(sb) if kc < 4 else [stoks[kc - 4][1]]
                ms = p.op("pe", lambda e, kc=kc, sb=sb, qh=qh: e.matmul(
                    C.pk(sb), kfull[:, kc * 128:(kc + 1) * 128], qh,
                    start=True, stop=False),
                    waits=[tq] + war)
                ms2 = p.op("pe", lambda e, sb=sb, negbb=negbb: e.matmul(
                    C.pk(sb), c["ones_row_bf"], negbb, start=False, stop=True),
                    waits=[t4b])
                C.wrote(sb, ms2)
                qb0 = qt * 4
                tmask = None
                for j in range(4):
                    if kc == qb0 + j:
                        tmask = p.op("dve", lambda e, j=j, sb=sb: e.tensor_tensor(
                            C.pk(sb)[:, j * 128:(j + 1) * 128],
                            C.pk(sb)[:, j * 128:(j + 1) * 128],
                            c["causal"], ALU.add), waits=C.tok["c_loads"] + C.raw(sb))
                        C.read(sb, tmask)
                texp = p.op("act", lambda e, kc=kc, sb=sb: e.activation(
                    ptil[:, kc * TT:(kc + 1) * TT], C.pk(sb), AF.Exp, scale=SCALE),
                    waits=C.raw(sb) + [tmask, C.tok.get(("A_ptil_pe",)), C.tok.get(("A_ptil_dve",))])
                C.read(sb, texp)
                pready = texp
                for j in range(4):
                    if kc > qb0 + j:
                        pready = p.op("dve", lambda e, kc=kc, j=j: e.memset(
                            ptil[:, kc * TT + j * 128:kc * TT + (j + 1) * 128], 0),
                            waits=[texp])
                tla = p.op("dve", lambda e, kc=kc: e.tensor_tensor(
                    lacc, lacc, ptil[:, kc * TT:(kc + 1) * TT], ALU.add),
                    waits=[pready, tl0])
                stoks.append((pready, texp, tla))
            if h < H - 1:
                qd = qgen(h + 1, 1 - slot)
            mpv = None
            for kc in range(nkc):
                mpv = p.op("pe", lambda e, kc=kc: e.matmul(
                    C.pk(3), vtm[:, kc * 128:(kc + 1) * 128], ptil[:, kc * TT:(kc + 1) * TT],
                    start=(kc == 0), stop=(kc == nkc - 1)),
                    waits=[stoks[kc][0]] + (C.war(3) if kc == 0 else []))
            C.wrote(3, mpv)
            C.tok[("A_ptil_pe",)] = mpv
            C.tok[("A_qh_pe", slot)] = mpv
            C.tok[("A_ptil_dve",)] = stoks[-1][2]
            tlb = p.op("dve", lambda e: e.tensor_copy(laccb, lacc), waits=[stoks[-1][2]])
            ml = p.op("pe", lambda e: e.matmul(C.pk(5, 1, TT), c["ones_col_bf"], laccb,
                                               start=True, stop=True),
                      waits=[tlb] + C.war(5))
            C.wrote(5, ml)
            trl = p.op("dve", lambda e: e.tensor_copy(rl, C.pk(5, 1, TT)), waits=C.raw(5))
            C.read(5, trl)
            trl2 = p.op("dve", lambda e: e.reciprocal(rl, rl), strict=[trl])
            mrl = p.op("pe", lambda e: e.matmul(C.pk(5, 128, TT), c["ones_row_f"], rl,
                                                start=True, stop=True),
                      waits=[trl2] + C.war(5))
            C.wrote(5, mrl)
            trle = p.op("dve", lambda e: e.tensor_copy(rls, C.pk(5, 128, TT)), waits=C.raw(5))
            C.read(5, trle)
            to = p.op("dve", lambda e, h=h: e.tensor_tensor(
                ocat[:, h * TT:(h + 1) * TT], C.pk(3), rls, ALU.mult),
                waits=C.raw(3) + [trle, C.tok.get(("A_ocat_pe",))])
            C.read(3, to)
            ocat_toks[h] = to
            pump()
        my_last = None
        for oc in range(NC):
            wb = oc % 2
            twl = p.dma("sync", wstream[wb].rearrange("p (c n) -> p c n", c=NC),
                        C.dram["wo"][:, oc * 128:(oc + 1) * 128].rearrange("(c p) n -> p c n", p=128),
                        waits=[C.tok.get(("A_ws", wb))])
            twc = p.op("dve", lambda e, wb=wb: e.tensor_copy(wstream_bf[wb], wstream[wb]),
                       waits=[twl])
            my = None
            for ci in range(NC):
                my = p.op("pe", lambda e, ci=ci, wb=wb: e.matmul(
                    C.pk(6), wstream_bf[wb][:, ci * 128:(ci + 1) * 128],
                    ocat[:, ci * TT:(ci + 1) * TT],
                    start=(ci == 0), stop=(ci == NC - 1)),
                    waits=[twc, ocat_toks[ci]] + (C.war(6) if ci == 0 else []))
            C.wrote(6, my)
            C.tok[("A_ws", wb)] = my
            my_last = my
            ysl = oc % 2
            ty = p.op("dve", lambda e, oc=oc, ysl=ysl: e.tensor_tensor(
                youts[ysl], C.pk(6), xin(oc), ALU.add),
                waits=C.raw(6) + [ydma[ysl]])
            C.read(6, ty)
            tst = p.dma("act", x1_out[oc * 128:(oc + 1) * 128, qt * TT:(qt + 1) * TT],
                        youts[ysl], waits=[ty])
            ydma[ysl] = tst
            x1_stores.append(tst)
        C.tok[("A_ocat_pe",)] = my_last
        prev_x_readers = [ty]
    ar.pop()
    return x1_stores



def _core_inputs(S, b, names):
    d = {}
    for n in names:
        if n == "x_fm":
            d[n] = S["x_fm"][b]
        else:
            d[n] = S[n]
    return d


def build_A(standalone=True, a1=True, debug=False):
    nc = bass.Bass("TRN2", target_bir_lowering=False)
    C = Ctx(nc)
    dbg_specs = []
    if debug:
        dbg_specs = [("dbg_k", (128, NT), BF16), ("dbg_vtm", (128, NT), BF16),
                     ("dbg_kstat", (1, 8), F32), ("dbg_cos", (128, NT), F32),
                     ("dbg_sin", (128, NT), F32), ("dbg_xn", (128, NC * TT), BF16),
                     ("dbg_qh", (128, TT), BF16), ("dbg_bbuf", (128, TT), F32),
                     ("dbg_ptil", (128, NC * TT), BF16), ("dbg_lacc", (128, TT), F32),
                     ("dbg_ocat", (128, H * TT), BF16)]
    declare_dram(C, [
        ("x_fm", (D, NT), F32), ("x1", (D, NT), F32),
        ("wq", (D, D), F32), ("wkv", (D, 256), F32), ("wo", (D, D), F32),
    ] + DRAM_CONST + dbg_specs, outputs=("x1",) + tuple(n for n, _, _ in dbg_specs))
    emit_prologue(C)
    stores = emit_phaseA(C, a1=a1)
    if debug:
        dmap = dict(dbg_k="kfull", dbg_vtm="vtm", dbg_kstat="kstat", dbg_xn="xn_tile",
                    dbg_qh="qh", dbg_bbuf="bbuf", dbg_ptil="ptil", dbg_lacc="lacc",
                    dbg_ocat="ocat")
        for dname, cname in dmap.items():
            stores = stores + [C.p.dma("pool", C.dram[dname], C.dbg[cname], waits=stores[-1:])]
        stores = stores + [C.p.dma("pool", C.dram["dbg_cos"], C.c["cos_t"], waits=stores[-1:])]
        stores = stores + [C.p.dma("pool", C.dram["dbg_sin"], C.c["sinsgn_t"], waits=stores[-1:])]
    C.p.op("pool", lambda e: e.nop(), waits=stores)
    ctxs = C.p.realize()
    return nc, C


A_INPUT_NAMES = ["x_fm", "wq", "wkv", "wo", "gamma_t", "ident_bf", "ident_f",
                 "causal_bias", "invfreq2", "lut", "iota_row", "invfreq_row"]


def run_A(S, cores=8):
    nc, C = build_A()
    in_maps = [_core_inputs(S, b, A_INPUT_NAMES) for b in range(cores)]
    res = run_bass_kernel_spmd(nc, in_maps, core_ids=list(range(cores)))
    return [r["x1"] for r in res.results]





def emit_phaseB(C, xin_name="x1", xout_name="x2"):
    p, ar, c = C.p, C.ar, C.c
    x1_d = C.dram[xin_name]
    x2_d = C.dram[xout_name]
    TH = 1024
    ar.push()
    sc = alloc_ln_scratch(C, TT, double=False)
    acc = ar.alloc(NC * TH, F32, name="acc")
    xn2 = ar.alloc(NC * TH, BF16, name="xn2")
    gwtf = ar.alloc(NC * 32, F32)
    gwtb = ar.alloc(NC * 32, BF16)
    g_act = ar.alloc(TH, BF16, name="g_act")
    hbuf = [ar.alloc(6 * TH, BF16, name=f"h{i}") for i in range(2)]
    wgf = [ar.alloc(NC * 128, F32) for _ in range(2)]
    wgb = [ar.alloc(NC * 128, BF16) for _ in range(2)]
    wdf = ar.alloc(6 * TT, F32)
    wdb = [ar.alloc(6 * TT, BF16) for _ in range(2)]
    ce_s = [ar.alloc(TH, F32) for _ in range(2)]
    cerow = sc["strow"][0:1, 0:TH]
    combT = ar.alloc(8 * 128, F32, parts=32)
    gb = {}
    for nm, k in (("eexp", 32), ("r", 32), ("msk", 32), ("cu", 32), ("lg", 32),
                  ("mcol", 8), ("ccol", 8), ("acol", 8), ("dcol", 8)):
        gb[nm] = ar.alloc(k, F32)
    g1 = {nm: ar.alloc(1, F32) for nm in ("m1", "negm", "cS", "th", "Z", "rZ", "lnz")}

    g1row = ar.alloc(8, F32)
    C.dbg_b = dict(combT=combT, ce0=ce_s[0], h0=hbuf[0], xn2=xn2, g_act=g_act,
                   eexp=gb["eexp"], mcol=gb["mcol"], ccol=gb["ccol"], acol=gb["acol"],
                   cu=gb["cu"], g1row=g1row)
    C.dbg_g1 = (g1, g1row)
    tgw = p.dma("sync", gwtf.rearrange("p (c n) -> p c n", c=NC),
                C.dram["gwt"].rearrange("(c p) n -> p c n", p=128))
    tgwc = p.op("dve", lambda e: e.tensor_copy(gwtb, gwtf), waits=[tgw])

    BIG = 1e30
    x2_stores = []
    for hh in range(2):
        hsl = slice(hh * TH, (hh + 1) * TH)
        tacc = []
        for sub in range(2):
            t0 = hh * TH + sub * TT
            tl = p.dma("sync", acc.rearrange("p (c t) -> p c t", c=NC, t=TH)[:, :, sub * TT:(sub + 1) * TT],
                       x1_d[:, t0:t0 + TT].rearrange("(c p) t -> p c t", p=128),
                       waits=[C.tok.get(("B_acc_done", hh, sub))])
            tacc.append(tl)
        xn_toks = {}
        for sub in range(2):
            xin = lambda ci, sub=sub: acc[:, ci * TH + sub * TT: ci * TH + (sub + 1) * TT]
            xnf = lambda ci, sub=sub: xn2[:, ci * TH + sub * TT: ci * TH + (sub + 1) * TT]
            toks = emit_ln(C, xin, TT, xnf, sc, lambda ci, sub=sub: [tacc[sub]],
                           double_gamma=False)
            for ci in range(NC):
                xn_toks[(ci, sub)] = toks[ci]
        xn_all = [xn_toks[(NC - 1, 0)], xn_toks[(NC - 1, 1)]]
        tcomb = []
        for tb in range(8):
            mg = None
            for ci in range(NC):
                mg = p.op("pe", lambda e, ci=ci, tb=tb: e.matmul(
                    C.pk(4, 128, 32), xn2[:, ci * TH + tb * 128: ci * TH + (tb + 1) * 128],
                    gwtb[:, ci * 32:(ci + 1) * 32],
                    start=(ci == 0), stop=(ci == NC - 1)),
                    waits=[xn_toks[(ci, tb // 4)], tgwc] + (C.war(4) if ci == 0 else []))
            C.wrote(4, mg)
            tlg = p.op("dve", lambda e: e.tensor_copy(gb["lg"], C.pk(4, 128, 32)),
                       waits=C.raw(4))
            C.read(4, tlg)
            lastv = tlg
            def dv(fn, extra=()):
                nonlocal lastv
                lastv = p.op("dve", fn, waits=extra, strict=[lastv])
                return lastv
            dv(lambda e: e.reduce_max(g1["m1"], gb["lg"], axis=AX))
            tnm = dv(lambda e: e.tensor_scalar(g1["negm"], g1["m1"], -1.0, None, ALU.mult))
            tex = p.op("act", lambda e: e.activation(gb["eexp"], gb["lg"], AF.Exp,
                                                     bias=g1["negm"]), waits=[tnm])
            dv(lambda e: e.tensor_copy(gb["r"], gb["eexp"]), extra=[tex])
            for i in range(8):
                dv(lambda e: e.reduce_max(g1["m1"], gb["r"], axis=AX))
                dv(lambda e, i=i: e.tensor_copy(gb["mcol"][:, i:i + 1], g1["m1"]))
                if i == 0:
                    dv(lambda e: e.tensor_copy(gb["ccol"][:, 0:1], g1["m1"]))
                else:
                    dv(lambda e, i=i: e.tensor_tensor(
                        gb["ccol"][:, i:i + 1], gb["ccol"][:, i - 1:i], g1["m1"], ALU.add))
                if i < 7:
                    dv(lambda e: e.tensor_scalar(gb["msk"], gb["r"], g1["m1"], BIG,
                                                 ALU.is_ge, ALU.mult))
                    dv(lambda e: e.tensor_tensor(gb["r"], gb["r"], gb["msk"], ALU.subtract))
            dv(lambda e: e.tensor_scalar(g1["cS"], gb["ccol"][:, 7:8], 0.8, None, ALU.mult))
            dv(lambda e: e.memset(gb["acol"][:, 0:1], 1.0))
            dv(lambda e: e.tensor_scalar(gb["acol"][:, 1:8], gb["ccol"][:, 0:7],
                                         g1["cS"], 1.0, ALU.is_lt, ALU.mult))
            dv(lambda e: e.tensor_tensor(gb["dcol"][:, 0:7], gb["acol"][:, 0:7],
                                         gb["acol"][:, 1:8], ALU.subtract))
            dv(lambda e: e.tensor_copy(gb["dcol"][:, 7:8], gb["acol"][:, 7:8]))
            dv(lambda e: e.tensor_tensor(gb["dcol"], gb["mcol"], gb["dcol"], ALU.mult))
            dv(lambda e: e.tensor_reduce(g1["th"], gb["dcol"], axis=AX, op=ALU.add))
            dv(lambda e: e.tensor_scalar(gb["msk"], gb["eexp"], g1["th"], None, ALU.is_ge))
            dv(lambda e: e.tensor_tensor(gb["cu"], gb["eexp"], gb["msk"], ALU.mult))
            tz = dv(lambda e: e.tensor_reduce(g1["Z"], gb["cu"], axis=AX, op=ALU.add))
            tln = p.op("act", lambda e: e.activation(g1["lnz"], g1["Z"], AF.Ln),
                       waits=[tz], strict=[tex])
            trz = p.op("act", lambda e: e.activation(g1["rZ"], g1["lnz"], AF.Exp, scale=-1.0),
                       strict=[tln])
            tcu = dv(lambda e: e.tensor_scalar(gb["cu"], gb["cu"], g1["rZ"], None,
                                               ALU.mult), extra=[trz])
            mtr = p.op("pe", lambda e: e.transpose(C.pk(6, 32, 128), gb["cu"], c["ident_f"]),
                       waits=[tcu] + C.war(6) + C.tok["c_loads"])
            C.wrote(6, mtr)
            tct = p.op("dve", lambda e, tb=tb: e.tensor_copy(
                combT[:, tb * 128:(tb + 1) * 128], C.pk(6, 32, 128)), waits=C.raw(6))
            C.read(6, tct)
            tcomb.append(tct)
            if tb == 7:
                g1r = C.dbg_g1[1]
                for gi, nm in enumerate(("m1", "negm", "cS", "th", "Z", "rZ", "lnz")):
                    p.op("dve", lambda e, gi=gi, nm=nm: e.tensor_copy(
                        g1r[:, gi:gi + 1], g1[nm]), waits=[tct])
        for eg in range(NE // 2):
            htoks = [None, None]
            wd_toks = [None, None]
            for e2 in range(2):
                ex = eg * 2 + e2
                tce2 = None
                for hf in range(2):
                    m1_ = p.op("pe", lambda e, ex=ex, hf=hf: e.matmul(
                        C.pk(6, 1, 512), c["ident_f"][0:32, ex:ex + 1],
                        combT[:, hf * 512:(hf + 1) * 512], start=True, stop=True),
                        waits=tcomb + C.war(6))
                    C.wrote(6, m1_)
                    tcr = p.op("dve", lambda e, hf=hf: e.tensor_copy(
                        cerow[0:1, hf * 512:(hf + 1) * 512], C.pk(6, 1, 512)),
                        waits=C.raw(6))
                    C.read(6, tcr)
                    mb1 = p.op("pe", lambda e, hf=hf: e.matmul(
                        C.pk(7, 128, 512), c["ones_row_f"],
                        cerow[0:1, hf * 512:(hf + 1) * 512], start=True, stop=True),
                        waits=[tcr] + C.war(7))
                    C.wrote(7, mb1)
                    tce2 = p.op("dve", lambda e, e2=e2, hf=hf: e.tensor_copy(
                        ce_s[e2][:, hf * 512:(hf + 1) * 512], C.pk(7, 128, 512)),
                        waits=C.raw(7) + [C.tok.get(("B_ce_used", e2))])
                    C.read(7, tce2)
                for pr in range(6):
                    ldt = [None, None]
                    for gu, ib in ((0, pr), (1, pr + 6)):
                        wb = (ex * 12 + ib) % 2
                        twl = p.dma("sync", wgf[wb].rearrange("p (c n) -> p c n", c=NC),
                                    C.dram["wgu_t"][ex, :, ib * 128:(ib + 1) * 128]
                                    .rearrange("(c p) n -> p c n", p=128),
                                    waits=[C.tok.get(("B_wg_used", wb))])
                        twc = p.op("act", lambda e, wb=wb: e.activation(wgb[wb], wgf[wb], AF.Copy),
                                   waits=[twl])
                        mm = None
                        for sub in range(2):
                            bk = sub
                            for ci in range(NC):
                                mm = p.op("pe", lambda e, ci=ci, wb=wb, sub=sub, bk=bk: e.matmul(
                                    C.pk(bk), wgb[wb][:, ci * 128:(ci + 1) * 128],
                                    xn2[:, ci * TH + sub * TT: ci * TH + (sub + 1) * TT],
                                    start=(ci == 0), stop=(ci == NC - 1)),
                                    waits=[twc, xn_all[sub]] + (C.war(bk) if ci == 0 else []))
                            C.wrote(bk, mm)
                        C.tok[("B_wg_used", wb)] = mm
                        if gu == 0:
                            for sub in range(2):
                                tga = p.op("act", lambda e, sub=sub: e.activation(
                                    g_act[:, sub * TT:(sub + 1) * TT], C.pk(sub), AF.Silu),
                                    waits=C.raw(sub) + [C.tok.get(("B_gact_used", sub))])
                                C.read(sub, tga)
                                ldt[sub] = tga
                        else:
                            for sub in range(2):
                                th1 = p.op("dve", lambda e, sub=sub, pr=pr, e2=e2: e.tensor_tensor(
                                    hbuf[e2][:, pr * TH + sub * TT: pr * TH + (sub + 1) * TT],
                                    g_act[:, sub * TT:(sub + 1) * TT], C.pk(sub), ALU.mult),
                                    waits=C.raw(sub) + [ldt[sub], tce2,
                                                        C.tok.get(("B_h_used", e2))])
                                C.read(sub, th1)
                                C.tok[("B_gact_used", sub)] = th1
                                th2 = p.op("dve", lambda e, sub=sub, pr=pr, e2=e2: e.tensor_tensor(
                                    hbuf[e2][:, pr * TH + sub * TT: pr * TH + (sub + 1) * TT],
                                    hbuf[e2][:, pr * TH + sub * TT: pr * TH + (sub + 1) * TT],
                                    ce_s[e2][:, sub * TT:(sub + 1) * TT], ALU.mult))
                                htoks[e2] = th2
                C.tok[("B_ce_used", e2)] = htoks[e2]
            mm = None
            for ocq in range(4):
                for e2 in range(2):
                    twl = p.dma("sync", wdf.rearrange("p (ic n) -> p ic n", ic=6),
                                C.dram["wd_t"][eg * 2 + e2, :, ocq * 512:(ocq + 1) * 512]
                                .rearrange("(ic p) n -> p ic n", p=128),
                                waits=[C.tok.get(("B_wdf_used",))])
                    twc = p.op("act", lambda e, e2=e2: e.activation(
                        wdb[e2], wdf, AF.Copy), waits=[twl, C.tok.get(("B_wdb_used", e2))])
                    C.tok[("B_wdf_used",)] = twc
                    C.tok[("B_wdb_cast", e2)] = twc
                for sub in range(2):
                    for oci in range(4):
                        oc = ocq * 4 + oci
                        bk = oci
                        mm = None
                        for e2 in range(2):
                            for ic in range(6):
                                mm = p.op("pe", lambda e, e2=e2, ic=ic, oci=oci, sub=sub, bk=bk: e.matmul(
                                    C.pk(bk), wdb[e2][:, ic * 512 + oci * 128: ic * 512 + (oci + 1) * 128],
                                    hbuf[e2][:, ic * TH + sub * TT: ic * TH + (sub + 1) * TT],
                                    start=(e2 == 0 and ic == 0), stop=(e2 == 1 and ic == 5)),
                                    waits=[htoks[e2], C.tok.get(("B_wdb_cast", e2))]
                                          + (C.war(bk) if (e2 == 0 and ic == 0) else []))
                            C.tok[("B_wdb_used", e2)] = mm
                        C.wrote(bk, mm)
                        tadd = p.op("dve", lambda e, oc=oc, sub=sub, bk=bk, hh=hh: e.tensor_tensor(
                            acc[:, oc * TH + sub * TT: oc * TH + (sub + 1) * TT],
                            acc[:, oc * TH + sub * TT: oc * TH + (sub + 1) * TT],
                            C.pk(bk), ALU.add), waits=C.raw(bk))
                        C.read(bk, tadd)
                        C.tok[("B_accadd", oc, sub)] = tadd
            for e2 in range(2):
                C.tok[("B_h_used", e2)] = mm
        for oc in range(NC):
            for sub in range(2):
                t0 = hh * TH + sub * TT
                tst = p.dma("act", x2_d[oc * 128:(oc + 1) * 128, t0:t0 + TT],
                            acc[:, oc * TH + sub * TT: oc * TH + (sub + 1) * TT],
                            waits=[C.tok[("B_accadd", oc, sub)]])
                C.tok[("B_acc_done", hh, sub)] = tst
                x2_stores.append(tst)
        C.tok[("B_acc_done", 1 - hh, 0)] = x2_stores[-1]
        C.tok[("B_acc_done", 1 - hh, 1)] = x2_stores[-1]
    ar.pop()
    return x2_stores


def build_B(debug=False):
    nc = bass.Bass("TRN2", target_bir_lowering=False)
    C = Ctx(nc)
    dbg = [("dbg_combT", (32, 1024), F32), ("dbg_ce0", (128, 1024), F32),
           ("dbg_eexp", (128, 32), F32), ("dbg_mcol", (128, 8), F32),
           ("dbg_ccol", (128, 8), F32), ("dbg_acol", (128, 8), F32),
           ("dbg_cu", (128, 32), F32), ("dbg_g1", (128, 8), F32),
           ("dbg_h0", (128, 6 * 1024), BF16), ("dbg_xn2", (128, NC * 1024), BF16),
           ("dbg_gact", (128, 1024), BF16)] if debug else []
    declare_dram(C, [
        ("x1", (D, NT), F32), ("x2", (D, NT), F32),
        ("gwt", (D, 32), F32), ("wgu_t", (NE, D, 2 * EI), F32),
        ("wd_t", (NE, EI, D), F32),
    ] + DRAM_CONST + dbg, outputs=("x2",) + tuple(n for n, _, _ in dbg))
    emit_prologue(C)
    stores = emit_phaseB(C)
    if debug:
        dm = dict(dbg_combT="combT", dbg_ce0="ce0", dbg_h0="h0", dbg_xn2="xn2",
                  dbg_gact="g_act", dbg_eexp="eexp", dbg_mcol="mcol", dbg_ccol="ccol",
                  dbg_acol="acol", dbg_cu="cu", dbg_g1="g1row")
        for dn, cn in dm.items():
            b = C.dbg_b.get(cn)
            if b is not None:
                stores = stores + [C.p.dma("pool", C.dram[dn], b, waits=stores[-1:])]
    C.p.op("pool", lambda e: e.nop(), waits=stores)
    C.p.realize()
    return nc, C


B_INPUT_NAMES = ["x1", "gwt", "wgu_t", "wd_t", "gamma_t", "ident_bf", "ident_f",
                 "causal_bias", "invfreq2", "lut", "iota_row", "invfreq_row"]





def emit_phaseC(C, xin_name="x2", xout_name="x3"):
    p, ar, c = C.p, C.ar, C.c
    x2_d = C.dram[xin_name]
    x3_d = C.dram[xout_name]
    ar.push()
    wT = {}
    for nm_ in ("q", "k", "v", "o"):
        wT[nm_] = ar.alloc(NT, BF16, parts=64)
    bas = {}
    for nm_ in ("q", "k", "v", "o"):
        bas[nm_] = ar.alloc(NC * HARM, BF16)
    ar.push()
    stg = [ar.alloc(NT, F32, parts=64) for _ in range(2)]
    halfpi64 = ar.alloc(1, F32, parts=64)
    tpi = p.op("dve", lambda e: e.memset(halfpi64, float(np.pi / 2)))
    tw_toks = {}
    for nm in ("q", "k", "v", "o"):
        ta = p.dma("sync", stg[0], C.dram[f"amp_{nm}_T"], waits=[tw_toks.get(("s", 0))])
        tp2 = p.dma("sync", stg[1], C.dram[f"phase_{nm}_T"], waits=[tw_toks.get(("s", 1))])
        tc_ = p.op("act", lambda e: e.activation(stg[1], stg[1], AF.Sin, bias=halfpi64[:, 0:1]),
                   waits=[tp2, tpi])
        tw = p.op("dve", lambda e, nm=nm: e.tensor_tensor(wT[nm], stg[0], stg[1], ALU.mult),
                  waits=[ta, tc_])
        tw_toks[("s", 0)] = tw
        tw_toks[("s", 1)] = tw
        tw_toks[nm] = tw
    bstg = ar.alloc(NC * HARM, F32)
    tb_prev = None
    for nm in ("q", "k", "v", "o"):
        tl = p.dma("sync", bstg.rearrange("p (c n) -> p c n", c=NC),
                   C.dram[f"basis_{nm}_T"].rearrange("(c p) n -> p c n", p=128),
                   waits=[tb_prev])
        tb_prev = p.op("dve", lambda e, nm=nm: e.tensor_copy(bas[nm], bstg), waits=[tl])
        tw_toks[("b", nm)] = tb_prev
    ar.pop()
    res = {nm: ar.alloc(NT, BF16, parts=64) for nm in ("q", "k", "v")}
    res_o = ar.alloc(NT, F32, parts=64)
    res_ob = ar.alloc(NT, BF16, parts=64)
    sc = alloc_ln_scratch(C, TT, double=False)
    x2t = ar.alloc(NC * TT, F32)
    xn3 = ar.alloc(NC * TT, BF16)
    prev_x = []
    res_toks = []
    for ti in range(NTT):
        tx = p.dma("sync", x2t.rearrange("p (c t) -> p c t", c=NC),
                   x2_d[:, ti * TT:(ti + 1) * TT].rearrange("(c p) t -> p c t", p=128),
                   waits=prev_x)
        xin = lambda ci: x2t[:, ci * TT:(ci + 1) * TT]
        xnf = lambda ci: xn3[:, ci * TT:(ci + 1) * TT]
        xt = emit_ln(C, xin, TT, xnf, sc, lambda ci: [tx], double_gamma=False)
        prev_x = [xt[-1]]
        for bi, nm in enumerate(("q", "k", "v")):
            bk = bi
            mr = None
            for ci in range(NC):
                mr = p.op("pe", lambda e, ci=ci, nm=nm, bk=bk: e.matmul(
                    C.pk(bk, 64, TT), bas[nm][:, ci * HARM:(ci + 1) * HARM], xnf(ci),
                    start=(ci == 0), stop=(ci == NC - 1)),
                    waits=[xt[ci], tw_toks[("b", nm)]] + (C.war(bk) if ci == 0 else []))
            C.wrote(bk, mr)
            te = p.op("dve", lambda e, nm=nm, ti=ti, bk=bk: e.tensor_copy(
                res[nm][:, ti * TT:(ti + 1) * TT], C.pk(bk, 64, TT)), waits=C.raw(bk))
            C.read(bk, te)
            res_toks.append(te)
    qbuf2 = [ar.alloc(NT, BF16) for _ in range(2)]
    kbuf2 = [ar.alloc(NT, BF16) for _ in range(2)]
    vbuf = ar.alloc(TT, BF16)
    vtm2 = [ar.alloc(NT, BF16) for _ in range(2)]
    nq2 = [ar.alloc(NT, F32, parts=1) for _ in range(2)]
    kst = ar.alloc(16, F32, parts=1)
    qsq = ar.alloc(TT, BF16)
    bbuf = ar.alloc(TT, F32)
    sxb = ar.alloc(TT, F32)
    ptil = ar.alloc(NC * TT, BF16)
    lacc = ar.alloc(TT, F32)
    laccb = ar.alloc(TT, BF16)
    rl = ar.alloc(TT, F32, parts=1)
    rls = ar.alloc(TT, F32)
    obuf = ar.alloc(TT, BF16)
    negbb = ar.alloc(TT, BF16, parts=1)
    tro = p.op("dve", lambda e: e.memset(res_o, 0.0))
    C.tok[("C_resob",)] = tro

    def gen(h, slot):
        hs = slice(h * DH, (h + 1) * DH)
        kbuf, qbuf, vtm, nq = kbuf2[slot], qbuf2[slot], vtm2[slot], nq2[slot]
        gen_toks = {}
        for src_nm, dstb in (("k", kbuf), ("q", qbuf)):
            for tb in range(NTT):
                mg = p.op("pe", lambda e, src_nm=src_nm, tb=tb, hs=hs: e.matmul(
                    C.pk(2), wT[src_nm][:, hs], res[src_nm][:, tb * TT:(tb + 1) * TT],
                    start=True, stop=True),
                    waits=res_toks + [tw_toks[src_nm]] + C.war(2))
                C.wrote(2, mg)
                te = p.op("dve", lambda e, dstb=dstb, tb=tb: e.tensor_copy(
                    dstb[:, tb * TT:(tb + 1) * TT], C.pk(2)),
                    waits=C.raw(2) + [C.tok.get(("C_kq_used", slot))])
                pass
                C.read(2, te)
                gen_toks[(src_nm, tb)] = te
        for tb in range(NTT):
            mg = p.op("pe", lambda e, tb=tb, hs=hs: e.matmul(
                C.pk(2), wT["v"][:, hs], res["v"][:, tb * TT:(tb + 1) * TT],
                start=True, stop=True),
                waits=res_toks + [tw_toks["v"]] + C.war(2))
            C.wrote(2, mg)
            tv = p.op("dve", lambda e, tb=tb: e.tensor_copy(vbuf, C.pk(2)),
                      waits=C.raw(2) + [C.tok.get(("C_vb_used",))])
            C.read(2, tv)
            tvt_last = None
            for j in range(4):
                kc = tb * 4 + j
                mtr = p.op("pe", lambda e, j=j: e.transpose(
                    C.pkbf(2, 128, 128), vbuf[:, j * 128:(j + 1) * 128], c["ident_bf"]),
                    waits=[tv] + C.war(2) + C.tok["c_loads"])
                C.wrote(2, mtr)
                tvt = p.op("dve", lambda e, kc=kc, vtm=vtm: e.tensor_copy(
                    vtm[:, kc * 128:(kc + 1) * 128], C.pkbf(2, 128, 128)),
                    waits=C.raw(2) + [C.tok.get(("C_kq_used", slot))])
                C.read(2, tvt)
                tvt_last = tvt
            C.tok[("C_vb_used",)] = tvt_last
            gen_toks[("v", tb)] = tvt_last
        ks0 = slot * 8
        nk_toks = []
        for tb in range(NTT):
            t7 = p.op("act", lambda e, tb=tb, kbuf=kbuf: e.activation(
                qsq, kbuf[:, tb * TT:(tb + 1) * TT], AF.Square),
                waits=[gen_toks[("k", tb)], C.tok.get(("C_qsq_pe",))])
            m3 = p.op("pe", lambda e: e.matmul(C.pk(2, 1, TT), c["ones_col_bf"], qsq,
                                               start=True, stop=True),
                      waits=[t7] + C.war(2) + C.tok["c_dve"])
            C.wrote(2, m3)
            C.tok[("C_qsq_pe",)] = m3
            t8 = p.op("dve", lambda e, tb=tb, ks0=ks0: e.reduce_max(
                kst[0:1, ks0 + tb:ks0 + tb + 1], C.pk(2, 1, TT), axis=AX),
                waits=C.raw(2), strict=[C.tok.get(("C_kst",))])
            C.read(2, t8)
            C.tok[("C_kst",)] = t8
            nk_toks.append(t8)
        tkm = p.op("dve", lambda e, ks0=ks0: e.reduce_max(
            kst[0:1, ks0 + 4:ks0 + 5], kst[0:1, ks0:ks0 + 4], axis=AX), strict=nk_toks[-1:])
        tkm2a = p.op("act", lambda e, ks0=ks0: e.activation(
            kst[0:1, ks0 + 4:ks0 + 5], kst[0:1, ks0 + 4:ks0 + 5], AF.Sqrt), strict=[tkm])
        tkm2 = p.op("dve", lambda e, ks0=ks0: e.tensor_scalar(
            kst[0:1, ks0 + 5:ks0 + 6], kst[0:1, ks0 + 4:ks0 + 5], -1.0,
            None, ALU.mult), waits=[tkm2a])
        nq_toks = []
        for tb in range(NTT):
            t7 = p.op("act", lambda e, tb=tb, qbuf=qbuf: e.activation(
                qsq, qbuf[:, tb * TT:(tb + 1) * TT], AF.Square),
                waits=[gen_toks[("q", tb)], C.tok.get(("C_qsq_pe",))])
            m3 = p.op("pe", lambda e: e.matmul(C.pk(2, 1, TT), c["ones_col_bf"], qsq,
                                               start=True, stop=True),
                      waits=[t7] + C.war(2))
            C.wrote(2, m3)
            C.tok[("C_qsq_pe",)] = m3
            t9 = p.op("act", lambda e, tb=tb, nq=nq: e.activation(
                nq[0:1, tb * TT:(tb + 1) * TT], C.pk(2, 1, TT), AF.Sqrt), waits=C.raw(2))
            C.read(2, t9)
            nq_toks.append(t9)
        return dict(gen_toks=gen_toks, nq_toks=nq_toks, tkm2=tkm2, slot=slot, ks0=ks0)

    gd = gen(0, 0)
    for h in range(H):
        slot = gd["slot"]
        kbuf, qbuf, vtm, nq = kbuf2[slot], qbuf2[slot], vtm2[slot], nq2[slot]
        gen_toks, nq_toks, tkm2, ks0 = gd["gen_toks"], gd["nq_toks"], gd["tkm2"], gd["ks0"]
        for qt in range(NTT):
            qsl = slice(qt * TT, (qt + 1) * TT)
            if qt == 1 and h < H - 1:
                gd = gen(h + 1, 1 - slot)
            t4 = p.op("dve", lambda e, qt=qt, nq=nq, ks0=ks0: e.tensor_scalar(
                sc["st"][0], nq[0:1, qt * TT:(qt + 1) * TT], kst[0:1, ks0 + 5:ks0 + 6],
                None, ALU.mult),
                waits=[nq_toks[qt], tkm2, C.tok.get(("C_ptil_pe",))])
            t4b = p.op("dve", lambda e: e.tensor_copy(negbb, sc["st"][0]), strict=[t4])
            tl0 = p.op("dve", lambda e: e.memset(lacc, 0.0))
            stoks = []
            SBANKS = (0, 1, 6, 7)
            for kc in range(NC):
                sb = SBANKS[kc % 4]
                war = C.war(sb) if kc < 4 else [stoks[kc - 4][1]]
                ms = p.op("pe", lambda e, kc=kc, sb=sb, qsl=qsl, kbuf=kbuf, qbuf=qbuf: e.matmul(
                    C.pk(sb), kbuf[:, kc * 128:(kc + 1) * 128], qbuf[:, qsl],
                    start=True, stop=False),
                    waits=[gen_toks[("k", kc // 4)], gen_toks[("q", qt)]] + war)
                ms2 = p.op("pe", lambda e, sb=sb: e.matmul(
                    C.pk(sb), c["ones_row_bf"], negbb, start=False, stop=True),
                    waits=[t4b])
                C.wrote(sb, ms2)

                texp = p.op("act", lambda e, kc=kc, sb=sb: e.activation(
                    ptil[:, kc * TT:(kc + 1) * TT], C.pk(sb), AF.Exp, scale=SCALE),
                    waits=C.raw(sb) + [C.tok.get(("C_ptil_pe",)), C.tok.get(("C_ptil_dve",))])
                C.read(sb, texp)
                tla = p.op("dve", lambda e, kc=kc: e.tensor_tensor(
                    lacc, lacc, ptil[:, kc * TT:(kc + 1) * TT], ALU.add),
                    waits=[texp, tl0])
                stoks.append((texp, texp, tla))
            mpv = None
            for kc in range(NC):
                mpv = p.op("pe", lambda e, kc=kc, vtm=vtm: e.matmul(
                    C.pk(3), vtm[:, kc * 128:(kc + 1) * 128], ptil[:, kc * TT:(kc + 1) * TT],
                    start=(kc == 0), stop=(kc == NC - 1)),
                    waits=[stoks[kc][1], gen_toks[("v", kc // 4)]] + (C.war(3) if kc == 0 else []))
            C.wrote(3, mpv)
            C.tok[("C_ptil_pe",)] = mpv
            C.tok[("C_ptil_dve",)] = stoks[-1][2]
            tlb = p.op("dve", lambda e: e.tensor_copy(laccb, lacc), waits=[stoks[-1][2]])
            ml = p.op("pe", lambda e: e.matmul(C.pk(4, 1, TT), c["ones_col_bf"], laccb,
                                               start=True, stop=True),
                      waits=[tlb] + C.war(4))
            C.wrote(4, ml)
            trl = p.op("dve", lambda e: e.tensor_copy(rl, C.pk(4, 1, TT)), waits=C.raw(4))
            C.read(4, trl)
            trl2 = p.op("dve", lambda e: e.reciprocal(rl, rl), strict=[trl])
            mrl = p.op("pe", lambda e: e.matmul(C.pk(4, 128, TT), c["ones_row_f"], rl,
                                                start=True, stop=True),
                      waits=[trl2] + C.war(4))
            C.wrote(4, mrl)
            trle = p.op("dve", lambda e: e.tensor_copy(rls, C.pk(4, 128, TT)), waits=C.raw(4))
            C.read(4, trle)
            to = p.op("dve", lambda e: e.tensor_tensor(obuf, C.pk(3), rls, ALU.mult),
                      waits=C.raw(3) + [trle, C.tok.get(("C_obuf_pe",))])
            C.read(3, to)
            mo = p.op("pe", lambda e, h=h: e.matmul(
                C.pk(5, 64, TT), bas["o"][:, h * HARM:(h + 1) * HARM], obuf,
                start=True, stop=True),
                waits=[to, tw_toks[("b", "o")]] + C.war(5))
            C.wrote(5, mo)
            C.tok[("C_obuf_pe",)] = mo
            ta = p.op("dve", lambda e, qt=qt: e.tensor_tensor(
                res_o[:, qt * TT:(qt + 1) * TT], res_o[:, qt * TT:(qt + 1) * TT],
                C.pk(5, 64, TT), ALU.add), waits=C.raw(5) + [C.tok[("C_resob",)]])
            C.read(5, ta)
            C.tok[("C_resob",)] = ta
            if qt == NTT - 1:
                C.tok[("C_kq_used", slot)] = mpv
    x2c = [ar.alloc(TT, F32) for _ in range(2)]
    yout = [ar.alloc(TT, F32) for _ in range(2)]
    tob = []
    for tb in range(NTT):
        tob.append(p.op("dve", lambda e, tb=tb: e.tensor_copy(
            res_ob[:, tb * TT:(tb + 1) * TT], res_o[:, tb * TT:(tb + 1) * TT]),
            waits=[C.tok[("C_resob",)]]))
    x3_stores = []
    ldma = [None, None]
    sdma = [None, None]
    for oc in range(NC):
        for tb in range(NTT):
            sl_ = (oc * NTT + tb) % 2
            tl = p.dma("sync", x2c[sl_],
                       x2_d[oc * 128:(oc + 1) * 128, tb * TT:(tb + 1) * TT],
                       waits=[ldma[sl_]])
            my = p.op("pe", lambda e, oc=oc, tb=tb: e.matmul(
                C.pk(6), wT["o"][:, oc * 128:(oc + 1) * 128], res_ob[:, tb * TT:(tb + 1) * TT],
                start=True, stop=True),
                waits=[tob[tb], tw_toks["o"]] + C.war(6))
            C.wrote(6, my)
            ty = p.op("dve", lambda e, sl_=sl_: e.tensor_tensor(
                yout[sl_], C.pk(6), x2c[sl_], ALU.add),
                waits=C.raw(6) + [tl, sdma[sl_]])
            C.read(6, ty)
            ldma[sl_] = ty
            tst = p.dma("act", x3_d[oc * 128:(oc + 1) * 128, tb * TT:(tb + 1) * TT],
                        yout[sl_], waits=[ty])
            sdma[sl_] = tst
            x3_stores.append(tst)
    ar.pop()
    return x3_stores


def build_C():
    nc = bass.Bass("TRN2", target_bir_lowering=False)
    C = Ctx(nc)
    specs = [("x2", (D, NT), F32), ("x3", (D, NT), F32)]
    for nm in ("q", "k", "v", "o"):
        specs += [(f"basis_{nm}_T", (D, HARM), F32), (f"phase_{nm}_T", (HARM, D), F32),
                  (f"amp_{nm}_T", (HARM, D), F32)]
    declare_dram(C, specs + DRAM_CONST, outputs=("x3",))
    emit_prologue(C)
    stores = emit_phaseC(C)
    C.p.op("pool", lambda e: e.nop(), waits=stores)
    C.p.realize()
    return nc, C


C_INPUT_NAMES = (["x2"] + [f"{w}_{nm}_T" for nm in ("q", "k", "v", "o")
                           for w in ("basis", "phase", "amp")] +
                 ["gamma_t", "ident_bf", "ident_f", "causal_bias", "invfreq2", "lut",
                  "iota_row", "invfreq_row"])



NTILE_ALL = GOB * NC * 2 + NC * GOB
NTILE_CORE = 260


def stage_gather_shards(S):
    allt = np.concatenate([S["gidx"].reshape(-1, 128, 128),
                           S["uidx"].reshape(-1, 128, 128),
                           S["didx"].reshape(-1, 128, 128)], axis=0)
    pad = NTILE_CORE * 8 - allt.shape[0]
    allt = np.concatenate([allt, np.zeros((pad, 128, 128), np.int32)], axis=0)
    return [np.ascontiguousarray(allt[b * NTILE_CORE:(b + 1) * NTILE_CORE]) for b in range(8)]


def reassemble_gather(outs):
    allw = np.concatenate(outs, axis=0)[:NTILE_ALL]
    n1 = GOB * NC
    wg = allw[:n1].reshape(GOB, NC, 128, 128)
    wu = allw[n1:2 * n1].reshape(GOB, NC, 128, 128)
    wd = allw[2 * n1:].reshape(NC, GOB, 128, 128)
    return wg, wu, wd


def emit_gather_gen(C, idx_name, out_name, ntiles):
    p, ar, c = C.p, C.ar, C.c
    ibuf = [ar.alloc(4 * 128, I32) for _ in range(2)]
    wbuf = [ar.alloc(4 * 128, BF16) for _ in range(2)]
    term = ar.alloc(4 * 128, BF16)
    idx_d = C.dram[idx_name]
    out_d = C.dram[out_name]
    ngroups = ntiles // 4
    sts = []
    ld = [None, None]
    st = [None, None]
    for g in range(ngroups):
        sl_ = g % 2
        tl = p.dma("sync", ibuf[sl_].rearrange("p (k n) -> p k n", k=4),
                   idx_d[g * 4:(g + 1) * 4].rearrange("k p n -> p k n"), waits=[ld[sl_]])
        tacc = p.op("dve", lambda e, sl_=sl_: e.tensor_scalar(
            wbuf[sl_], ibuf[sl_], 0, c["lutcol"][:, 0:1], ALU.is_equal, ALU.mult),
            waits=[tl, C.tok["lutcol"], st[sl_]])
        for v in range(1, 16):
            p.op("dve", lambda e, v=v, sl_=sl_: e.tensor_scalar(
                term, ibuf[sl_], v, c["lutcol"][:, v:v + 1], ALU.is_equal, ALU.mult))
            tacc = p.op("dve", lambda e, sl_=sl_: e.tensor_tensor(
                wbuf[sl_], wbuf[sl_], term, ALU.add))
        ld[sl_] = tacc
        tso = p.dma("act", out_d[g * 4:(g + 1) * 4].rearrange("k p n -> p k n"),
                    wbuf[sl_].rearrange("p (k n) -> p k n", k=4), waits=[tacc])
        st[sl_] = tso
        sts.append(tso)
        C.tok["gather_stores"] = sts
        yield g
    while True:
        yield None


def emit_gather(C, idx_name, out_name, ntiles, groups=None):
    p, ar, c = C.p, C.ar, C.c
    ar.push()
    ibuf = [ar.alloc(4 * 128, I32) for _ in range(2)]
    wbuf = [ar.alloc(4 * 128, BF16) for _ in range(2)]
    term = ar.alloc(4 * 128, BF16)
    idx_d = C.dram[idx_name]
    out_d = C.dram[out_name]
    ngroups = ntiles // 4
    sts = []
    ld = [None, None]
    st = [None, None]
    for g in range(ngroups):
        sl_ = g % 2
        tl = p.dma("sync", ibuf[sl_].rearrange("p (k n) -> p k n", k=4),
                   idx_d[g * 4:(g + 1) * 4].rearrange("k p n -> p k n"), waits=[ld[sl_]])
        tacc = p.op("dve", lambda e, sl_=sl_: e.tensor_scalar(
            wbuf[sl_], ibuf[sl_], 0, c["lutcol"][:, 0:1], ALU.is_equal, ALU.mult),
            waits=[tl, C.tok["lutcol"], st[sl_]])
        for v in range(1, 16):
            tt_ = p.op("dve", lambda e, v=v, sl_=sl_: e.tensor_scalar(
                term, ibuf[sl_], v, c["lutcol"][:, v:v + 1], ALU.is_equal, ALU.mult))
            tacc = p.op("dve", lambda e, sl_=sl_: e.tensor_tensor(
                wbuf[sl_], wbuf[sl_], term, ALU.add))
        ld[sl_] = tacc
        tso = p.dma("act", out_d[g * 4:(g + 1) * 4].rearrange("k p n -> p k n"),
                    wbuf[sl_].rearrange("p (k n) -> p k n", k=4), waits=[tacc])
        st[sl_] = tso
        sts.append(tso)
    ar.pop()
    return sts


def build_G():
    nc = bass.Bass("TRN2", target_bir_lowering=False)
    C = Ctx(nc)
    declare_dram(C, [
        ("idx_shard", (NTILE_CORE, 128, 128), I32),
        ("w_shard", (NTILE_CORE, 128, 128), BF16),
    ] + DRAM_CONST, outputs=("w_shard",))
    emit_prologue(C)
    stores = emit_gather(C, "idx_shard", "w_shard", NTILE_CORE)
    C.p.op("pool", lambda e: e.nop(), waits=stores)
    C.p.realize()
    return nc, C



def emit_phaseD(C, xin_name="x3", out_name="out", wsrc=None):
    p, ar, c = C.p, C.ar, C.c
    x3_d = C.dram[xin_name]
    out_d = C.dram[out_name]
    if wsrc is None:
        wsrc = lambda kind, blk: {"g": C.dram["wg_sc"][blk].rearrange("k p n -> p k n"),
                                  "u": C.dram["wu_sc"][blk].rearrange("k p n -> p k n"),
                                  "d": C.dram["wd_sc"][blk].rearrange("k p n -> p k n")}[kind]
    ar.push()
    sc = alloc_ln_scratch(C, TT, double=False)
    x3q = ar.alloc(NC * TT, F32, name="x3q")
    xn4 = ar.alloc(NC * TT, BF16)
    hq = ar.alloc(GOB * TT, BF16, name="hq")
    wgub = [ar.alloc(NC * 128, BF16) for _ in range(2)]
    wub = [ar.alloc(NC * 128, BF16) for _ in range(2)]
    wdb = [ar.alloc(GOB * 128, BF16) for _ in range(2)]
    yout = [ar.alloc(TT, F32) for _ in range(2)]
    out_stores = []
    prev_x = []
    sdma = [None, None]
    for tq in range(NTT):
        tx = p.dma("sync", x3q.rearrange("p (c t) -> p c t", c=NC),
                   x3_d[:, tq * TT:(tq + 1) * TT].rearrange("(c p) t -> p c t", p=128),
                   waits=prev_x)
        xin = lambda ci: x3q[:, ci * TT:(ci + 1) * TT]
        xnf = lambda ci: xn4[:, ci * TT:(ci + 1) * TT]
        xt = emit_ln(C, xin, TT, xnf, sc, lambda ci: [tx], double_gamma=False)
        for ob in range(GOB):
            wb = ob % 2
            tg = p.dma("sync", wgub[wb].rearrange("p (k n) -> p k n", k=NC),
                       wsrc("g", ob),
                       waits=[C.tok.get(("D_wg_used", wb)), C.tok.get("wall_ready")])
            tu = p.dma("sync", wub[wb].rearrange("p (k n) -> p k n", k=NC),
                       wsrc("u", ob),
                       waits=[C.tok.get(("D_wu_used", wb))])
            mgx = mux = None
            for ci in range(NC):
                mgx = p.op("pe", lambda e, ci=ci, wb=wb: e.matmul(
                    C.pk(0), wgub[wb][:, ci * 128:(ci + 1) * 128], xnf(ci),
                    start=(ci == 0), stop=(ci == NC - 1)),
                    waits=[tg, xt[ci]] + (C.war(0) if ci == 0 else []))
                mux = p.op("pe", lambda e, ci=ci, wb=wb: e.matmul(
                    C.pk(1), wub[wb][:, ci * 128:(ci + 1) * 128], xnf(ci),
                    start=(ci == 0), stop=(ci == NC - 1)),
                    waits=[tu] + (C.war(1) if ci == 0 else []))
            C.wrote(0, mgx)
            C.wrote(1, mux)
            C.tok[("D_wg_used", wb)] = mgx
            C.tok[("D_wu_used", wb)] = mux
            tga = p.op("act", lambda e: e.activation(sc["tmpf"], C.pk(0), AF.Silu),
                       waits=C.raw(0) + [C.tok.get(("D_tmpf_used",))])
            C.read(0, tga)
            th = p.op("dve", lambda e, ob=ob: e.tensor_tensor(
                hq[:, ob * TT:(ob + 1) * TT], sc["tmpf"], C.pk(1), ALU.mult),
                waits=C.raw(1) + [tga, C.tok.get(("D_hq_used", ob))])
            C.read(1, th)
            C.tok[("D_tmpf_used",)] = th
            C.tok[("D_h", ob)] = th
        for oc in range(NC):
            wb = oc % 2
            td = p.dma("sync", wdb[wb].rearrange("p (k n) -> p k n", k=GOB),
                       wsrc("d", oc),
                       waits=[C.tok.get(("D_wd_used", wb))])
            my = None
            for ic in range(GOB):
                my = p.op("pe", lambda e, ic=ic, wb=wb: e.matmul(
                    C.pk(2), wdb[wb][:, ic * 128:(ic + 1) * 128], hq[:, ic * TT:(ic + 1) * TT],
                    start=(ic == 0), stop=(ic == GOB - 1)),
                    waits=[td, C.tok[("D_h", ic)]] + (C.war(2) if ic == 0 else []))
            C.wrote(2, my)
            C.tok[("D_wd_used", wb)] = my
            for ob in range(GOB):
                C.tok[("D_hq_used", ob)] = my
            ysl = oc % 2
            ty = p.op("dve", lambda e, oc=oc, ysl=ysl: e.tensor_tensor(
                yout[ysl], C.pk(2), xin(oc), ALU.add),
                waits=C.raw(2) + [sdma[ysl]])
            C.read(2, ty)
            tst = p.dma("act", out_d[oc * 128:(oc + 1) * 128, tq * TT:(tq + 1) * TT],
                        yout[ysl], waits=[ty])
            sdma[ysl] = tst
            out_stores.append(tst)
        prev_x = [ty]
    ar.pop()
    return out_stores


def build_D():
    nc = bass.Bass("TRN2", target_bir_lowering=False)
    C = Ctx(nc)
    declare_dram(C, [
        ("x3", (D, NT), F32), ("out", (D, NT), F32),
        ("wg_sc", (GOB, NC, 128, 128), BF16), ("wu_sc", (GOB, NC, 128, 128), BF16),
        ("wd_sc", (NC, GOB, 128, 128), BF16),
    ] + DRAM_CONST, outputs=("out",))
    emit_prologue(C)
    stores = emit_phaseD(C)
    C.p.op("pool", lambda e: e.nop(), waits=stores)
    C.p.realize()
    return nc, C


D_INPUT_NAMES = ["x3", "wg_sc", "wu_sc", "wd_sc", "gamma_t", "ident_bf", "ident_f",
                 "causal_bias", "invfreq2", "lut", "iota_row", "invfreq_row"]


CONST_NAMES = ["gamma_t", "ident_bf", "ident_f", "causal_bias", "invfreq2", "lut",
               "iota_row", "invfreq_row"]



def build_fused():
    nc = bass.Bass("TRN2", target_bir_lowering=False)
    C = Ctx(nc)
    specs = [
        ("x_fm", (D, NT), F32), ("out", (D, NT), F32),
        ("wq", (D, D), F32), ("wkv", (D, 256), F32), ("wo", (D, D), F32),
        ("gwt", (D, 32), F32), ("wgu_t", (NE, D, 2 * EI), F32), ("wd_t", (NE, EI, D), F32),
        ("idx_shard", (NTILE_CORE, 128, 128), I32),
    ]
    for nm in ("q", "k", "v", "o"):
        specs += [(f"basis_{nm}_T", (D, HARM), F32), (f"phase_{nm}_T", (HARM, D), F32),
                  (f"amp_{nm}_T", (HARM, D), F32)]
    declare_dram(C, specs + DRAM_CONST, outputs=("out",))
    sc_a = nc.dram_tensor("sc_a", [D, NT], F32)
    sc_b = nc.dram_tensor("sc_b", [D, NT], F32)
    wall_in = nc.dram_tensor("wall_in", [NTILE_CORE, 128, 128], BF16)
    wall_out = nc.dram_tensor("wall_out", [8 * NTILE_CORE * 128, 128], BF16, addr_space="Shared")
    C.dram["sc_a"] = sc_a.ap()
    C.dram["sc_b"] = sc_b.ap()
    C.dram["wall_in"] = wall_in.ap()
    wall_out_ap = wall_out.ap()

    emit_prologue(C)
    p = C.p
    gg = emit_gather_gen(C, "idx_shard", "wall_in", NTILE_CORE)

    def pump():
        next(gg, None)

    emit_phaseA(C, xin_name="x_fm", xout_name="sc_a", pump=pump)
    for _ in range(4):
        pump()
    p.fence()
    emit_phaseB(C, xin_name="sc_a", xout_name="sc_b")
    tcc = p.op("pool", lambda e: e.collective_compute(
        "AllGather", ALU.bypass, replica_groups=[list(range(8))],
        ins=[wall_in.ap().rearrange("t p n -> (t p) n").opt()], outs=[wall_out.ap().opt()]),
        waits=C.tok["gather_stores"])
    C.tok["wall_ready"] = tcc
    p.fence()
    emit_phaseC(C, xin_name="sc_b", xout_name="sc_a")
    p.fence()

    def wsrc(kind, blk):
        base = {"g": 0, "u": GOB * NC, "d": 2 * GOB * NC}[kind]
        n = NC if kind in ("g", "u") else GOB
        t0 = base + blk * n
        return wall_out_ap[t0 * 128:(t0 + n) * 128, :].rearrange("(k p) n -> p k n", p=128)

    stores = emit_phaseD(C, xin_name="sc_a", out_name="out", wsrc=wsrc)
    p.op("pool", lambda e: e.nop(), waits=stores)
    C.p.realize()
    return nc, C


FUSED_INPUT_NAMES = (["x_fm", "wq", "wkv", "wo", "gwt", "wgu_t", "wd_t", "idx_shard"] +
                     [f"{w}_{nm}_T" for nm in ("q", "k", "v", "o")
                      for w in ("basis", "phase", "amp")] + CONST_NAMES)


_BUILD_CACHE = {}


def _get(name, fn):
    if name not in _BUILD_CACHE:
        _BUILD_CACHE[name] = fn()
    return _BUILD_CACHE[name]


def kernel(**inputs):
    if FUSED:
        return kernel_fused(**inputs)
    return kernel_unfused(**inputs)


FUSED = True


def kernel_fused(**inputs):
    S = stage_inputs(inputs)
    shards = stage_gather_shards(S)
    nc, _ = _get("F", build_fused)
    in_maps = []
    for b in range(8):
        d = {n: S[n] for n in FUSED_INPUT_NAMES if n not in ("x_fm", "idx_shard")}
        d["x_fm"] = S["x_fm"][b]
        d["idx_shard"] = shards[b]
        in_maps.append(d)
    res = run_bass_kernel_spmd(nc, in_maps, core_ids=list(range(8)))
    return np.stack([np.ascontiguousarray(r["out"].T) for r in res.results])


def kernel_unfused(**inputs):
    S = stage_inputs(inputs)
    consts = {n: S[n] for n in CONST_NAMES}
    cores = list(range(8))

    shards = stage_gather_shards(S)
    nc_g, _ = _get("G", build_G)
    res = run_bass_kernel_spmd(nc_g, [dict(consts, idx_shard=shards[b]) for b in cores],
                               core_ids=cores)
    wg, wu, wd = reassemble_gather([r["w_shard"] for r in res.results])

    nc_a, _ = _get("A", build_A)
    amap = dict(consts, wq=S["wq"], wkv=S["wkv"], wo=S["wo"])
    res = run_bass_kernel_spmd(nc_a, [dict(amap, x_fm=S["x_fm"][b]) for b in cores],
                               core_ids=cores)
    x1 = [r["x1"] for r in res.results]

    nc_b, _ = _get("B", build_B)
    bmap = dict(consts, gwt=S["gwt"], wgu_t=S["wgu_t"], wd_t=S["wd_t"])
    res = run_bass_kernel_spmd(nc_b, [dict(bmap, x1=x1[b]) for b in cores],
                               core_ids=cores)
    x2 = [r["x2"] for r in res.results]

    nc_c, _ = _get("C", build_C)
    cmap = dict(consts)
    for nm in ("q", "k", "v", "o"):
        for w in ("basis", "phase", "amp"):
            cmap[f"{w}_{nm}_T"] = S[f"{w}_{nm}_T"]
    res = run_bass_kernel_spmd(nc_c, [dict(cmap, x2=x2[b]) for b in cores],
                               core_ids=cores)
    x3 = [r["x3"] for r in res.results]

    nc_d, _ = _get("D", build_D)
    dmap = dict(consts, wg_sc=wg, wu_sc=wu, wd_sc=wd)
    res = run_bass_kernel_spmd(nc_d, [dict(dmap, x3=x3[b]) for b in cores],
                               core_ids=cores)
    return np.stack([np.ascontiguousarray(r["out"].T) for r in res.results])


if __name__ == "__main__":
    pass


# version 06
# speedup vs baseline: 1.0213x; 1.0213x over previous
import numpy as np
import ml_dtypes
import concourse.bass as bass
import concourse.mybir as mybir
from concourse.bass_utils import run_bass_kernel_spmd
from concourse.alu_op_type import AluOpType as ALU

F32 = mybir.dt.float32
BF16 = mybir.dt.bfloat16
I32 = mybir.dt.int32
AF = mybir.ActivationFunctionType
AX = mybir.AxisListType.X

D = 2048
NT = 2048
NC = 16
TT = 512
NTT = NT // TT
H = 16
DH = 128
NE = 32
EI = 768
HARM = 64
GI = 5504
GOB = GI // 128
EPS = 1e-5
BF = ml_dtypes.bfloat16
SCALE = float(DH ** -0.5)



class Tok:
    __slots__ = ("eng", "idx", "step")
    def __init__(self, eng, idx, step):
        self.eng, self.idx, self.step = eng, idx, step


class _Eng:
    def __init__(self, name):
        self.name = name
        self.ops = []
        self.need_inc = set()
        self.sem = None


class Prog:
    def __init__(self, nc):
        self.nc = nc
        self.e = {n: _Eng(n) for n in ("pe", "dve", "act", "pool", "sync")}
        self.pending = {}
        self.consumed = set()

    def fence(self):
        self.pending = {n: Tok(n, len(st.ops) - 1, 16 if st.ops[-1][2] else 1)
                        for n, st in self.e.items() if st.ops}
        self.consumed = set()

    def op(self, eng, fn, waits=(), dma=False, strict=()):
        st = self.e[eng]
        ws = []
        for w in waits:
            if w is None:
                continue
            if isinstance(w, (list, tuple)):
                ws += [x for x in w if x is not None]
            else:
                ws.append(w)
        sws = [w for w in (strict if isinstance(strict, (list, tuple)) else [strict])
               if w is not None]
        if self.pending and eng not in self.consumed:
            ws += [t for n, t in self.pending.items() if n != eng]
            self.consumed.add(eng)
        for w in ws + sws:
            self.e[w.eng].need_inc.add(w.idx)
        st.ops.append((fn, ws, dma, sws))
        return Tok(eng, len(st.ops) - 1, 16 if dma else 1)

    def dma(self, queue, dst, src, waits=()):
        eng = {"sync": "sync", "act": "act", "pool": "pool"}[queue]
        return self.op(eng, lambda e, dst=dst, src=src: e.dma_start(dst, src),
                       waits=waits, dma=True)

    def chain(self, eng, fns, waits=()):
        t = None
        for i, fn in enumerate(fns):
            t = self.op(eng, fn, waits=(waits if i == 0 else ()),
                        strict=([t] if t is not None else ()))
        return t

    def realize(self):
        nc = self.nc
        ctxs = []
        for n, st in self.e.items():
            if st.ops:
                cm = nc.semaphore(f"sem_{n}")
                st.sem = cm.__enter__()
                ctxs.append(cm)
        cum = {}
        for n, st in self.e.items():
            for i, (fn, ws, dma, sws) in enumerate(st.ops):
                if dma:
                    st.need_inc.add(i)
            vals, c = {}, 0
            for i, (fn, ws, dma, sws) in enumerate(st.ops):
                if i in st.need_inc:
                    c += 16 if dma else 1
                vals[i] = c
            cum[n] = vals
        engmap = {"pe": "tensor", "dve": "vector", "act": "scalar",
                  "pool": "gpsimd", "sync": "sync"}
        blk = nc.Block()
        blk.__enter__()
        for n, st in self.e.items():
            if not st.ops:
                continue
            def make(st=st):
                def body(e):
                    for i, (fn, ws, dma, sws) in enumerate(st.ops):
                        wvals = {}
                        for w in ws:
                            if w.eng == st.name and not dma and w.step != 16:
                                continue
                            v = cum[w.eng][w.idx]
                            if wvals.get(w.eng, -1) < v:
                                wvals[w.eng] = v
                        for w in sws:
                            v = cum[w.eng][w.idx]
                            if wvals.get(w.eng, -1) < v:
                                wvals[w.eng] = v
                        for we, wv in wvals.items():
                            e.wait_ge(self.e[we].sem, wv)
                        inst = fn(e)
                        if i in st.need_inc:
                            inst.then_inc(st.sem, 16 if dma else 1)
                return body
            getattr(blk, engmap[n])(make())
        blk.__exit__(None, None, None)
        return ctxs


class Arena:
    def __init__(self, nc, kb=206):
        self.bytes = kb * 1024
        self.cm = nc.sbuf_tensor("arena", [128, self.bytes], mybir.dt.uint8)
        self.th = self.cm.__enter__()
        self.views = {F32: self.th.bitcast(F32).ap(),
                      BF16: self.th.bitcast(BF16).ap(),
                      I32: self.th.bitcast(I32).ap()}
        self.off = 0
        self.marks = []
        self.peak = 0

    def alloc(self, cols, dtype, parts=128, name=None):
        size = {F32: 4, BF16: 2, I32: 4}[dtype]
        align = 512 if cols * size >= 512 else 64
        self.off = (self.off + align - 1) // align * align
        self.log = getattr(self, "log", [])
        self.log.append((self.off, cols * size, name))
        assert self.off + cols * size <= self.bytes, \
            f"arena overflow: need {self.off + cols*size}, have {self.bytes}"
        v = self.views[dtype][0:parts, self.off // size: self.off // size + cols]
        self.off += cols * size
        self.peak = max(self.peak, self.off)
        return v

    def push(self):
        self.marks.append(self.off)

    def pop(self):
        self.off = self.marks.pop()


class Ctx:
    def __init__(self, nc):
        self.nc = nc
        self.p = Prog(nc)
        self.ar = Arena(nc)
        self._pc = []
        self.pb = []
        self.pbbf = []
        for i in range(8):
            cm = nc.psum_tensor(f"pb{i}", [128, 512], F32)
            h = cm.__enter__()
            self.pb.append(h)
            self.pbbf.append(h.bitcast(BF16))
            self._pc.append(cm)
        self.dram = {}
        self.tok = {}
        self.c = {}
        self.bankw = [None] * 8
        self.bankr = [dict() for _ in range(8)]

    def pk(self, i, parts=128, cols=512):
        return self.pb[i][0:parts, 0:cols]

    def pkbf(self, i, parts=128, cols=512):
        return self.pbbf[i][0:parts, 0:cols]

    def war(self, b):
        return list(self.bankr[b].values())

    def wrote(self, b, tok):
        self.bankw[b] = tok
        self.bankr[b] = {}

    def raw(self, b):
        return [self.bankw[b]]

    def read(self, b, tok):
        self.bankr[b][tok.eng] = tok


def declare_dram(C, specs, outputs=()):
    for name, shape, dt_ in specs:
        kind = "ExternalOutput" if name in outputs else "ExternalInput"
        C.dram[name] = C.nc.dram_tensor(name, list(shape), dt_, kind=kind).ap()



def stage_inputs(inputs):
    S = {}
    x = np.asarray(inputs["L0_attn_x"])
    S["x_fm"] = [np.ascontiguousarray(x[b].T) for b in range(8)]
    S["gamma_t"] = np.ascontiguousarray(
        np.asarray(inputs["L0_attn_gamma"]).reshape(NC, 128).T)
    S["wq"] = np.ascontiguousarray(inputs["L0_attn_Wq"])
    S["wkv"] = np.ascontiguousarray(inputs["L0_attn_Wkv"])
    S["wo"] = np.ascontiguousarray(inputs["L0_attn_Wo"])
    S["gwt"] = np.ascontiguousarray(np.asarray(inputs["L0_ffn_gate_weight"]).T)
    S["wgu_t"] = np.ascontiguousarray(
        np.asarray(inputs["L0_ffn_gate_up_proj"]).transpose(0, 2, 1))
    S["wd_t"] = np.ascontiguousarray(
        np.asarray(inputs["L0_ffn_down_proj"]).transpose(0, 2, 1))
    for nm in ("q", "k", "v", "o"):
        S[f"basis_{nm}_T"] = np.ascontiguousarray(np.asarray(inputs[f"L1_attn_basis_{nm}"]).T)
        S[f"phase_{nm}_T"] = np.ascontiguousarray(np.asarray(inputs[f"L1_attn_phase_{nm}"]).T)
        S[f"amp_{nm}_T"] = np.ascontiguousarray(np.asarray(inputs[f"L1_attn_amp_{nm}"]).T)
    S["lut"] = np.ascontiguousarray(np.asarray(inputs["L1_ffn_lut"]).reshape(1, 16))
    gt = np.asarray(inputs["L1_ffn_gate_idx"]).T
    ut = np.asarray(inputs["L1_ffn_up_idx"]).T
    dt_ = np.asarray(inputs["L1_ffn_down_idx"]).T
    def tile_hw(m, nr, ncb):
        return np.ascontiguousarray(
            m.reshape(nr, 128, ncb, 128).transpose(2, 0, 1, 3)).astype(np.int32)
    S["gidx"] = tile_hw(gt, NC, GOB)
    S["uidx"] = tile_hw(ut, NC, GOB)
    S["didx"] = tile_hw(dt_, GOB, NC)
    S["ident_bf"] = np.eye(128, dtype=BF)
    S["ident_f"] = np.eye(128, dtype=np.float32)
    cb = np.zeros((128, 128), np.float32)
    for p in range(128):
        cb[p, :p] = -1e30
    S["causal_bias"] = cb
    inv = 1.0 / (10000.0 ** (np.arange(0, DH, 2, dtype=np.float32) / DH))
    S["invfreq2"] = np.concatenate([inv, inv]).reshape(128, 1).astype(np.float32)
    S["iota_row"] = np.arange(NT, dtype=np.float32).reshape(1, NT)
    S["invfreq_row"] = np.concatenate([inv, inv]).reshape(1, 128).astype(np.float32)
    return S


DRAM_CONST = [
    ("gamma_t", (128, NC), F32), ("ident_bf", (128, 128), BF16),
    ("ident_f", (128, 128), F32), ("causal_bias", (128, 128), F32),
    ("invfreq2", (128, 1), F32), ("lut", (1, 16), F32), ("iota_row", (1, NT), F32),
    ("invfreq_row", (1, 128), F32),
]



def emit_prologue(C):
    p, ar, c = C.p, C.ar, C.c
    c["ones_col_bf"] = ar.alloc(1, BF16)
    c["ones_row_bf"] = ar.alloc(128, BF16, parts=1)
    c["ones_row_f"] = ar.alloc(128, F32, parts=1)
    c["eps1"] = ar.alloc(1, F32, parts=1)
    p.op("dve", lambda e: e.memset(c["ones_col_bf"], 1.0))
    p.op("dve", lambda e: e.memset(c["ones_row_bf"], 1.0))
    p.op("dve", lambda e: e.memset(c["ones_row_f"], 1.0))
    tm = p.op("dve", lambda e: e.memset(c["eps1"], EPS))
    C.tok["c_dve"] = [tm]
    loads = []
    for name, cols, dt_ in (("ident_bf", 128, BF16), ("ident_f", 128, F32),
                            ("causal", 128, F32), ("gamma", NC, F32),
                            ("invfreq2", 1, F32)):
        src = {"causal": "causal_bias", "gamma": "gamma_t"}.get(name, name)
        if src in C.dram:
            c[name] = ar.alloc(cols, dt_)
            loads.append(p.dma("sync", c[name], C.dram[src]))
    if "lut" in C.dram:
        c["lut_row"] = ar.alloc(16, F32, parts=1)
        tl = p.dma("sync", c["lut_row"], C.dram["lut"])
        c["lutcol"] = ar.alloc(16, F32)
        mm = p.op("pe", lambda e: e.matmul(C.pk(7, 128, 16), c["ones_row_f"], c["lut_row"],
                                           start=True, stop=True),
                  waits=[tl, tm] + C.war(7))
        C.wrote(7, mm)
        tc = p.op("dve", lambda e: e.tensor_copy(c["lutcol"], C.pk(7, 128, 16)),
                  waits=C.raw(7))
        C.read(7, tc)
        C.tok["lutcol"] = tc
        loads.append(tl)
    C.tok["c_loads"] = loads
    C.tok["consts"] = [tm] + loads


def emit_ln(C, xin, T, dst, sc, deps, double_gamma=False, pb=(4, 5), src_bf=False):
    p, c = C.p, C.c
    bA, bB = pb

    def stats(src, sdeps, is_bf):
        s1, s2 = C.pk(bA, 1, T), C.pk(bB, 1, T)
        warA, warB = C.war(bA), C.war(bB)
        mmtok = [None, None]
        for ci in range(NC):
            sl = ci % 2
            war_sl = [mmtok[1]] if ci >= 2 else []
            if is_bf:
                xb = src(ci)
                tc_ = None
            else:
                tc_ = p.op("act", lambda e, ci=ci, sl=sl: e.activation(
                    sc["xb2"][sl], src(ci), AF.Copy), waits=sdeps(ci) + war_sl)
                xb = None
            ts_ = p.op("act", lambda e, ci=ci, sl=sl: e.activation(
                sc["sq2"][sl], src(ci), AF.Square),
                waits=(sdeps(ci) + war_sl) if is_bf else [])
            m1 = p.op("pe", lambda e, ci=ci, sl=sl, xb=xb: e.matmul(
                s1, c["ones_col_bf"], xb if xb is not None else sc["xb2"][sl],
                start=(ci == 0), stop=(ci == NC - 1)),
                waits=([ts_] if is_bf else [tc_]) + ((warA + C.tok["c_dve"]) if ci == 0 else []))
            m2 = p.op("pe", lambda e, ci=ci, sl=sl: e.matmul(
                s2, c["ones_col_bf"], sc["sq2"][sl],
                start=(ci == 0), stop=(ci == NC - 1)),
                waits=[ts_] + (warB if ci == 0 else []))
            mmtok = [m1, m2]
        C.wrote(bA, mmtok[0])
        C.wrote(bB, mmtok[1])
        mu, e2, var, rstd = sc["st"]
        t1 = p.op("dve", lambda e: e.tensor_scalar(mu, s1, -1.0 / D, None, ALU.mult),
                  waits=C.raw(bA) + C.raw(bB))
        p.op("dve", lambda e: e.tensor_scalar(e2, s2, 1.0 / D, None, ALU.mult))
        p.op("dve", lambda e: e.tensor_tensor(var, mu, mu, ALU.mult))
        t4 = p.op("dve", lambda e: e.tensor_tensor(var, e2, var, ALU.subtract))
        C.read(bA, t4)
        C.read(bB, t4)
        t5 = p.op("act", lambda e: e.activation(var, var, AF.Ln, bias=c["eps1"][0:1, 0:1]),
                  waits=[t4] + C.tok["c_dve"])
        t6 = p.op("act", lambda e: e.activation(rstd, var, AF.Exp, scale=-0.5))
        b1 = p.op("pe", lambda e: e.matmul(C.pk(bA, 128, T), c["ones_row_f"], mu,
                                           start=True, stop=True),
                  waits=[t6] + C.war(bA))
        C.wrote(bA, b1)
        b2 = p.op("pe", lambda e: e.matmul(C.pk(bB, 128, T), c["ones_row_f"], rstd,
                                           start=True, stop=True),
                  waits=[t6] + C.war(bB))
        C.wrote(bB, b2)
        return C.pk(bA, 128, T), C.pk(bB, 128, T), [b1, b2]

    out = []
    if not double_gamma:
        mu_b, r_b, bc = stats(xin, deps, src_bf)
        last = None
        for ci in range(NC):
            p.op("dve", lambda e, ci=ci: e.tensor_tensor(sc["tmpf"], xin(ci), mu_b, ALU.add),
                 waits=deps(ci) + bc)
            last = p.op("dve", lambda e, ci=ci: e.tensor_tensor(dst(ci), sc["tmpf"], r_b, ALU.mult))
            out.append(last)
        C.read(bA, last)
        C.read(bB, last)
        return out
    ln1 = sc["ln1"]
    mu_b, r_b, bc = stats(xin, deps, src_bf)
    l1t = []
    for ci in range(NC):
        p.op("dve", lambda e, ci=ci: e.tensor_tensor(sc["tmpf"], xin(ci), mu_b, ALU.add),
             waits=deps(ci) + bc)
        l1t.append(p.op("dve", lambda e, ci=ci: e.tensor_tensor(ln1(ci), sc["tmpf"], r_b, ALU.mult)))
    C.read(bA, l1t[-1])
    C.read(bB, l1t[-1])
    mu2, r2, bc2 = stats(ln1, lambda ci: [l1t[ci]], True)
    last = None
    for ci in range(NC):
        p.op("dve", lambda e, ci=ci: e.tensor_tensor(sc["tmpf"], ln1(ci), mu2, ALU.add),
             waits=[l1t[ci]] + bc2)
        p.op("dve", lambda e, ci=ci: e.tensor_tensor(sc["tmpf"], sc["tmpf"], r2, ALU.mult))
        last = p.op("dve", lambda e, ci=ci: e.tensor_scalar(
            dst(ci), sc["tmpf"], c["gamma"][:, ci:ci + 1], None, ALU.mult))
        out.append(last)
    C.read(bA, last)
    C.read(bB, last)
    return out


def alloc_ln_scratch(C, T, double=False):
    ar = C.ar
    sc = {
        "xb2": [ar.alloc(T, BF16) for _ in range(2)],
        "sq2": [ar.alloc(T, BF16) for _ in range(2)],
        "tmpf": ar.alloc(T, F32),
    }
    strow = ar.alloc(4 * T, F32, parts=1)
    sc["st"] = [strow[0:1, i * T:(i + 1) * T] for i in range(4)]
    sc["strow"] = strow
    if double:
        ln1b = ar.alloc(NC * T, BF16)
        sc["ln1"] = lambda ci: ln1b[:, ci * T:(ci + 1) * T]
    return sc


def emit_rotary_tables(C):
    p, ar, c = C.p, C.ar, C.c
    c["cos_t"] = ar.alloc(NT, F32)
    c["sinsgn_t"] = ar.alloc(NT, F32)
    ar.push()
    fr = ar.alloc(NT, F32, name="fr")
    red = ar.alloc(NT, F32, name="red")
    red2 = ar.alloc(NT, F32, name="red2")
    n32 = ar.alloc(NT, I32, name="n32")
    nf = ar.alloc(NT, F32, name="nf")
    sgn = ar.alloc(1, F32)
    iorow = ar.alloc(NT, F32, parts=1)
    ifrow = ar.alloc(128, F32, parts=1)
    tir = p.dma("sync", iorow, C.dram["iota_row"])
    tif = p.dma("sync", ifrow, C.dram["invfreq_row"])
    pos_toks = []
    for j4 in range(4):
        mmi = p.op("pe", lambda e, j4=j4: e.matmul(
            C.pk(6), ifrow, iorow[0:1, j4 * 512:(j4 + 1) * 512],
            start=True, stop=True),
            waits=[tir, tif] + C.war(6))
        C.wrote(6, mmi)
        tpc = p.op("dve", lambda e, j4=j4: e.tensor_copy(
            fr[:, j4 * 512:(j4 + 1) * 512], C.pk(6)), waits=C.raw(6))
        C.read(6, tpc)
        pos_toks.append(tpc)
    tfr = pos_toks[-1]
    TWO_PI = 2.0 * np.pi
    PI_HI = np.float32(TWO_PI)
    PI_LO = float(np.float64(TWO_PI) - np.float64(PI_HI))

    def reduce_to(dst, shift, war=()):
        p.op("dve", lambda e: e.tensor_scalar(red, fr, 1.0 / TWO_PI, shift / TWO_PI,
                                              ALU.mult, ALU.add), waits=war)
        p.op("dve", lambda e: e.tensor_copy(n32, red))
        p.op("dve", lambda e: e.tensor_copy(nf, n32))
        p.op("dve", lambda e: e.scalar_tensor_tensor(red2, nf, -float(PI_HI), fr,
                                                     ALU.mult, ALU.add))
        if shift:
            p.op("dve", lambda e: e.tensor_scalar(red2, red2, shift, None, ALU.add))
        p.op("dve", lambda e: e.scalar_tensor_tensor(dst, nf, -PI_LO, red2,
                                                     ALU.mult, ALU.add))
        p.op("dve", lambda e: e.tensor_scalar(nf, dst, float(np.pi), float(TWO_PI),
                                              ALU.is_gt, ALU.mult))
        return p.op("dve", lambda e: e.tensor_tensor(dst, dst, nf, ALU.subtract))

    t1 = reduce_to(red, 0.0)
    tsin = p.op("act", lambda e: e.activation(c["sinsgn_t"], red, AF.Sin), waits=[t1])
    t2 = reduce_to(red2, float(np.pi / 2), war=[tsin])
    tcos = p.op("act", lambda e: e.activation(c["cos_t"], red2, AF.Sin), waits=[t2])
    p.op("dve", lambda e: e.memset(sgn[0:64, :], -1.0))
    p.op("dve", lambda e: e.memset(sgn[64:128, :], 1.0))
    tss = p.op("dve", lambda e: e.tensor_scalar(c["sinsgn_t"], c["sinsgn_t"], sgn[:, 0:1],
                                                None, ALU.mult), waits=[tsin])
    C.tok["rotary"] = [tss, tcos]
    C.tok["rotary_temps_done"] = [tss, tcos, tfr]
    ar.pop()



def emit_phaseA(C, a1=True, xin_name="x_fm", xout_name="x1", pump=None):
    p, ar, c = C.p, C.ar, C.c
    x_in = C.dram[xin_name]
    x1_out = C.dram[xout_name]
    xn_sc = C.dram["xn_sc"]
    pump = pump or (lambda: None)
    ar.push()
    kfull = ar.alloc(NT, BF16, name="kfull")
    vtm = ar.alloc(NT, BF16, name="vtm")
    kstat = ar.alloc(8, F32, parts=1, name="kstat")
    emit_rotary_tables(C)
    sc = alloc_ln_scratch(C, TT, double=True)
    x_tile = ar.alloc(NC * TT, F32, name="x_tile")
    xn_tile = ar.alloc(NC * TT, BF16, name="xn_tile")
    wkvf = ar.alloc(2 * 256, F32)
    wkv_full = ar.alloc(NC * 256, BF16)
    wstream = [ar.alloc(NC * 128, F32) for _ in range(2)]
    wstream_bf = [ar.alloc(NC * 128, BF16) for _ in range(2)]
    vbf = ar.alloc(TT, BF16)
    qf = ar.alloc(TT, F32)
    qsh = ar.alloc(TT, F32)
    qsq = ar.alloc(TT, BF16)
    ptil = ar.alloc(NC * TT, BF16, name="ptil")
    lacc = ar.alloc(TT, F32, name="lacc")
    laccb = ar.alloc(TT, BF16)
    rl = ar.alloc(TT, F32, parts=1)
    rls = ar.alloc(TT, F32)
    ocat = ar.alloc(H * TT, BF16, name="ocat")
    qh2 = [ar.alloc(TT, BF16) for _ in range(2)]
    negbb2 = [ar.alloc(TT, BF16, parts=1) for _ in range(2)]
    nqrow = [sc["strow"][0:1, 2 * TT:3 * TT], sc["strow"][0:1, 3 * TT:4 * TT]]
    kf = ar.alloc(TT, F32)
    ksh = ar.alloc(TT, F32)
    youts = [ar.alloc(TT, F32) for _ in range(2)]

    xin = lambda ci: x_tile[:, ci * TT:(ci + 1) * TT]
    xnf = lambda ci: xn_tile[:, ci * TT:(ci + 1) * TT]
    C.dbg = dict(kfull=kfull, vtm=vtm, kstat=kstat, x_tile=x_tile, xn_tile=xn_tile,
                 ptil=ptil, lacc=lacc, ocat=ocat)

    twkv = []
    for ci in range(NC):
        sl = ci % 2
        war = [twkv[ci - 2]] if ci >= 2 else list(C.tok["rotary_temps_done"])
        tl = p.dma("sync", wkvf[:, sl * 256:(sl + 1) * 256],
                   C.dram["wkv"][ci * 128:(ci + 1) * 128, :], waits=war)
        tc_ = p.op("dve", lambda e, ci=ci, sl=sl: e.tensor_copy(
            wkv_full[:, ci * 256:(ci + 1) * 256], wkvf[:, sl * 256:(sl + 1) * 256]),
            waits=[tl])
        twkv.append(tc_)

    prev_x_readers = list(C.tok["rotary_temps_done"])
    ksq_toks = []
    kv_ready = []
    for ti in range(NTT):
        tx = p.dma("sync", x_tile.rearrange("p (c t) -> p c t", c=NC),
                   x_in[:, ti * TT:(ti + 1) * TT].rearrange("(c p) t -> p c t", p=128),
                   waits=prev_x_readers)
        xn_toks = emit_ln(C, xin, TT, xnf, sc, lambda ci: [tx], double_gamma=True)
        prev_x_readers = [xn_toks[-1]]
        txn_st = p.dma("act", xn_sc[:, ti * TT:(ti + 1) * TT].rearrange("(c p) t -> p c t", p=128),
                       xn_tile.rearrange("p (c t) -> p c t", c=NC), waits=xn_toks)
        C.tok[("A_xnst", ti)] = txn_st
        mk = mv = None
        for ci in range(NC):
            mk = p.op("pe", lambda e, ci=ci: e.matmul(
                C.pk(0), wkv_full[:, ci * 256:ci * 256 + 128], xnf(ci),
                start=(ci == 0), stop=(ci == NC - 1)),
                waits=[xn_toks[ci], twkv[ci]] + (C.war(0) if ci == 0 else []))
            mv = p.op("pe", lambda e, ci=ci: e.matmul(
                C.pk(1), wkv_full[:, ci * 256 + 128:ci * 256 + 256], xnf(ci),
                start=(ci == 0), stop=(ci == NC - 1)),
                waits=(C.war(1) if ci == 0 else []))
        C.wrote(0, mk)
        C.wrote(1, mv)
        csl = slice(ti * TT, (ti + 1) * TT)
        t1 = p.op("dve", lambda e: e.tensor_copy(kf, C.pk(0)), waits=C.raw(0) + C.tok["rotary"])
        C.read(0, t1)
        p.op("dve", lambda e: e.tensor_copy(ksh[0:64, :], kf[64:128, :]))
        p.op("dve", lambda e: e.tensor_copy(ksh[64:128, :], kf[0:64, :]))
        p.op("dve", lambda e, csl=csl: e.tensor_tensor(kf, kf, c["cos_t"][:, csl], ALU.mult))
        p.op("dve", lambda e, csl=csl: e.tensor_tensor(ksh, ksh, c["sinsgn_t"][:, csl], ALU.mult))
        t6 = p.op("dve", lambda e, csl=csl: e.tensor_tensor(kfull[:, csl], kf, ksh, ALU.add))
        kv_ready.append(t6)
        t7 = p.op("act", lambda e, csl=csl: e.activation(qsq, kfull[:, csl], AF.Square), waits=[t6])
        m3 = p.op("pe", lambda e: e.matmul(C.pk(2, 1, TT), c["ones_col_bf"], qsq,
                                           start=True, stop=True),
                  waits=[t7] + C.war(2))
        C.wrote(2, m3)
        t8 = p.op("dve", lambda e, ti=ti: e.reduce_max(kstat[0:1, ti:ti + 1],
                                                       C.pk(2, 1, TT), axis=AX),
                  waits=C.raw(2))
        C.read(2, t8)
        ksq_toks.append(t8)
        tv = p.op("dve", lambda e: e.tensor_copy(vbf, C.pk(1)), waits=C.raw(1))
        C.read(1, tv)
        for j in range(4):
            kc = ti * 4 + j
            mtr = p.op("pe", lambda e, j=j: e.transpose(
                C.pkbf(3, 128, 128), vbf[:, j * 128:(j + 1) * 128], c["ident_bf"]),
                waits=[tv] + C.war(3) + C.tok["c_loads"])
            C.wrote(3, mtr)
            tve = p.op("dve", lambda e, kc=kc: e.tensor_copy(
                vtm[:, kc * 128:(kc + 1) * 128], C.pkbf(3, 128, 128)),
                waits=C.raw(3))
            C.read(3, tve)
            kv_ready.append(tve)
    tkm = p.op("dve", lambda e: e.reduce_max(kstat[0:1, 4:5], kstat[0:1, 0:4], axis=AX),
               waits=ksq_toks)
    tkm2a = p.op("act", lambda e: e.activation(kstat[0:1, 4:5], kstat[0:1, 4:5], AF.Sqrt),
                 waits=[tkm])
    tkm2 = p.op("dve", lambda e: e.tensor_scalar(kstat[0:1, 5:6], kstat[0:1, 4:5], -1.0,
                                                 None, ALU.mult), waits=[tkm2a])

    if not a1:
        ar.pop()
        return [tkm2]
    first_pe_kv_wait = kv_ready
    x1_stores = []
    ydma = [None, None]
    for qt in range(NTT):
        tx = p.dma("sync", x_tile.rearrange("p (c t) -> p c t", c=NC),
                   x_in[:, qt * TT:(qt + 1) * TT].rearrange("(c p) t -> p c t", p=128),
                   waits=prev_x_readers)
        txn = p.dma("sync", xn_tile.rearrange("p (c t) -> p c t", c=NC),
                    xn_sc[:, qt * TT:(qt + 1) * TT].rearrange("(c p) t -> p c t", p=128),
                    waits=[C.tok[("A_xnst", qt)], C.tok.get(("A_xn_used",))])
        xn_toks = [txn] * NC
        nkc = (qt + 1) * 4
        qsl = slice(qt * TT, (qt + 1) * TT)
        ocat_toks = [None] * H

        def qgen(h, slot):
            wb = h % 2
            qbank = 6 + slot
            twl = p.dma("sync", wstream[wb].rearrange("p (c n) -> p c n", c=NC),
                        C.dram["wq"][:, h * DH:(h + 1) * DH].rearrange("(c p) n -> p c n", p=128),
                        waits=[C.tok.get(("A_ws", wb))])
            twc = p.op("dve", lambda e, wb=wb: e.tensor_copy(wstream_bf[wb], wstream[wb]),
                       waits=[twl])
            mq = None
            for ci in range(NC):
                mq = p.op("pe", lambda e, ci=ci, wb=wb, qbank=qbank: e.matmul(
                    C.pk(qbank), wstream_bf[wb][:, ci * 128:(ci + 1) * 128], xnf(ci),
                    start=(ci == 0), stop=(ci == NC - 1)),
                    waits=[twc, xn_toks[ci]] + (C.war(qbank) if ci == 0 else [])
                          + (first_pe_kv_wait if (qt == 0 and h == 0 and ci == 0) else []))
            C.wrote(qbank, mq)
            C.tok[("A_ws", wb)] = mq
            t1 = p.op("dve", lambda e, qbank=qbank: e.tensor_copy(qf, C.pk(qbank)),
                      waits=C.raw(qbank))
            C.read(qbank, t1)
            p.op("dve", lambda e: e.tensor_copy(qsh[0:64, :], qf[64:128, :]))
            p.op("dve", lambda e: e.tensor_copy(qsh[64:128, :], qf[0:64, :]))
            p.op("dve", lambda e, qsl=qsl: e.tensor_tensor(qf, qf, c["cos_t"][:, qsl], ALU.mult))
            p.op("dve", lambda e, qsl=qsl: e.tensor_tensor(qsh, qsh, c["sinsgn_t"][:, qsl], ALU.mult))
            tq = p.op("dve", lambda e, slot=slot: e.tensor_tensor(qh2[slot], qf, qsh, ALU.add),
                      waits=[C.tok.get(("A_qh_pe", slot)), C.tok.get(("A_qh_act", slot))])
            t2 = p.op("act", lambda e, slot=slot: e.activation(qsq, qh2[slot], AF.Square),
                      waits=[tq, C.tok.get(("A_qsq_pe",))])
            C.tok[("A_qh_act", slot)] = t2
            m2 = p.op("pe", lambda e, qbank=qbank: e.matmul(C.pk(qbank, 1, TT),
                                                            c["ones_col_bf"], qsq,
                                                            start=True, stop=True),
                      waits=[t2] + C.war(qbank))
            C.wrote(qbank, m2)
            C.tok[("A_qsq_pe",)] = m2
            t3 = p.op("act", lambda e, slot=slot, qbank=qbank: e.activation(
                nqrow[slot], C.pk(qbank, 1, TT), AF.Sqrt), waits=C.raw(qbank))
            C.read(qbank, t3)
            t4 = p.op("dve", lambda e, slot=slot: e.tensor_scalar(
                nqrow[slot], nqrow[slot], kstat[0:1, 5:6], None, ALU.mult),
                waits=[t3, tkm2])
            t4b = p.op("dve", lambda e, slot=slot: e.tensor_copy(negbb2[slot], nqrow[slot]),
                       strict=[t4])
            return dict(tq=tq, t4b=t4b, slot=slot)

        qd = qgen(0, 0)
        for h in range(H):
            slot = qd["slot"]
            qh = qh2[slot]
            negbb = negbb2[slot]
            tq = qd["tq"]
            t4b = qd["t4b"]
            tl0 = p.op("dve", lambda e: e.memset(lacc, 0.0))
            stoks = []
            SBANKS = (0, 1, 2, 4)
            s_pe = []
            for kc in range(nkc):
                sb = SBANKS[kc % 4]
                war = C.war(sb) if kc < 4 else [stoks[kc - 4][1]]
                ms = p.op("pe", lambda e, kc=kc, sb=sb, qh=qh: e.matmul(
                    C.pk(sb), kfull[:, kc * 128:(kc + 1) * 128], qh,
                    start=True, stop=False),
                    waits=[tq] + war)
                ms2 = p.op("pe", lambda e, sb=sb, negbb=negbb: e.matmul(
                    C.pk(sb), c["ones_row_bf"], negbb, start=False, stop=True),
                    waits=[t4b])
                C.wrote(sb, ms2)
                qb0 = qt * 4
                tmask = None
                for j in range(4):
                    if kc == qb0 + j:
                        tmask = p.op("dve", lambda e, j=j, sb=sb: e.tensor_tensor(
                            C.pk(sb)[:, j * 128:(j + 1) * 128],
                            C.pk(sb)[:, j * 128:(j + 1) * 128],
                            c["causal"], ALU.add), waits=C.tok["c_loads"] + C.raw(sb))
                        C.read(sb, tmask)
                texp = p.op("act", lambda e, kc=kc, sb=sb: e.activation(
                    ptil[:, kc * TT:(kc + 1) * TT], C.pk(sb), AF.Exp, scale=SCALE),
                    waits=C.raw(sb) + [tmask, C.tok.get(("A_ptil_pe",)), C.tok.get(("A_ptil_dve",))])
                C.read(sb, texp)
                pready = texp
                for j in range(4):
                    if kc > qb0 + j:
                        pready = p.op("dve", lambda e, kc=kc, j=j: e.memset(
                            ptil[:, kc * TT + j * 128:kc * TT + (j + 1) * 128], 0),
                            waits=[texp])
                tla = p.op("dve", lambda e, kc=kc: e.tensor_tensor(
                    lacc, lacc, ptil[:, kc * TT:(kc + 1) * TT], ALU.add),
                    waits=[pready, tl0])
                stoks.append((pready, texp, tla))
            if h < H - 1:
                qd = qgen(h + 1, 1 - slot)
            mpv = None
            for kc in range(nkc):
                mpv = p.op("pe", lambda e, kc=kc: e.matmul(
                    C.pk(3), vtm[:, kc * 128:(kc + 1) * 128], ptil[:, kc * TT:(kc + 1) * TT],
                    start=(kc == 0), stop=(kc == nkc - 1)),
                    waits=[stoks[kc][0]] + (C.war(3) if kc == 0 else []))
            C.wrote(3, mpv)
            C.tok[("A_ptil_pe",)] = mpv
            C.tok[("A_qh_pe", slot)] = mpv
            C.tok[("A_ptil_dve",)] = stoks[-1][2]
            tlb = p.op("dve", lambda e: e.tensor_copy(laccb, lacc), waits=[stoks[-1][2]])
            ml = p.op("pe", lambda e: e.matmul(C.pk(5, 1, TT), c["ones_col_bf"], laccb,
                                               start=True, stop=True),
                      waits=[tlb] + C.war(5))
            C.wrote(5, ml)
            trl = p.op("dve", lambda e: e.tensor_copy(rl, C.pk(5, 1, TT)), waits=C.raw(5))
            C.read(5, trl)
            trl2 = p.op("dve", lambda e: e.reciprocal(rl, rl), strict=[trl])
            mrl = p.op("pe", lambda e: e.matmul(C.pk(5, 128, TT), c["ones_row_f"], rl,
                                                start=True, stop=True),
                      waits=[trl2] + C.war(5))
            C.wrote(5, mrl)
            trle = p.op("dve", lambda e: e.tensor_copy(rls, C.pk(5, 128, TT)), waits=C.raw(5))
            C.read(5, trle)
            to = p.op("dve", lambda e, h=h: e.tensor_tensor(
                ocat[:, h * TT:(h + 1) * TT], C.pk(3), rls, ALU.mult),
                waits=C.raw(3) + [trle, C.tok.get(("A_ocat_pe",))])
            C.read(3, to)
            ocat_toks[h] = to
            pump()
        my_last = None
        for oc in range(NC):
            wb = oc % 2
            twl = p.dma("sync", wstream[wb].rearrange("p (c n) -> p c n", c=NC),
                        C.dram["wo"][:, oc * 128:(oc + 1) * 128].rearrange("(c p) n -> p c n", p=128),
                        waits=[C.tok.get(("A_ws", wb))])
            twc = p.op("dve", lambda e, wb=wb: e.tensor_copy(wstream_bf[wb], wstream[wb]),
                       waits=[twl])
            my = None
            for ci in range(NC):
                my = p.op("pe", lambda e, ci=ci, wb=wb: e.matmul(
                    C.pk(6), wstream_bf[wb][:, ci * 128:(ci + 1) * 128],
                    ocat[:, ci * TT:(ci + 1) * TT],
                    start=(ci == 0), stop=(ci == NC - 1)),
                    waits=[twc, ocat_toks[ci]] + (C.war(6) if ci == 0 else []))
            C.wrote(6, my)
            C.tok[("A_ws", wb)] = my
            my_last = my
            ysl = oc % 2
            ty = p.op("dve", lambda e, oc=oc, ysl=ysl: e.tensor_tensor(
                youts[ysl], C.pk(6), xin(oc), ALU.add),
                waits=C.raw(6) + [ydma[ysl]])
            C.read(6, ty)
            tst = p.dma("act", x1_out[oc * 128:(oc + 1) * 128, qt * TT:(qt + 1) * TT],
                        youts[ysl], waits=[ty])
            ydma[ysl] = tst
            x1_stores.append(tst)
        C.tok[("A_ocat_pe",)] = my_last
        C.tok[("A_xn_used",)] = my_last
        prev_x_readers = [ty]
    ar.pop()
    return x1_stores



def _core_inputs(S, b, names):
    d = {}
    for n in names:
        if n == "x_fm":
            d[n] = S["x_fm"][b]
        else:
            d[n] = S[n]
    return d


def build_A(standalone=True, a1=True, debug=False):
    nc = bass.Bass("TRN2", target_bir_lowering=False)
    C = Ctx(nc)
    dbg_specs = []
    if debug:
        dbg_specs = [("dbg_k", (128, NT), BF16), ("dbg_vtm", (128, NT), BF16),
                     ("dbg_kstat", (1, 8), F32), ("dbg_cos", (128, NT), F32),
                     ("dbg_sin", (128, NT), F32), ("dbg_xn", (128, NC * TT), BF16),
                     ("dbg_qh", (128, TT), BF16), ("dbg_bbuf", (128, TT), F32),
                     ("dbg_ptil", (128, NC * TT), BF16), ("dbg_lacc", (128, TT), F32),
                     ("dbg_ocat", (128, H * TT), BF16)]
    declare_dram(C, [
        ("x_fm", (D, NT), F32), ("x1", (D, NT), F32),
        ("wq", (D, D), F32), ("wkv", (D, 256), F32), ("wo", (D, D), F32),
    ] + DRAM_CONST + dbg_specs, outputs=("x1",) + tuple(n for n, _, _ in dbg_specs))
    C.dram["xn_sc"] = nc.dram_tensor("xn_sc", [D, NT], BF16).ap()
    emit_prologue(C)
    stores = emit_phaseA(C, a1=a1)
    if debug:
        dmap = dict(dbg_k="kfull", dbg_vtm="vtm", dbg_kstat="kstat", dbg_xn="xn_tile",
                    dbg_qh="qh", dbg_bbuf="bbuf", dbg_ptil="ptil", dbg_lacc="lacc",
                    dbg_ocat="ocat")
        for dname, cname in dmap.items():
            stores = stores + [C.p.dma("pool", C.dram[dname], C.dbg[cname], waits=stores[-1:])]
        stores = stores + [C.p.dma("pool", C.dram["dbg_cos"], C.c["cos_t"], waits=stores[-1:])]
        stores = stores + [C.p.dma("pool", C.dram["dbg_sin"], C.c["sinsgn_t"], waits=stores[-1:])]
    C.p.op("pool", lambda e: e.nop(), waits=stores)
    ctxs = C.p.realize()
    return nc, C


A_INPUT_NAMES = ["x_fm", "wq", "wkv", "wo", "gamma_t", "ident_bf", "ident_f",
                 "causal_bias", "invfreq2", "lut", "iota_row", "invfreq_row"]


def run_A(S, cores=8):
    nc, C = build_A()
    in_maps = [_core_inputs(S, b, A_INPUT_NAMES) for b in range(cores)]
    res = run_bass_kernel_spmd(nc, in_maps, core_ids=list(range(cores)))
    return [r["x1"] for r in res.results]





def emit_phaseB(C, xin_name="x1", xout_name="x2"):
    p, ar, c = C.p, C.ar, C.c
    x1_d = C.dram[xin_name]
    x2_d = C.dram[xout_name]
    TH = 1024
    ar.push()
    sc = alloc_ln_scratch(C, TT, double=False)
    acc = ar.alloc(NC * TH, F32, name="acc")
    xn2 = ar.alloc(NC * TH, BF16, name="xn2")
    gwtf = ar.alloc(NC * 32, F32)
    gwtb = ar.alloc(NC * 32, BF16)
    g_act = ar.alloc(TH, BF16, name="g_act")
    hbuf = [ar.alloc(6 * TH, BF16, name=f"h{i}") for i in range(2)]
    wgf = [ar.alloc(NC * 128, F32) for _ in range(2)]
    wgb = [ar.alloc(NC * 128, BF16) for _ in range(2)]
    wdf = ar.alloc(6 * TT, F32)
    wdb = [ar.alloc(6 * TT, BF16) for _ in range(2)]
    ce_s = [ar.alloc(TH, F32) for _ in range(2)]
    cerow = sc["strow"][0:1, 0:TH]
    combT = ar.alloc(8 * 128, F32, parts=32)
    gb = {}
    for nm, k in (("eexp", 32), ("r", 32), ("msk", 32), ("cu", 32), ("lg", 32),
                  ("mcol", 8), ("ccol", 8), ("acol", 8), ("dcol", 8)):
        gb[nm] = ar.alloc(k, F32)
    g1 = {nm: ar.alloc(1, F32) for nm in ("m1", "negm", "cS", "th", "Z", "rZ", "lnz")}

    g1row = ar.alloc(8, F32)
    C.dbg_b = dict(combT=combT, ce0=ce_s[0], h0=hbuf[0], xn2=xn2, g_act=g_act,
                   eexp=gb["eexp"], mcol=gb["mcol"], ccol=gb["ccol"], acol=gb["acol"],
                   cu=gb["cu"], g1row=g1row)
    C.dbg_g1 = (g1, g1row)
    tgw = p.dma("sync", gwtf.rearrange("p (c n) -> p c n", c=NC),
                C.dram["gwt"].rearrange("(c p) n -> p c n", p=128))
    tgwc = p.op("dve", lambda e: e.tensor_copy(gwtb, gwtf), waits=[tgw])

    BIG = 1e30
    x2_stores = []
    for hh in range(2):
        hsl = slice(hh * TH, (hh + 1) * TH)
        tacc = []
        for sub in range(2):
            t0 = hh * TH + sub * TT
            tl = p.dma("sync", acc.rearrange("p (c t) -> p c t", c=NC, t=TH)[:, :, sub * TT:(sub + 1) * TT],
                       x1_d[:, t0:t0 + TT].rearrange("(c p) t -> p c t", p=128),
                       waits=[C.tok.get(("B_acc_done", hh, sub))])
            tacc.append(tl)
        xn_toks = {}
        for sub in range(2):
            xin = lambda ci, sub=sub: acc[:, ci * TH + sub * TT: ci * TH + (sub + 1) * TT]
            xnf = lambda ci, sub=sub: xn2[:, ci * TH + sub * TT: ci * TH + (sub + 1) * TT]
            toks = emit_ln(C, xin, TT, xnf, sc, lambda ci, sub=sub: [tacc[sub]],
                           double_gamma=False)
            for ci in range(NC):
                xn_toks[(ci, sub)] = toks[ci]
        xn_all = [xn_toks[(NC - 1, 0)], xn_toks[(NC - 1, 1)]]
        tcomb = []
        for tb in range(8):
            mg = None
            for ci in range(NC):
                mg = p.op("pe", lambda e, ci=ci, tb=tb: e.matmul(
                    C.pk(4, 128, 32), xn2[:, ci * TH + tb * 128: ci * TH + (tb + 1) * 128],
                    gwtb[:, ci * 32:(ci + 1) * 32],
                    start=(ci == 0), stop=(ci == NC - 1)),
                    waits=[xn_toks[(ci, tb // 4)], tgwc] + (C.war(4) if ci == 0 else []))
            C.wrote(4, mg)
            tlg = p.op("dve", lambda e: e.tensor_copy(gb["lg"], C.pk(4, 128, 32)),
                       waits=C.raw(4))
            C.read(4, tlg)
            lastv = tlg
            def dv(fn, extra=()):
                nonlocal lastv
                lastv = p.op("dve", fn, waits=extra, strict=[lastv])
                return lastv
            dv(lambda e: e.reduce_max(g1["m1"], gb["lg"], axis=AX))
            tnm = dv(lambda e: e.tensor_scalar(g1["negm"], g1["m1"], -1.0, None, ALU.mult))
            tex = p.op("act", lambda e: e.activation(gb["eexp"], gb["lg"], AF.Exp,
                                                     bias=g1["negm"]), waits=[tnm])
            dv(lambda e: e.tensor_copy(gb["r"], gb["eexp"]), extra=[tex])
            for i in range(8):
                dv(lambda e: e.reduce_max(g1["m1"], gb["r"], axis=AX))
                dv(lambda e, i=i: e.tensor_copy(gb["mcol"][:, i:i + 1], g1["m1"]))
                if i == 0:
                    dv(lambda e: e.tensor_copy(gb["ccol"][:, 0:1], g1["m1"]))
                else:
                    dv(lambda e, i=i: e.tensor_tensor(
                        gb["ccol"][:, i:i + 1], gb["ccol"][:, i - 1:i], g1["m1"], ALU.add))
                if i < 7:
                    dv(lambda e: e.tensor_scalar(gb["msk"], gb["r"], g1["m1"], BIG,
                                                 ALU.is_ge, ALU.mult))
                    dv(lambda e: e.tensor_tensor(gb["r"], gb["r"], gb["msk"], ALU.subtract))
            dv(lambda e: e.tensor_scalar(g1["cS"], gb["ccol"][:, 7:8], 0.8, None, ALU.mult))
            dv(lambda e: e.memset(gb["acol"][:, 0:1], 1.0))
            dv(lambda e: e.tensor_scalar(gb["acol"][:, 1:8], gb["ccol"][:, 0:7],
                                         g1["cS"], 1.0, ALU.is_lt, ALU.mult))
            dv(lambda e: e.tensor_tensor(gb["dcol"][:, 0:7], gb["acol"][:, 0:7],
                                         gb["acol"][:, 1:8], ALU.subtract))
            dv(lambda e: e.tensor_copy(gb["dcol"][:, 7:8], gb["acol"][:, 7:8]))
            dv(lambda e: e.tensor_tensor(gb["dcol"], gb["mcol"], gb["dcol"], ALU.mult))
            dv(lambda e: e.tensor_reduce(g1["th"], gb["dcol"], axis=AX, op=ALU.add))
            dv(lambda e: e.tensor_scalar(gb["msk"], gb["eexp"], g1["th"], None, ALU.is_ge))
            dv(lambda e: e.tensor_tensor(gb["cu"], gb["eexp"], gb["msk"], ALU.mult))
            tz = dv(lambda e: e.tensor_reduce(g1["Z"], gb["cu"], axis=AX, op=ALU.add))
            tln = p.op("act", lambda e: e.activation(g1["lnz"], g1["Z"], AF.Ln),
                       waits=[tz], strict=[tex])
            trz = p.op("act", lambda e: e.activation(g1["rZ"], g1["lnz"], AF.Exp, scale=-1.0),
                       strict=[tln])
            tcu = dv(lambda e: e.tensor_scalar(gb["cu"], gb["cu"], g1["rZ"], None,
                                               ALU.mult), extra=[trz])
            mtr = p.op("pe", lambda e: e.transpose(C.pk(6, 32, 128), gb["cu"], c["ident_f"]),
                       waits=[tcu] + C.war(6) + C.tok["c_loads"])
            C.wrote(6, mtr)
            tct = p.op("dve", lambda e, tb=tb: e.tensor_copy(
                combT[:, tb * 128:(tb + 1) * 128], C.pk(6, 32, 128)), waits=C.raw(6))
            C.read(6, tct)
            tcomb.append(tct)
            if tb == 7:
                g1r = C.dbg_g1[1]
                for gi, nm in enumerate(("m1", "negm", "cS", "th", "Z", "rZ", "lnz")):
                    p.op("dve", lambda e, gi=gi, nm=nm: e.tensor_copy(
                        g1r[:, gi:gi + 1], g1[nm]), waits=[tct])
        for eg in range(NE // 2):
            htoks = [None, None]
            wd_toks = [None, None]
            for e2 in range(2):
                ex = eg * 2 + e2
                tce2 = None
                for hf in range(2):
                    m1_ = p.op("pe", lambda e, ex=ex, hf=hf: e.matmul(
                        C.pk(6, 1, 512), c["ident_f"][0:32, ex:ex + 1],
                        combT[:, hf * 512:(hf + 1) * 512], start=True, stop=True),
                        waits=tcomb + C.war(6))
                    C.wrote(6, m1_)
                    tcr = p.op("dve", lambda e, hf=hf: e.tensor_copy(
                        cerow[0:1, hf * 512:(hf + 1) * 512], C.pk(6, 1, 512)),
                        waits=C.raw(6))
                    C.read(6, tcr)
                    mb1 = p.op("pe", lambda e, hf=hf: e.matmul(
                        C.pk(7, 128, 512), c["ones_row_f"],
                        cerow[0:1, hf * 512:(hf + 1) * 512], start=True, stop=True),
                        waits=[tcr] + C.war(7))
                    C.wrote(7, mb1)
                    tce2 = p.op("dve", lambda e, e2=e2, hf=hf: e.tensor_copy(
                        ce_s[e2][:, hf * 512:(hf + 1) * 512], C.pk(7, 128, 512)),
                        waits=C.raw(7) + [C.tok.get(("B_ce_used", e2))])
                    C.read(7, tce2)
                for pr in range(6):
                    ldt = [None, None]
                    for gu, ib in ((0, pr), (1, pr + 6)):
                        wb = (ex * 12 + ib) % 2
                        twl = p.dma("sync", wgf[wb].rearrange("p (c n) -> p c n", c=NC),
                                    C.dram["wgu_t"][ex, :, ib * 128:(ib + 1) * 128]
                                    .rearrange("(c p) n -> p c n", p=128),
                                    waits=[C.tok.get(("B_wg_used", wb))])
                        twc = p.op("act", lambda e, wb=wb: e.activation(wgb[wb], wgf[wb], AF.Copy),
                                   waits=[twl])
                        mm = None
                        for sub in range(2):
                            bk = sub
                            for ci in range(NC):
                                mm = p.op("pe", lambda e, ci=ci, wb=wb, sub=sub, bk=bk: e.matmul(
                                    C.pk(bk), wgb[wb][:, ci * 128:(ci + 1) * 128],
                                    xn2[:, ci * TH + sub * TT: ci * TH + (sub + 1) * TT],
                                    start=(ci == 0), stop=(ci == NC - 1)),
                                    waits=[twc, xn_all[sub]] + (C.war(bk) if ci == 0 else []))
                            C.wrote(bk, mm)
                        C.tok[("B_wg_used", wb)] = mm
                        if gu == 0:
                            for sub in range(2):
                                tga = p.op("act", lambda e, sub=sub: e.activation(
                                    g_act[:, sub * TT:(sub + 1) * TT], C.pk(sub), AF.Silu),
                                    waits=C.raw(sub) + [C.tok.get(("B_gact_used", sub))])
                                C.read(sub, tga)
                                ldt[sub] = tga
                        else:
                            for sub in range(2):
                                th1 = p.op("dve", lambda e, sub=sub, pr=pr, e2=e2: e.tensor_tensor(
                                    hbuf[e2][:, pr * TH + sub * TT: pr * TH + (sub + 1) * TT],
                                    g_act[:, sub * TT:(sub + 1) * TT], C.pk(sub), ALU.mult),
                                    waits=C.raw(sub) + [ldt[sub], tce2,
                                                        C.tok.get(("B_h_used", e2))])
                                C.read(sub, th1)
                                C.tok[("B_gact_used", sub)] = th1
                                th2 = p.op("dve", lambda e, sub=sub, pr=pr, e2=e2: e.tensor_tensor(
                                    hbuf[e2][:, pr * TH + sub * TT: pr * TH + (sub + 1) * TT],
                                    hbuf[e2][:, pr * TH + sub * TT: pr * TH + (sub + 1) * TT],
                                    ce_s[e2][:, sub * TT:(sub + 1) * TT], ALU.mult))
                                htoks[e2] = th2
                C.tok[("B_ce_used", e2)] = htoks[e2]
            mm = None
            for ocq in range(4):
                for e2 in range(2):
                    twl = p.dma("act", wdf.rearrange("p (ic n) -> p ic n", ic=6),
                                C.dram["wd_t"][eg * 2 + e2, :, ocq * 512:(ocq + 1) * 512]
                                .rearrange("(ic p) n -> p ic n", p=128),
                                waits=[C.tok.get(("B_wdf_used",))])
                    twc = p.op("act", lambda e, e2=e2: e.activation(
                        wdb[e2], wdf, AF.Copy), waits=[twl, C.tok.get(("B_wdb_used", e2))])
                    C.tok[("B_wdf_used",)] = twc
                    C.tok[("B_wdb_cast", e2)] = twc
                for sub in range(2):
                    for oci in range(4):
                        oc = ocq * 4 + oci
                        bk = oci
                        mm = None
                        for e2 in range(2):
                            for ic in range(6):
                                mm = p.op("pe", lambda e, e2=e2, ic=ic, oci=oci, sub=sub, bk=bk: e.matmul(
                                    C.pk(bk), wdb[e2][:, ic * 512 + oci * 128: ic * 512 + (oci + 1) * 128],
                                    hbuf[e2][:, ic * TH + sub * TT: ic * TH + (sub + 1) * TT],
                                    start=(e2 == 0 and ic == 0), stop=(e2 == 1 and ic == 5)),
                                    waits=[htoks[e2], C.tok.get(("B_wdb_cast", e2))]
                                          + (C.war(bk) if (e2 == 0 and ic == 0) else []))
                            C.tok[("B_wdb_used", e2)] = mm
                        C.wrote(bk, mm)
                        tadd = p.op("dve", lambda e, oc=oc, sub=sub, bk=bk, hh=hh: e.tensor_tensor(
                            acc[:, oc * TH + sub * TT: oc * TH + (sub + 1) * TT],
                            acc[:, oc * TH + sub * TT: oc * TH + (sub + 1) * TT],
                            C.pk(bk), ALU.add), waits=C.raw(bk))
                        C.read(bk, tadd)
                        C.tok[("B_accadd", oc, sub)] = tadd
            for e2 in range(2):
                C.tok[("B_h_used", e2)] = mm
        for oc in range(NC):
            for sub in range(2):
                t0 = hh * TH + sub * TT
                tst = p.dma("act", x2_d[oc * 128:(oc + 1) * 128, t0:t0 + TT],
                            acc[:, oc * TH + sub * TT: oc * TH + (sub + 1) * TT],
                            waits=[C.tok[("B_accadd", oc, sub)]])
                C.tok[("B_acc_done", hh, sub)] = tst
                x2_stores.append(tst)
        C.tok[("B_acc_done", 1 - hh, 0)] = x2_stores[-1]
        C.tok[("B_acc_done", 1 - hh, 1)] = x2_stores[-1]
    ar.pop()
    return x2_stores


def build_B(debug=False):
    nc = bass.Bass("TRN2", target_bir_lowering=False)
    C = Ctx(nc)
    dbg = [("dbg_combT", (32, 1024), F32), ("dbg_ce0", (128, 1024), F32),
           ("dbg_eexp", (128, 32), F32), ("dbg_mcol", (128, 8), F32),
           ("dbg_ccol", (128, 8), F32), ("dbg_acol", (128, 8), F32),
           ("dbg_cu", (128, 32), F32), ("dbg_g1", (128, 8), F32),
           ("dbg_h0", (128, 6 * 1024), BF16), ("dbg_xn2", (128, NC * 1024), BF16),
           ("dbg_gact", (128, 1024), BF16)] if debug else []
    declare_dram(C, [
        ("x1", (D, NT), F32), ("x2", (D, NT), F32),
        ("gwt", (D, 32), F32), ("wgu_t", (NE, D, 2 * EI), F32),
        ("wd_t", (NE, EI, D), F32),
    ] + DRAM_CONST + dbg, outputs=("x2",) + tuple(n for n, _, _ in dbg))
    emit_prologue(C)
    stores = emit_phaseB(C)
    if debug:
        dm = dict(dbg_combT="combT", dbg_ce0="ce0", dbg_h0="h0", dbg_xn2="xn2",
                  dbg_gact="g_act", dbg_eexp="eexp", dbg_mcol="mcol", dbg_ccol="ccol",
                  dbg_acol="acol", dbg_cu="cu", dbg_g1="g1row")
        for dn, cn in dm.items():
            b = C.dbg_b.get(cn)
            if b is not None:
                stores = stores + [C.p.dma("pool", C.dram[dn], b, waits=stores[-1:])]
    C.p.op("pool", lambda e: e.nop(), waits=stores)
    C.p.realize()
    return nc, C


B_INPUT_NAMES = ["x1", "gwt", "wgu_t", "wd_t", "gamma_t", "ident_bf", "ident_f",
                 "causal_bias", "invfreq2", "lut", "iota_row", "invfreq_row"]





def emit_phaseC(C, xin_name="x2", xout_name="x3"):
    p, ar, c = C.p, C.ar, C.c
    x2_d = C.dram[xin_name]
    x3_d = C.dram[xout_name]
    ar.push()
    wT = {}
    for nm_ in ("q", "k", "v", "o"):
        wT[nm_] = ar.alloc(NT, BF16, parts=64)
    bas = {}
    for nm_ in ("q", "k", "v", "o"):
        bas[nm_] = ar.alloc(NC * HARM, BF16)
    ar.push()
    stg = [ar.alloc(NT, F32, parts=64) for _ in range(2)]
    halfpi64 = ar.alloc(1, F32, parts=64)
    tpi = p.op("dve", lambda e: e.memset(halfpi64, float(np.pi / 2)))
    tw_toks = {}
    for nm in ("q", "k", "v", "o"):
        ta = p.dma("sync", stg[0], C.dram[f"amp_{nm}_T"], waits=[tw_toks.get(("s", 0))])
        tp2 = p.dma("sync", stg[1], C.dram[f"phase_{nm}_T"], waits=[tw_toks.get(("s", 1))])
        tc_ = p.op("act", lambda e: e.activation(stg[1], stg[1], AF.Sin, bias=halfpi64[:, 0:1]),
                   waits=[tp2, tpi])
        tw = p.op("dve", lambda e, nm=nm: e.tensor_tensor(wT[nm], stg[0], stg[1], ALU.mult),
                  waits=[ta, tc_])
        tw_toks[("s", 0)] = tw
        tw_toks[("s", 1)] = tw
        tw_toks[nm] = tw
    bstg = ar.alloc(NC * HARM, F32)
    tb_prev = None
    for nm in ("q", "k", "v", "o"):
        tl = p.dma("sync", bstg.rearrange("p (c n) -> p c n", c=NC),
                   C.dram[f"basis_{nm}_T"].rearrange("(c p) n -> p c n", p=128),
                   waits=[tb_prev])
        tb_prev = p.op("dve", lambda e, nm=nm: e.tensor_copy(bas[nm], bstg), waits=[tl])
        tw_toks[("b", nm)] = tb_prev
    ar.pop()
    res = {nm: ar.alloc(NT, BF16, parts=64) for nm in ("q", "k", "v")}
    res_o = ar.alloc(NT, F32, parts=64)
    res_ob = ar.alloc(NT, BF16, parts=64)
    sc = alloc_ln_scratch(C, TT, double=False)
    x2t = ar.alloc(NC * TT, F32)
    xn3 = ar.alloc(NC * TT, BF16)
    prev_x = []
    res_toks = []
    for ti in range(NTT):
        tx = p.dma("sync", x2t.rearrange("p (c t) -> p c t", c=NC),
                   x2_d[:, ti * TT:(ti + 1) * TT].rearrange("(c p) t -> p c t", p=128),
                   waits=prev_x)
        xin = lambda ci: x2t[:, ci * TT:(ci + 1) * TT]
        xnf = lambda ci: xn3[:, ci * TT:(ci + 1) * TT]
        xt = emit_ln(C, xin, TT, xnf, sc, lambda ci: [tx], double_gamma=False)
        prev_x = [xt[-1]]
        for bi, nm in enumerate(("q", "k", "v")):
            bk = bi
            mr = None
            for ci in range(NC):
                mr = p.op("pe", lambda e, ci=ci, nm=nm, bk=bk: e.matmul(
                    C.pk(bk, 64, TT), bas[nm][:, ci * HARM:(ci + 1) * HARM], xnf(ci),
                    start=(ci == 0), stop=(ci == NC - 1)),
                    waits=[xt[ci], tw_toks[("b", nm)]] + (C.war(bk) if ci == 0 else []))
            C.wrote(bk, mr)
            te = p.op("dve", lambda e, nm=nm, ti=ti, bk=bk: e.tensor_copy(
                res[nm][:, ti * TT:(ti + 1) * TT], C.pk(bk, 64, TT)), waits=C.raw(bk))
            C.read(bk, te)
            res_toks.append(te)
    qbuf2 = [ar.alloc(NT, BF16) for _ in range(2)]
    kbuf2 = [ar.alloc(NT, BF16) for _ in range(2)]
    vbuf = ar.alloc(TT, BF16)
    vtm2 = [ar.alloc(NT, BF16) for _ in range(2)]
    nq2 = [ar.alloc(NT, F32, parts=1) for _ in range(2)]
    kst = ar.alloc(16, F32, parts=1)
    qsq = ar.alloc(TT, BF16)
    bbuf = ar.alloc(TT, F32)
    sxb = ar.alloc(TT, F32)
    ptil = ar.alloc(NC * TT, BF16)
    lacc = ar.alloc(TT, F32)
    laccb = ar.alloc(TT, BF16)
    rl = ar.alloc(TT, F32, parts=1)
    rls = ar.alloc(TT, F32)
    obuf = ar.alloc(TT, BF16)
    negbb = ar.alloc(TT, BF16, parts=1)
    tro = p.op("dve", lambda e: e.memset(res_o, 0.0))
    C.tok[("C_resob",)] = tro

    def gen(h, slot):
        hs = slice(h * DH, (h + 1) * DH)
        kbuf, qbuf, vtm, nq = kbuf2[slot], qbuf2[slot], vtm2[slot], nq2[slot]
        gen_toks = {}
        for src_nm, dstb in (("k", kbuf), ("q", qbuf)):
            for tb in range(NTT):
                mg = p.op("pe", lambda e, src_nm=src_nm, tb=tb, hs=hs: e.matmul(
                    C.pk(2), wT[src_nm][:, hs], res[src_nm][:, tb * TT:(tb + 1) * TT],
                    start=True, stop=True),
                    waits=res_toks + [tw_toks[src_nm]] + C.war(2))
                C.wrote(2, mg)
                te = p.op("dve", lambda e, dstb=dstb, tb=tb: e.tensor_copy(
                    dstb[:, tb * TT:(tb + 1) * TT], C.pk(2)),
                    waits=C.raw(2) + [C.tok.get(("C_kq_used", slot))])
                pass
                C.read(2, te)
                gen_toks[(src_nm, tb)] = te
        for tb in range(NTT):
            mg = p.op("pe", lambda e, tb=tb, hs=hs: e.matmul(
                C.pk(2), wT["v"][:, hs], res["v"][:, tb * TT:(tb + 1) * TT],
                start=True, stop=True),
                waits=res_toks + [tw_toks["v"]] + C.war(2))
            C.wrote(2, mg)
            tv = p.op("dve", lambda e, tb=tb: e.tensor_copy(vbuf, C.pk(2)),
                      waits=C.raw(2) + [C.tok.get(("C_vb_used",))])
            C.read(2, tv)
            tvt_last = None
            for j in range(4):
                kc = tb * 4 + j
                mtr = p.op("pe", lambda e, j=j: e.transpose(
                    C.pkbf(2, 128, 128), vbuf[:, j * 128:(j + 1) * 128], c["ident_bf"]),
                    waits=[tv] + C.war(2) + C.tok["c_loads"])
                C.wrote(2, mtr)
                tvt = p.op("dve", lambda e, kc=kc, vtm=vtm: e.tensor_copy(
                    vtm[:, kc * 128:(kc + 1) * 128], C.pkbf(2, 128, 128)),
                    waits=C.raw(2) + [C.tok.get(("C_kq_used", slot))])
                C.read(2, tvt)
                tvt_last = tvt
            C.tok[("C_vb_used",)] = tvt_last
            gen_toks[("v", tb)] = tvt_last
        ks0 = slot * 8
        nk_toks = []
        for tb in range(NTT):
            t7 = p.op("act", lambda e, tb=tb, kbuf=kbuf: e.activation(
                qsq, kbuf[:, tb * TT:(tb + 1) * TT], AF.Square),
                waits=[gen_toks[("k", tb)], C.tok.get(("C_qsq_pe",))])
            m3 = p.op("pe", lambda e: e.matmul(C.pk(2, 1, TT), c["ones_col_bf"], qsq,
                                               start=True, stop=True),
                      waits=[t7] + C.war(2) + C.tok["c_dve"])
            C.wrote(2, m3)
            C.tok[("C_qsq_pe",)] = m3
            t8 = p.op("dve", lambda e, tb=tb, ks0=ks0: e.reduce_max(
                kst[0:1, ks0 + tb:ks0 + tb + 1], C.pk(2, 1, TT), axis=AX),
                waits=C.raw(2), strict=[C.tok.get(("C_kst",))])
            C.read(2, t8)
            C.tok[("C_kst",)] = t8
            nk_toks.append(t8)
        tkm = p.op("dve", lambda e, ks0=ks0: e.reduce_max(
            kst[0:1, ks0 + 4:ks0 + 5], kst[0:1, ks0:ks0 + 4], axis=AX), strict=nk_toks[-1:])
        tkm2a = p.op("act", lambda e, ks0=ks0: e.activation(
            kst[0:1, ks0 + 4:ks0 + 5], kst[0:1, ks0 + 4:ks0 + 5], AF.Sqrt), strict=[tkm])
        tkm2 = p.op("dve", lambda e, ks0=ks0: e.tensor_scalar(
            kst[0:1, ks0 + 5:ks0 + 6], kst[0:1, ks0 + 4:ks0 + 5], -1.0,
            None, ALU.mult), waits=[tkm2a])
        nq_toks = []
        for tb in range(NTT):
            t7 = p.op("act", lambda e, tb=tb, qbuf=qbuf: e.activation(
                qsq, qbuf[:, tb * TT:(tb + 1) * TT], AF.Square),
                waits=[gen_toks[("q", tb)], C.tok.get(("C_qsq_pe",))])
            m3 = p.op("pe", lambda e: e.matmul(C.pk(2, 1, TT), c["ones_col_bf"], qsq,
                                               start=True, stop=True),
                      waits=[t7] + C.war(2))
            C.wrote(2, m3)
            C.tok[("C_qsq_pe",)] = m3
            t9 = p.op("act", lambda e, tb=tb, nq=nq: e.activation(
                nq[0:1, tb * TT:(tb + 1) * TT], C.pk(2, 1, TT), AF.Sqrt), waits=C.raw(2))
            C.read(2, t9)
            nq_toks.append(t9)
        return dict(gen_toks=gen_toks, nq_toks=nq_toks, tkm2=tkm2, slot=slot, ks0=ks0)

    gd = gen(0, 0)
    for h in range(H):
        slot = gd["slot"]
        kbuf, qbuf, vtm, nq = kbuf2[slot], qbuf2[slot], vtm2[slot], nq2[slot]
        gen_toks, nq_toks, tkm2, ks0 = gd["gen_toks"], gd["nq_toks"], gd["tkm2"], gd["ks0"]
        for qt in range(NTT):
            qsl = slice(qt * TT, (qt + 1) * TT)
            if qt == 1 and h < H - 1:
                gd = gen(h + 1, 1 - slot)
            t4 = p.op("dve", lambda e, qt=qt, nq=nq, ks0=ks0: e.tensor_scalar(
                sc["st"][0], nq[0:1, qt * TT:(qt + 1) * TT], kst[0:1, ks0 + 5:ks0 + 6],
                None, ALU.mult),
                waits=[nq_toks[qt], tkm2, C.tok.get(("C_ptil_pe",))])
            t4b = p.op("dve", lambda e: e.tensor_copy(negbb, sc["st"][0]), strict=[t4])
            tl0 = p.op("dve", lambda e: e.memset(lacc, 0.0))
            stoks = []
            SBANKS = (0, 1, 6, 7)
            for kc in range(NC):
                sb = SBANKS[kc % 4]
                war = C.war(sb) if kc < 4 else [stoks[kc - 4][1]]
                ms = p.op("pe", lambda e, kc=kc, sb=sb, qsl=qsl, kbuf=kbuf, qbuf=qbuf: e.matmul(
                    C.pk(sb), kbuf[:, kc * 128:(kc + 1) * 128], qbuf[:, qsl],
                    start=True, stop=False),
                    waits=[gen_toks[("k", kc // 4)], gen_toks[("q", qt)]] + war)
                ms2 = p.op("pe", lambda e, sb=sb: e.matmul(
                    C.pk(sb), c["ones_row_bf"], negbb, start=False, stop=True),
                    waits=[t4b])
                C.wrote(sb, ms2)

                texp = p.op("act", lambda e, kc=kc, sb=sb: e.activation(
                    ptil[:, kc * TT:(kc + 1) * TT], C.pk(sb), AF.Exp, scale=SCALE),
                    waits=C.raw(sb) + [C.tok.get(("C_ptil_pe",)), C.tok.get(("C_ptil_dve",))])
                C.read(sb, texp)
                tla = p.op("dve", lambda e, kc=kc: e.tensor_tensor(
                    lacc, lacc, ptil[:, kc * TT:(kc + 1) * TT], ALU.add),
                    waits=[texp, tl0])
                stoks.append((texp, texp, tla))
            mpv = None
            for kc in range(NC):
                mpv = p.op("pe", lambda e, kc=kc, vtm=vtm: e.matmul(
                    C.pk(3), vtm[:, kc * 128:(kc + 1) * 128], ptil[:, kc * TT:(kc + 1) * TT],
                    start=(kc == 0), stop=(kc == NC - 1)),
                    waits=[stoks[kc][1], gen_toks[("v", kc // 4)]] + (C.war(3) if kc == 0 else []))
            C.wrote(3, mpv)
            C.tok[("C_ptil_pe",)] = mpv
            C.tok[("C_ptil_dve",)] = stoks[-1][2]
            tlb = p.op("dve", lambda e: e.tensor_copy(laccb, lacc), waits=[stoks[-1][2]])
            ml = p.op("pe", lambda e: e.matmul(C.pk(4, 1, TT), c["ones_col_bf"], laccb,
                                               start=True, stop=True),
                      waits=[tlb] + C.war(4))
            C.wrote(4, ml)
            trl = p.op("dve", lambda e: e.tensor_copy(rl, C.pk(4, 1, TT)), waits=C.raw(4))
            C.read(4, trl)
            trl2 = p.op("dve", lambda e: e.reciprocal(rl, rl), strict=[trl])
            mrl = p.op("pe", lambda e: e.matmul(C.pk(4, 128, TT), c["ones_row_f"], rl,
                                                start=True, stop=True),
                      waits=[trl2] + C.war(4))
            C.wrote(4, mrl)
            trle = p.op("dve", lambda e: e.tensor_copy(rls, C.pk(4, 128, TT)), waits=C.raw(4))
            C.read(4, trle)
            to = p.op("dve", lambda e: e.tensor_tensor(obuf, C.pk(3), rls, ALU.mult),
                      waits=C.raw(3) + [trle, C.tok.get(("C_obuf_pe",))])
            C.read(3, to)
            mo = p.op("pe", lambda e, h=h: e.matmul(
                C.pk(5, 64, TT), bas["o"][:, h * HARM:(h + 1) * HARM], obuf,
                start=True, stop=True),
                waits=[to, tw_toks[("b", "o")]] + C.war(5))
            C.wrote(5, mo)
            C.tok[("C_obuf_pe",)] = mo
            ta = p.op("dve", lambda e, qt=qt: e.tensor_tensor(
                res_o[:, qt * TT:(qt + 1) * TT], res_o[:, qt * TT:(qt + 1) * TT],
                C.pk(5, 64, TT), ALU.add), waits=C.raw(5) + [C.tok[("C_resob",)]])
            C.read(5, ta)
            C.tok[("C_resob",)] = ta
            if qt == NTT - 1:
                C.tok[("C_kq_used", slot)] = mpv
    x2c = [ar.alloc(TT, F32) for _ in range(2)]
    yout = [ar.alloc(TT, F32) for _ in range(2)]
    tob = []
    for tb in range(NTT):
        tob.append(p.op("dve", lambda e, tb=tb: e.tensor_copy(
            res_ob[:, tb * TT:(tb + 1) * TT], res_o[:, tb * TT:(tb + 1) * TT]),
            waits=[C.tok[("C_resob",)]]))
    x3_stores = []
    ldma = [None, None]
    sdma = [None, None]
    for oc in range(NC):
        for tb in range(NTT):
            sl_ = (oc * NTT + tb) % 2
            tl = p.dma("sync", x2c[sl_],
                       x2_d[oc * 128:(oc + 1) * 128, tb * TT:(tb + 1) * TT],
                       waits=[ldma[sl_]])
            my = p.op("pe", lambda e, oc=oc, tb=tb: e.matmul(
                C.pk(6), wT["o"][:, oc * 128:(oc + 1) * 128], res_ob[:, tb * TT:(tb + 1) * TT],
                start=True, stop=True),
                waits=[tob[tb], tw_toks["o"]] + C.war(6))
            C.wrote(6, my)
            ty = p.op("dve", lambda e, sl_=sl_: e.tensor_tensor(
                yout[sl_], C.pk(6), x2c[sl_], ALU.add),
                waits=C.raw(6) + [tl, sdma[sl_]])
            C.read(6, ty)
            ldma[sl_] = ty
            tst = p.dma("act", x3_d[oc * 128:(oc + 1) * 128, tb * TT:(tb + 1) * TT],
                        yout[sl_], waits=[ty])
            sdma[sl_] = tst
            x3_stores.append(tst)
    ar.pop()
    return x3_stores


def build_C():
    nc = bass.Bass("TRN2", target_bir_lowering=False)
    C = Ctx(nc)
    specs = [("x2", (D, NT), F32), ("x3", (D, NT), F32)]
    for nm in ("q", "k", "v", "o"):
        specs += [(f"basis_{nm}_T", (D, HARM), F32), (f"phase_{nm}_T", (HARM, D), F32),
                  (f"amp_{nm}_T", (HARM, D), F32)]
    declare_dram(C, specs + DRAM_CONST, outputs=("x3",))
    emit_prologue(C)
    stores = emit_phaseC(C)
    C.p.op("pool", lambda e: e.nop(), waits=stores)
    C.p.realize()
    return nc, C


C_INPUT_NAMES = (["x2"] + [f"{w}_{nm}_T" for nm in ("q", "k", "v", "o")
                           for w in ("basis", "phase", "amp")] +
                 ["gamma_t", "ident_bf", "ident_f", "causal_bias", "invfreq2", "lut",
                  "iota_row", "invfreq_row"])



NTILE_ALL = GOB * NC * 2 + NC * GOB
NTILE_CORE = 260


def stage_gather_shards(S):
    allt = np.concatenate([S["gidx"].reshape(-1, 128, 128),
                           S["uidx"].reshape(-1, 128, 128),
                           S["didx"].reshape(-1, 128, 128)], axis=0)
    pad = NTILE_CORE * 8 - allt.shape[0]
    allt = np.concatenate([allt, np.zeros((pad, 128, 128), np.int32)], axis=0)
    return [np.ascontiguousarray(allt[b * NTILE_CORE:(b + 1) * NTILE_CORE]) for b in range(8)]


def reassemble_gather(outs):
    allw = np.concatenate(outs, axis=0)[:NTILE_ALL]
    n1 = GOB * NC
    wg = allw[:n1].reshape(GOB, NC, 128, 128)
    wu = allw[n1:2 * n1].reshape(GOB, NC, 128, 128)
    wd = allw[2 * n1:].reshape(NC, GOB, 128, 128)
    return wg, wu, wd


def emit_gather_gen(C, idx_name, out_name, ntiles):
    p, ar, c = C.p, C.ar, C.c
    ibuf = [ar.alloc(4 * 128, I32) for _ in range(2)]
    wbuf = [ar.alloc(4 * 128, BF16) for _ in range(2)]
    term = ar.alloc(4 * 128, BF16)
    idx_d = C.dram[idx_name]
    out_d = C.dram[out_name]
    ngroups = ntiles // 4
    sts = []
    ld = [None, None]
    st = [None, None]
    for g in range(ngroups):
        sl_ = g % 2
        tl = p.dma("sync", ibuf[sl_].rearrange("p (k n) -> p k n", k=4),
                   idx_d[g * 4:(g + 1) * 4].rearrange("k p n -> p k n"), waits=[ld[sl_]])
        tacc = p.op("dve", lambda e, sl_=sl_: e.tensor_scalar(
            wbuf[sl_], ibuf[sl_], 0, c["lutcol"][:, 0:1], ALU.is_equal, ALU.mult),
            waits=[tl, C.tok["lutcol"], st[sl_]])
        for v in range(1, 16):
            p.op("dve", lambda e, v=v, sl_=sl_: e.tensor_scalar(
                term, ibuf[sl_], v, c["lutcol"][:, v:v + 1], ALU.is_equal, ALU.mult))
            tacc = p.op("dve", lambda e, sl_=sl_: e.tensor_tensor(
                wbuf[sl_], wbuf[sl_], term, ALU.add))
        ld[sl_] = tacc
        tso = p.dma("act", out_d[g * 4:(g + 1) * 4].rearrange("k p n -> p k n"),
                    wbuf[sl_].rearrange("p (k n) -> p k n", k=4), waits=[tacc])
        st[sl_] = tso
        sts.append(tso)
        C.tok["gather_stores"] = sts
        yield g
    while True:
        yield None


def emit_gather(C, idx_name, out_name, ntiles, groups=None):
    p, ar, c = C.p, C.ar, C.c
    ar.push()
    ibuf = [ar.alloc(4 * 128, I32) for _ in range(2)]
    wbuf = [ar.alloc(4 * 128, BF16) for _ in range(2)]
    term = ar.alloc(4 * 128, BF16)
    idx_d = C.dram[idx_name]
    out_d = C.dram[out_name]
    ngroups = ntiles // 4
    sts = []
    ld = [None, None]
    st = [None, None]
    for g in range(ngroups):
        sl_ = g % 2
        tl = p.dma("sync", ibuf[sl_].rearrange("p (k n) -> p k n", k=4),
                   idx_d[g * 4:(g + 1) * 4].rearrange("k p n -> p k n"), waits=[ld[sl_]])
        tacc = p.op("dve", lambda e, sl_=sl_: e.tensor_scalar(
            wbuf[sl_], ibuf[sl_], 0, c["lutcol"][:, 0:1], ALU.is_equal, ALU.mult),
            waits=[tl, C.tok["lutcol"], st[sl_]])
        for v in range(1, 16):
            tt_ = p.op("dve", lambda e, v=v, sl_=sl_: e.tensor_scalar(
                term, ibuf[sl_], v, c["lutcol"][:, v:v + 1], ALU.is_equal, ALU.mult))
            tacc = p.op("dve", lambda e, sl_=sl_: e.tensor_tensor(
                wbuf[sl_], wbuf[sl_], term, ALU.add))
        ld[sl_] = tacc
        tso = p.dma("act", out_d[g * 4:(g + 1) * 4].rearrange("k p n -> p k n"),
                    wbuf[sl_].rearrange("p (k n) -> p k n", k=4), waits=[tacc])
        st[sl_] = tso
        sts.append(tso)
    ar.pop()
    return sts


def build_G():
    nc = bass.Bass("TRN2", target_bir_lowering=False)
    C = Ctx(nc)
    declare_dram(C, [
        ("idx_shard", (NTILE_CORE, 128, 128), I32),
        ("w_shard", (NTILE_CORE, 128, 128), BF16),
    ] + DRAM_CONST, outputs=("w_shard",))
    emit_prologue(C)
    stores = emit_gather(C, "idx_shard", "w_shard", NTILE_CORE)
    C.p.op("pool", lambda e: e.nop(), waits=stores)
    C.p.realize()
    return nc, C



def emit_phaseD(C, xin_name="x3", out_name="out", wsrc=None):
    p, ar, c = C.p, C.ar, C.c
    x3_d = C.dram[xin_name]
    out_d = C.dram[out_name]
    if wsrc is None:
        wsrc = lambda kind, blk: {"g": C.dram["wg_sc"][blk].rearrange("k p n -> p k n"),
                                  "u": C.dram["wu_sc"][blk].rearrange("k p n -> p k n"),
                                  "d": C.dram["wd_sc"][blk].rearrange("k p n -> p k n")}[kind]
    ar.push()
    sc = alloc_ln_scratch(C, TT, double=False)
    x3q = ar.alloc(NC * TT, F32, name="x3q")
    xn4 = ar.alloc(NC * TT, BF16)
    hq = ar.alloc(GOB * TT, BF16, name="hq")
    wgub = [ar.alloc(NC * 128, BF16) for _ in range(2)]
    wub = [ar.alloc(NC * 128, BF16) for _ in range(2)]
    wdb = [ar.alloc(GOB * 128, BF16) for _ in range(2)]
    yout = [ar.alloc(TT, F32) for _ in range(2)]
    out_stores = []
    prev_x = []
    sdma = [None, None]
    for tq in range(NTT):
        tx = p.dma("sync", x3q.rearrange("p (c t) -> p c t", c=NC),
                   x3_d[:, tq * TT:(tq + 1) * TT].rearrange("(c p) t -> p c t", p=128),
                   waits=prev_x)
        xin = lambda ci: x3q[:, ci * TT:(ci + 1) * TT]
        xnf = lambda ci: xn4[:, ci * TT:(ci + 1) * TT]
        xt = emit_ln(C, xin, TT, xnf, sc, lambda ci: [tx], double_gamma=False)
        for ob in range(GOB):
            wb = ob % 2
            tg = p.dma("sync", wgub[wb].rearrange("p (k n) -> p k n", k=NC),
                       wsrc("g", ob),
                       waits=[C.tok.get(("D_wg_used", wb)), C.tok.get("wall_ready")])
            tu = p.dma("sync", wub[wb].rearrange("p (k n) -> p k n", k=NC),
                       wsrc("u", ob),
                       waits=[C.tok.get(("D_wu_used", wb))])
            mgx = mux = None
            for ci in range(NC):
                mgx = p.op("pe", lambda e, ci=ci, wb=wb: e.matmul(
                    C.pk(0), wgub[wb][:, ci * 128:(ci + 1) * 128], xnf(ci),
                    start=(ci == 0), stop=(ci == NC - 1)),
                    waits=[tg, xt[ci]] + (C.war(0) if ci == 0 else []))
                mux = p.op("pe", lambda e, ci=ci, wb=wb: e.matmul(
                    C.pk(1), wub[wb][:, ci * 128:(ci + 1) * 128], xnf(ci),
                    start=(ci == 0), stop=(ci == NC - 1)),
                    waits=[tu] + (C.war(1) if ci == 0 else []))
            C.wrote(0, mgx)
            C.wrote(1, mux)
            C.tok[("D_wg_used", wb)] = mgx
            C.tok[("D_wu_used", wb)] = mux
            tga = p.op("act", lambda e: e.activation(sc["tmpf"], C.pk(0), AF.Silu),
                       waits=C.raw(0) + [C.tok.get(("D_tmpf_used",))])
            C.read(0, tga)
            th = p.op("dve", lambda e, ob=ob: e.tensor_tensor(
                hq[:, ob * TT:(ob + 1) * TT], sc["tmpf"], C.pk(1), ALU.mult),
                waits=C.raw(1) + [tga, C.tok.get(("D_hq_used", ob))])
            C.read(1, th)
            C.tok[("D_tmpf_used",)] = th
            C.tok[("D_h", ob)] = th
        for oc in range(NC):
            wb = oc % 2
            td = p.dma("sync", wdb[wb].rearrange("p (k n) -> p k n", k=GOB),
                       wsrc("d", oc),
                       waits=[C.tok.get(("D_wd_used", wb))])
            my = None
            for ic in range(GOB):
                my = p.op("pe", lambda e, ic=ic, wb=wb: e.matmul(
                    C.pk(2), wdb[wb][:, ic * 128:(ic + 1) * 128], hq[:, ic * TT:(ic + 1) * TT],
                    start=(ic == 0), stop=(ic == GOB - 1)),
                    waits=[td, C.tok[("D_h", ic)]] + (C.war(2) if ic == 0 else []))
            C.wrote(2, my)
            C.tok[("D_wd_used", wb)] = my
            for ob in range(GOB):
                C.tok[("D_hq_used", ob)] = my
            ysl = oc % 2
            ty = p.op("dve", lambda e, oc=oc, ysl=ysl: e.tensor_tensor(
                yout[ysl], C.pk(2), xin(oc), ALU.add),
                waits=C.raw(2) + [sdma[ysl]])
            C.read(2, ty)
            tst = p.dma("act", out_d[oc * 128:(oc + 1) * 128, tq * TT:(tq + 1) * TT],
                        yout[ysl], waits=[ty])
            sdma[ysl] = tst
            out_stores.append(tst)
        prev_x = [ty]
    ar.pop()
    return out_stores


def build_D():
    nc = bass.Bass("TRN2", target_bir_lowering=False)
    C = Ctx(nc)
    declare_dram(C, [
        ("x3", (D, NT), F32), ("out", (D, NT), F32),
        ("wg_sc", (GOB, NC, 128, 128), BF16), ("wu_sc", (GOB, NC, 128, 128), BF16),
        ("wd_sc", (NC, GOB, 128, 128), BF16),
    ] + DRAM_CONST, outputs=("out",))
    emit_prologue(C)
    stores = emit_phaseD(C)
    C.p.op("pool", lambda e: e.nop(), waits=stores)
    C.p.realize()
    return nc, C


D_INPUT_NAMES = ["x3", "wg_sc", "wu_sc", "wd_sc", "gamma_t", "ident_bf", "ident_f",
                 "causal_bias", "invfreq2", "lut", "iota_row", "invfreq_row"]


CONST_NAMES = ["gamma_t", "ident_bf", "ident_f", "causal_bias", "invfreq2", "lut",
               "iota_row", "invfreq_row"]



def build_fused():
    nc = bass.Bass("TRN2", target_bir_lowering=False)
    C = Ctx(nc)
    specs = [
        ("x_fm", (D, NT), F32), ("out", (D, NT), F32),
        ("wq", (D, D), F32), ("wkv", (D, 256), F32), ("wo", (D, D), F32),
        ("gwt", (D, 32), F32), ("wgu_t", (NE, D, 2 * EI), F32), ("wd_t", (NE, EI, D), F32),
        ("idx_shard", (NTILE_CORE, 128, 128), I32),
    ]
    for nm in ("q", "k", "v", "o"):
        specs += [(f"basis_{nm}_T", (D, HARM), F32), (f"phase_{nm}_T", (HARM, D), F32),
                  (f"amp_{nm}_T", (HARM, D), F32)]
    declare_dram(C, specs + DRAM_CONST, outputs=("out",))
    sc_a = nc.dram_tensor("sc_a", [D, NT], F32)
    sc_b = nc.dram_tensor("sc_b", [D, NT], F32)
    C.dram["xn_sc"] = nc.dram_tensor("xn_sc", [D, NT], BF16).ap()
    wall_in = nc.dram_tensor("wall_in", [NTILE_CORE, 128, 128], BF16)
    wall_out = nc.dram_tensor("wall_out", [8 * NTILE_CORE * 128, 128], BF16, addr_space="Shared")
    C.dram["sc_a"] = sc_a.ap()
    C.dram["sc_b"] = sc_b.ap()
    C.dram["wall_in"] = wall_in.ap()
    wall_out_ap = wall_out.ap()

    emit_prologue(C)
    p = C.p
    gg = emit_gather_gen(C, "idx_shard", "wall_in", NTILE_CORE)

    def pump():
        next(gg, None)

    emit_phaseA(C, xin_name="x_fm", xout_name="sc_a", pump=pump)
    for _ in range(4):
        pump()
    p.fence()
    emit_phaseB(C, xin_name="sc_a", xout_name="sc_b")
    tcc = p.op("pool", lambda e: e.collective_compute(
        "AllGather", ALU.bypass, replica_groups=[list(range(8))],
        ins=[wall_in.ap().rearrange("t p n -> (t p) n").opt()], outs=[wall_out.ap().opt()]),
        waits=C.tok["gather_stores"])
    C.tok["wall_ready"] = tcc
    p.fence()
    emit_phaseC(C, xin_name="sc_b", xout_name="sc_a")
    p.fence()

    def wsrc(kind, blk):
        base = {"g": 0, "u": GOB * NC, "d": 2 * GOB * NC}[kind]
        n = NC if kind in ("g", "u") else GOB
        t0 = base + blk * n
        return wall_out_ap[t0 * 128:(t0 + n) * 128, :].rearrange("(k p) n -> p k n", p=128)

    stores = emit_phaseD(C, xin_name="sc_a", out_name="out", wsrc=wsrc)
    p.op("pool", lambda e: e.nop(), waits=stores)
    C.p.realize()
    return nc, C


FUSED_INPUT_NAMES = (["x_fm", "wq", "wkv", "wo", "gwt", "wgu_t", "wd_t", "idx_shard"] +
                     [f"{w}_{nm}_T" for nm in ("q", "k", "v", "o")
                      for w in ("basis", "phase", "amp")] + CONST_NAMES)


_BUILD_CACHE = {}


def _get(name, fn):
    if name not in _BUILD_CACHE:
        _BUILD_CACHE[name] = fn()
    return _BUILD_CACHE[name]


def kernel(**inputs):
    if FUSED:
        return kernel_fused(**inputs)
    return kernel_unfused(**inputs)


FUSED = True


def kernel_fused(**inputs):
    S = stage_inputs(inputs)
    shards = stage_gather_shards(S)
    nc, _ = _get("F", build_fused)
    in_maps = []
    for b in range(8):
        d = {n: S[n] for n in FUSED_INPUT_NAMES if n not in ("x_fm", "idx_shard")}
        d["x_fm"] = S["x_fm"][b]
        d["idx_shard"] = shards[b]
        in_maps.append(d)
    res = run_bass_kernel_spmd(nc, in_maps, core_ids=list(range(8)))
    return np.stack([np.ascontiguousarray(r["out"].T) for r in res.results])


def kernel_unfused(**inputs):
    S = stage_inputs(inputs)
    consts = {n: S[n] for n in CONST_NAMES}
    cores = list(range(8))

    shards = stage_gather_shards(S)
    nc_g, _ = _get("G", build_G)
    res = run_bass_kernel_spmd(nc_g, [dict(consts, idx_shard=shards[b]) for b in cores],
                               core_ids=cores)
    wg, wu, wd = reassemble_gather([r["w_shard"] for r in res.results])

    nc_a, _ = _get("A", build_A)
    amap = dict(consts, wq=S["wq"], wkv=S["wkv"], wo=S["wo"])
    res = run_bass_kernel_spmd(nc_a, [dict(amap, x_fm=S["x_fm"][b]) for b in cores],
                               core_ids=cores)
    x1 = [r["x1"] for r in res.results]

    nc_b, _ = _get("B", build_B)
    bmap = dict(consts, gwt=S["gwt"], wgu_t=S["wgu_t"], wd_t=S["wd_t"])
    res = run_bass_kernel_spmd(nc_b, [dict(bmap, x1=x1[b]) for b in cores],
                               core_ids=cores)
    x2 = [r["x2"] for r in res.results]

    nc_c, _ = _get("C", build_C)
    cmap = dict(consts)
    for nm in ("q", "k", "v", "o"):
        for w in ("basis", "phase", "amp"):
            cmap[f"{w}_{nm}_T"] = S[f"{w}_{nm}_T"]
    res = run_bass_kernel_spmd(nc_c, [dict(cmap, x2=x2[b]) for b in cores],
                               core_ids=cores)
    x3 = [r["x3"] for r in res.results]

    nc_d, _ = _get("D", build_D)
    dmap = dict(consts, wg_sc=wg, wu_sc=wu, wd_sc=wd)
    res = run_bass_kernel_spmd(nc_d, [dict(dmap, x3=x3[b]) for b in cores],
                               core_ids=cores)
    return np.stack([np.ascontiguousarray(r["out"].T) for r in res.results])


if __name__ == "__main__":
    pass


# version 07 (final)
# speedup vs baseline: 1.0305x; 1.0090x over previous
import numpy as np
import ml_dtypes
import concourse.bass as bass
import concourse.mybir as mybir
from concourse.bass_utils import run_bass_kernel_spmd
from concourse.alu_op_type import AluOpType as ALU

F32 = mybir.dt.float32
BF16 = mybir.dt.bfloat16
I32 = mybir.dt.int32
AF = mybir.ActivationFunctionType
AX = mybir.AxisListType.X

D = 2048
NT = 2048
NC = 16
TT = 512
NTT = NT // TT
H = 16
DH = 128
NE = 32
EI = 768
HARM = 64
GI = 5504
GOB = GI // 128
EPS = 1e-5
BF = ml_dtypes.bfloat16
SCALE = float(DH ** -0.5)



class Tok:
    __slots__ = ("eng", "idx", "step")
    def __init__(self, eng, idx, step):
        self.eng, self.idx, self.step = eng, idx, step


class _Eng:
    def __init__(self, name):
        self.name = name
        self.ops = []
        self.need_inc = set()
        self.sem = None


class Prog:
    def __init__(self, nc):
        self.nc = nc
        self.e = {n: _Eng(n) for n in ("pe", "dve", "act", "pool", "sync")}
        self.pending = {}
        self.consumed = set()

    def fence(self):
        self.pending = {n: Tok(n, len(st.ops) - 1, 16 if st.ops[-1][2] else 1)
                        for n, st in self.e.items() if st.ops}
        self.consumed = set()

    def op(self, eng, fn, waits=(), dma=False, strict=()):
        st = self.e[eng]
        ws = []
        for w in waits:
            if w is None:
                continue
            if isinstance(w, (list, tuple)):
                ws += [x for x in w if x is not None]
            else:
                ws.append(w)
        sws = [w for w in (strict if isinstance(strict, (list, tuple)) else [strict])
               if w is not None]
        if self.pending and eng not in self.consumed:
            ws += [t for n, t in self.pending.items() if n != eng]
            self.consumed.add(eng)
        for w in ws + sws:
            self.e[w.eng].need_inc.add(w.idx)
        st.ops.append((fn, ws, dma, sws))
        return Tok(eng, len(st.ops) - 1, 16 if dma else 1)

    def dma(self, queue, dst, src, waits=()):
        eng = {"sync": "sync", "act": "act", "pool": "pool"}[queue]
        return self.op(eng, lambda e, dst=dst, src=src: e.dma_start(dst, src),
                       waits=waits, dma=True)

    def chain(self, eng, fns, waits=()):
        t = None
        for i, fn in enumerate(fns):
            t = self.op(eng, fn, waits=(waits if i == 0 else ()),
                        strict=([t] if t is not None else ()))
        return t

    def realize(self):
        nc = self.nc
        ctxs = []
        for n, st in self.e.items():
            if st.ops:
                cm = nc.semaphore(f"sem_{n}")
                st.sem = cm.__enter__()
                ctxs.append(cm)
        cum = {}
        for n, st in self.e.items():
            for i, (fn, ws, dma, sws) in enumerate(st.ops):
                if dma:
                    st.need_inc.add(i)
            vals, c = {}, 0
            for i, (fn, ws, dma, sws) in enumerate(st.ops):
                if i in st.need_inc:
                    c += 16 if dma else 1
                vals[i] = c
            cum[n] = vals
        engmap = {"pe": "tensor", "dve": "vector", "act": "scalar",
                  "pool": "gpsimd", "sync": "sync"}
        blk = nc.Block()
        blk.__enter__()
        for n, st in self.e.items():
            if not st.ops:
                continue
            def make(st=st):
                def body(e):
                    for i, (fn, ws, dma, sws) in enumerate(st.ops):
                        wvals = {}
                        for w in ws:
                            if w.eng == st.name and not dma and w.step != 16:
                                continue
                            v = cum[w.eng][w.idx]
                            if wvals.get(w.eng, -1) < v:
                                wvals[w.eng] = v
                        for w in sws:
                            v = cum[w.eng][w.idx]
                            if wvals.get(w.eng, -1) < v:
                                wvals[w.eng] = v
                        for we, wv in wvals.items():
                            e.wait_ge(self.e[we].sem, wv)
                        inst = fn(e)
                        if i in st.need_inc:
                            inst.then_inc(st.sem, 16 if dma else 1)
                return body
            getattr(blk, engmap[n])(make())
        blk.__exit__(None, None, None)
        return ctxs


class Arena:
    def __init__(self, nc, kb=206):
        self.bytes = kb * 1024
        self.cm = nc.sbuf_tensor("arena", [128, self.bytes], mybir.dt.uint8)
        self.th = self.cm.__enter__()
        self.views = {F32: self.th.bitcast(F32).ap(),
                      BF16: self.th.bitcast(BF16).ap(),
                      I32: self.th.bitcast(I32).ap()}
        self.off = 0
        self.marks = []
        self.peak = 0

    def alloc(self, cols, dtype, parts=128, name=None):
        size = {F32: 4, BF16: 2, I32: 4}[dtype]
        align = 512 if cols * size >= 512 else 64
        self.off = (self.off + align - 1) // align * align
        self.log = getattr(self, "log", [])
        self.log.append((self.off, cols * size, name))
        assert self.off + cols * size <= self.bytes, \
            f"arena overflow: need {self.off + cols*size}, have {self.bytes}"
        v = self.views[dtype][0:parts, self.off // size: self.off // size + cols]
        self.off += cols * size
        self.peak = max(self.peak, self.off)
        return v

    def push(self):
        self.marks.append(self.off)

    def pop(self):
        self.off = self.marks.pop()


class Ctx:
    def __init__(self, nc):
        self.nc = nc
        self.p = Prog(nc)
        self.ar = Arena(nc)
        self._pc = []
        self.pb = []
        self.pbbf = []
        for i in range(8):
            cm = nc.psum_tensor(f"pb{i}", [128, 512], F32)
            h = cm.__enter__()
            self.pb.append(h)
            self.pbbf.append(h.bitcast(BF16))
            self._pc.append(cm)
        self.dram = {}
        self.tok = {}
        self.c = {}
        self.bankw = [None] * 8
        self.bankr = [dict() for _ in range(8)]

    def pk(self, i, parts=128, cols=512):
        return self.pb[i][0:parts, 0:cols]

    def pkbf(self, i, parts=128, cols=512):
        return self.pbbf[i][0:parts, 0:cols]

    def war(self, b):
        return list(self.bankr[b].values())

    def wrote(self, b, tok):
        self.bankw[b] = tok
        self.bankr[b] = {}

    def raw(self, b):
        return [self.bankw[b]]

    def read(self, b, tok):
        self.bankr[b][tok.eng] = tok


def declare_dram(C, specs, outputs=()):
    for name, shape, dt_ in specs:
        kind = "ExternalOutput" if name in outputs else "ExternalInput"
        C.dram[name] = C.nc.dram_tensor(name, list(shape), dt_, kind=kind).ap()



def stage_inputs(inputs):
    S = {}
    x = np.asarray(inputs["L0_attn_x"])
    S["x_fm"] = [np.ascontiguousarray(x[b].T) for b in range(8)]
    S["gamma_t"] = np.ascontiguousarray(
        np.asarray(inputs["L0_attn_gamma"]).reshape(NC, 128).T)
    S["wq"] = np.ascontiguousarray(inputs["L0_attn_Wq"])
    S["wkv"] = np.ascontiguousarray(inputs["L0_attn_Wkv"])
    S["wo"] = np.ascontiguousarray(inputs["L0_attn_Wo"])
    S["gwt"] = np.ascontiguousarray(np.asarray(inputs["L0_ffn_gate_weight"]).T)
    S["wgu_t"] = np.ascontiguousarray(
        np.asarray(inputs["L0_ffn_gate_up_proj"]).transpose(0, 2, 1))
    S["wd_t"] = np.ascontiguousarray(
        np.asarray(inputs["L0_ffn_down_proj"]).transpose(0, 2, 1))
    for nm in ("q", "k", "v", "o"):
        S[f"basis_{nm}_T"] = np.ascontiguousarray(np.asarray(inputs[f"L1_attn_basis_{nm}"]).T)
        S[f"phase_{nm}_T"] = np.ascontiguousarray(np.asarray(inputs[f"L1_attn_phase_{nm}"]).T)
        S[f"amp_{nm}_T"] = np.ascontiguousarray(np.asarray(inputs[f"L1_attn_amp_{nm}"]).T)
    S["lut"] = np.ascontiguousarray(np.asarray(inputs["L1_ffn_lut"]).reshape(1, 16))
    gt = np.asarray(inputs["L1_ffn_gate_idx"]).T
    ut = np.asarray(inputs["L1_ffn_up_idx"]).T
    dt_ = np.asarray(inputs["L1_ffn_down_idx"]).T
    def tile_hw(m, nr, ncb):
        return np.ascontiguousarray(
            m.reshape(nr, 128, ncb, 128).transpose(2, 0, 1, 3)).astype(np.int32)
    S["gidx"] = tile_hw(gt, NC, GOB)
    S["uidx"] = tile_hw(ut, NC, GOB)
    S["didx"] = tile_hw(dt_, GOB, NC)
    S["ident_bf"] = np.eye(128, dtype=BF)
    S["ident_f"] = np.eye(128, dtype=np.float32)
    cb = np.zeros((128, 128), np.float32)
    for p in range(128):
        cb[p, :p] = -1e30
    S["causal_bias"] = cb
    inv = 1.0 / (10000.0 ** (np.arange(0, DH, 2, dtype=np.float32) / DH))
    S["invfreq2"] = np.concatenate([inv, inv]).reshape(128, 1).astype(np.float32)
    S["iota_row"] = np.arange(NT, dtype=np.float32).reshape(1, NT)
    S["invfreq_row"] = np.concatenate([inv, inv]).reshape(1, 128).astype(np.float32)
    return S


DRAM_CONST = [
    ("gamma_t", (128, NC), F32), ("ident_bf", (128, 128), BF16),
    ("ident_f", (128, 128), F32), ("causal_bias", (128, 128), F32),
    ("invfreq2", (128, 1), F32), ("lut", (1, 16), F32), ("iota_row", (1, NT), F32),
    ("invfreq_row", (1, 128), F32),
]



def emit_prologue(C):
    p, ar, c = C.p, C.ar, C.c
    c["ones_col_bf"] = ar.alloc(1, BF16)
    c["ones_row_bf"] = ar.alloc(128, BF16, parts=1)
    c["ones_row_f"] = ar.alloc(128, F32, parts=1)
    c["eps1"] = ar.alloc(1, F32, parts=1)
    c["eps2"] = ar.alloc(1, F32, parts=1)
    p.op("dve", lambda e: e.memset(c["ones_col_bf"], 1.0))
    p.op("dve", lambda e: e.memset(c["ones_row_bf"], 1.0))
    p.op("dve", lambda e: e.memset(c["ones_row_f"], 1.0))
    p.op("dve", lambda e: e.memset(c["eps1"], EPS))
    tm = p.op("dve", lambda e: e.memset(c["eps2"], float(EPS * EPS)))
    C.tok["c_dve"] = [tm]
    loads = []
    for name, cols, dt_ in (("ident_bf", 128, BF16), ("ident_f", 128, F32),
                            ("causal", 128, F32), ("gamma", NC, F32),
                            ("invfreq2", 1, F32)):
        src = {"causal": "causal_bias", "gamma": "gamma_t"}.get(name, name)
        if src in C.dram:
            c[name] = ar.alloc(cols, dt_)
            loads.append(p.dma("sync", c[name], C.dram[src]))
    if "lut" in C.dram:
        c["lut_row"] = ar.alloc(16, F32, parts=1)
        tl = p.dma("sync", c["lut_row"], C.dram["lut"])
        c["lutcol"] = ar.alloc(16, F32)
        mm = p.op("pe", lambda e: e.matmul(C.pk(7, 128, 16), c["ones_row_f"], c["lut_row"],
                                           start=True, stop=True),
                  waits=[tl, tm] + C.war(7))
        C.wrote(7, mm)
        tc = p.op("dve", lambda e: e.tensor_copy(c["lutcol"], C.pk(7, 128, 16)),
                  waits=C.raw(7))
        C.read(7, tc)
        C.tok["lutcol"] = tc
        loads.append(tl)
    C.tok["c_loads"] = loads
    C.tok["consts"] = [tm] + loads


def emit_ln(C, xin, T, dst, sc, deps, double_gamma=False, pb=(4, 5), src_bf=False):
    p, c = C.p, C.c
    bA, bB = pb

    def stats(src, sdeps, is_bf):
        s1, s2 = C.pk(bA, 1, T), C.pk(bB, 1, T)
        warA, warB = C.war(bA), C.war(bB)
        mmtok = [None, None]
        for ci in range(NC):
            sl = ci % 2
            war_sl = [mmtok[1]] if ci >= 2 else []
            if is_bf:
                xb = src(ci)
                tc_ = None
            else:
                tc_ = p.op("act", lambda e, ci=ci, sl=sl: e.activation(
                    sc["xb2"][sl], src(ci), AF.Copy), waits=sdeps(ci) + war_sl)
                xb = None
            ts_ = p.op("act", lambda e, ci=ci, sl=sl: e.activation(
                sc["sq2"][sl], src(ci), AF.Square),
                waits=(sdeps(ci) + war_sl) if is_bf else [])
            m1 = p.op("pe", lambda e, ci=ci, sl=sl, xb=xb: e.matmul(
                s1, c["ones_col_bf"], xb if xb is not None else sc["xb2"][sl],
                start=(ci == 0), stop=(ci == NC - 1)),
                waits=([ts_] if is_bf else [tc_]) + ((warA + C.tok["c_dve"]) if ci == 0 else []))
            m2 = p.op("pe", lambda e, ci=ci, sl=sl: e.matmul(
                s2, c["ones_col_bf"], sc["sq2"][sl],
                start=(ci == 0), stop=(ci == NC - 1)),
                waits=[ts_] + (warB if ci == 0 else []))
            mmtok = [m1, m2]
        C.wrote(bA, mmtok[0])
        C.wrote(bB, mmtok[1])
        mu, e2, var, rstd = sc["st"]
        t1 = p.op("dve", lambda e: e.tensor_scalar(mu, s1, -1.0 / D, None, ALU.mult),
                  waits=C.raw(bA) + C.raw(bB))
        p.op("dve", lambda e: e.tensor_scalar(e2, s2, 1.0 / D, None, ALU.mult))
        p.op("dve", lambda e: e.tensor_tensor(var, mu, mu, ALU.mult))
        t4 = p.op("dve", lambda e: e.tensor_tensor(var, e2, var, ALU.subtract))
        C.read(bA, t4)
        C.read(bB, t4)
        if sc.get("double"):
            t5 = p.op("act", lambda e: e.activation(var, var, AF.Ln,
                                                    bias=c["eps2"][0:1, 0:1],
                                                    scale=float(1.0 + EPS)),
                      waits=[t4] + C.tok["c_dve"])
        else:
            t5 = p.op("act", lambda e: e.activation(var, var, AF.Ln, bias=c["eps1"][0:1, 0:1]),
                      waits=[t4] + C.tok["c_dve"])
        t6 = p.op("act", lambda e: e.activation(rstd, var, AF.Exp, scale=-0.5))
        b1 = p.op("pe", lambda e: e.matmul(C.pk(bA, 128, T), c["ones_row_f"], mu,
                                           start=True, stop=True),
                  waits=[t6] + C.war(bA))
        C.wrote(bA, b1)
        b2 = p.op("pe", lambda e: e.matmul(C.pk(bB, 128, T), c["ones_row_f"], rstd,
                                           start=True, stop=True),
                  waits=[t6] + C.war(bB))
        C.wrote(bB, b2)
        return C.pk(bA, 128, T), C.pk(bB, 128, T), [b1, b2]

    sc["double"] = double_gamma
    out = []
    mu_b, r_b, bc = stats(xin, deps, src_bf)
    last = None
    for ci in range(NC):
        p.op("dve", lambda e, ci=ci: e.tensor_tensor(sc["tmpf"], xin(ci), mu_b, ALU.add),
             waits=deps(ci) + bc)
        if double_gamma:
            p.op("dve", lambda e, ci=ci: e.tensor_tensor(sc["tmpf"], sc["tmpf"], r_b, ALU.mult))
            last = p.op("dve", lambda e, ci=ci: e.tensor_scalar(
                dst(ci), sc["tmpf"], c["gamma"][:, ci:ci + 1], None, ALU.mult))
        else:
            last = p.op("dve", lambda e, ci=ci: e.tensor_tensor(dst(ci), sc["tmpf"], r_b, ALU.mult))
        out.append(last)
    C.read(bA, last)
    C.read(bB, last)
    return out


def alloc_ln_scratch(C, T, double=False):
    ar = C.ar
    sc = {
        "xb2": [ar.alloc(T, BF16) for _ in range(2)],
        "sq2": [ar.alloc(T, BF16) for _ in range(2)],
        "tmpf": ar.alloc(T, F32),
    }
    strow = ar.alloc(4 * T, F32, parts=1)
    sc["st"] = [strow[0:1, i * T:(i + 1) * T] for i in range(4)]
    sc["strow"] = strow
    return sc


def emit_rotary_tables(C):
    p, ar, c = C.p, C.ar, C.c
    c["cos_t"] = ar.alloc(NT, F32)
    c["sinsgn_t"] = ar.alloc(NT, F32)
    ar.push()
    fr = ar.alloc(NT, F32, name="fr")
    red = ar.alloc(NT, F32, name="red")
    red2 = ar.alloc(NT, F32, name="red2")
    n32 = ar.alloc(NT, I32, name="n32")
    nf = ar.alloc(NT, F32, name="nf")
    sgn = ar.alloc(1, F32)
    iorow = ar.alloc(NT, F32, parts=1)
    ifrow = ar.alloc(128, F32, parts=1)
    tir = p.dma("sync", iorow, C.dram["iota_row"])
    tif = p.dma("sync", ifrow, C.dram["invfreq_row"])
    pos_toks = []
    for j4 in range(4):
        mmi = p.op("pe", lambda e, j4=j4: e.matmul(
            C.pk(6), ifrow, iorow[0:1, j4 * 512:(j4 + 1) * 512],
            start=True, stop=True),
            waits=[tir, tif] + C.war(6))
        C.wrote(6, mmi)
        tpc = p.op("dve", lambda e, j4=j4: e.tensor_copy(
            fr[:, j4 * 512:(j4 + 1) * 512], C.pk(6)), waits=C.raw(6))
        C.read(6, tpc)
        pos_toks.append(tpc)
    tfr = pos_toks[-1]
    TWO_PI = 2.0 * np.pi
    PI_HI = np.float32(TWO_PI)
    PI_LO = float(np.float64(TWO_PI) - np.float64(PI_HI))

    def reduce_to(dst, shift, war=()):
        p.op("dve", lambda e: e.tensor_scalar(red, fr, 1.0 / TWO_PI, shift / TWO_PI,
                                              ALU.mult, ALU.add), waits=war)
        p.op("dve", lambda e: e.tensor_copy(n32, red))
        p.op("dve", lambda e: e.tensor_copy(nf, n32))
        p.op("dve", lambda e: e.scalar_tensor_tensor(red2, nf, -float(PI_HI), fr,
                                                     ALU.mult, ALU.add))
        if shift:
            p.op("dve", lambda e: e.tensor_scalar(red2, red2, shift, None, ALU.add))
        p.op("dve", lambda e: e.scalar_tensor_tensor(dst, nf, -PI_LO, red2,
                                                     ALU.mult, ALU.add))
        p.op("dve", lambda e: e.tensor_scalar(nf, dst, float(np.pi), float(TWO_PI),
                                              ALU.is_gt, ALU.mult))
        return p.op("dve", lambda e: e.tensor_tensor(dst, dst, nf, ALU.subtract))

    t1 = reduce_to(red, 0.0)
    tsin = p.op("act", lambda e: e.activation(c["sinsgn_t"], red, AF.Sin), waits=[t1])
    t2 = reduce_to(red2, float(np.pi / 2), war=[tsin])
    tcos = p.op("act", lambda e: e.activation(c["cos_t"], red2, AF.Sin), waits=[t2])
    p.op("dve", lambda e: e.memset(sgn[0:64, :], -1.0))
    p.op("dve", lambda e: e.memset(sgn[64:128, :], 1.0))
    tss = p.op("dve", lambda e: e.tensor_scalar(c["sinsgn_t"], c["sinsgn_t"], sgn[:, 0:1],
                                                None, ALU.mult), waits=[tsin])
    C.tok["rotary"] = [tss, tcos]
    C.tok["rotary_temps_done"] = [tss, tcos, tfr]
    ar.pop()



def emit_phaseA(C, a1=True, xin_name="x_fm", xout_name="x1", pump=None):
    p, ar, c = C.p, C.ar, C.c
    x_in = C.dram[xin_name]
    x1_out = C.dram[xout_name]
    xn_sc = C.dram["xn_sc"]
    pump = pump or (lambda: None)
    ar.push()
    kfull = ar.alloc(NT, BF16, name="kfull")
    vtm = ar.alloc(NT, BF16, name="vtm")
    kstat = ar.alloc(8, F32, parts=1, name="kstat")
    emit_rotary_tables(C)
    sc = alloc_ln_scratch(C, TT, double=True)
    x_tile = ar.alloc(NC * TT, F32, name="x_tile")
    xn_tile = ar.alloc(NC * TT, BF16, name="xn_tile")
    wkvf = ar.alloc(2 * 256, F32)
    wkv_full = ar.alloc(NC * 256, BF16)
    wstream = [ar.alloc(NC * 128, F32) for _ in range(2)]
    wstream_bf = [ar.alloc(NC * 128, BF16) for _ in range(2)]
    vbf = ar.alloc(TT, BF16)
    qf = ar.alloc(TT, F32)
    qsh = ar.alloc(TT, F32)
    qsq = ar.alloc(TT, BF16)
    ptil = ar.alloc(NC * TT, BF16, name="ptil")
    lacc = ar.alloc(TT, F32, name="lacc")
    laccb = ar.alloc(TT, BF16)
    rl = ar.alloc(TT, F32, parts=1)
    rls = ar.alloc(TT, F32)
    ocat = ar.alloc(H * TT, BF16, name="ocat")
    qh2 = [ar.alloc(TT, BF16) for _ in range(2)]
    negbb2 = [ar.alloc(TT, BF16, parts=1) for _ in range(2)]
    nqrow = [ar.alloc(TT, F32, parts=1) for _ in range(2)]
    kf = ar.alloc(TT, F32)
    ksh = ar.alloc(TT, F32)
    youts = [ar.alloc(TT, F32) for _ in range(2)]

    xin = lambda ci: x_tile[:, ci * TT:(ci + 1) * TT]
    xnf = lambda ci: xn_tile[:, ci * TT:(ci + 1) * TT]
    C.dbg = dict(kfull=kfull, vtm=vtm, kstat=kstat, x_tile=x_tile, xn_tile=xn_tile,
                 ptil=ptil, lacc=lacc, ocat=ocat)

    twkv = []
    for ci in range(NC):
        sl = ci % 2
        war = [twkv[ci - 2]] if ci >= 2 else list(C.tok["rotary_temps_done"])
        tl = p.dma("sync", wkvf[:, sl * 256:(sl + 1) * 256],
                   C.dram["wkv"][ci * 128:(ci + 1) * 128, :], waits=war)
        tc_ = p.op("dve", lambda e, ci=ci, sl=sl: e.tensor_copy(
            wkv_full[:, ci * 256:(ci + 1) * 256], wkvf[:, sl * 256:(sl + 1) * 256]),
            waits=[tl])
        twkv.append(tc_)

    prev_x_readers = list(C.tok["rotary_temps_done"])
    ksq_toks = []
    kv_ready = []
    for ti in range(NTT):
        txs = []
        for ci in range(NC):
            txs.append(p.dma("sync", x_tile[:, ci * TT:(ci + 1) * TT],
                             x_in[ci * 128:(ci + 1) * 128, ti * TT:(ti + 1) * TT],
                             waits=(prev_x_readers if ci == 0 else [])))
        tx = txs[-1]
        xn_toks = emit_ln(C, xin, TT, xnf, sc, lambda ci: [txs[ci]], double_gamma=True)
        txn_st = p.dma("act", xn_sc[:, ti * TT:(ti + 1) * TT].rearrange("(c p) t -> p c t", p=128),
                       xn_tile.rearrange("p (c t) -> p c t", c=NC), waits=xn_toks)
        C.tok[("A_xnst", ti)] = txn_st
        prev_x_readers = [xn_toks[-1], txn_st]
        mk = mv = None
        for ci in range(NC):
            mk = p.op("pe", lambda e, ci=ci: e.matmul(
                C.pk(0), wkv_full[:, ci * 256:ci * 256 + 128], xnf(ci),
                start=(ci == 0), stop=(ci == NC - 1)),
                waits=[xn_toks[ci], twkv[ci]] + (C.war(0) if ci == 0 else []))
            mv = p.op("pe", lambda e, ci=ci: e.matmul(
                C.pk(1), wkv_full[:, ci * 256 + 128:ci * 256 + 256], xnf(ci),
                start=(ci == 0), stop=(ci == NC - 1)),
                waits=(C.war(1) if ci == 0 else []))
        C.wrote(0, mk)
        C.wrote(1, mv)
        csl = slice(ti * TT, (ti + 1) * TT)
        t1 = p.op("dve", lambda e: e.tensor_copy(kf, C.pk(0)), waits=C.raw(0) + C.tok["rotary"])
        C.read(0, t1)
        p.op("dve", lambda e: e.tensor_copy(ksh[0:64, :], kf[64:128, :]))
        p.op("dve", lambda e: e.tensor_copy(ksh[64:128, :], kf[0:64, :]))
        p.op("dve", lambda e, csl=csl: e.tensor_tensor(kf, kf, c["cos_t"][:, csl], ALU.mult))
        p.op("dve", lambda e, csl=csl: e.tensor_tensor(ksh, ksh, c["sinsgn_t"][:, csl], ALU.mult))
        t6 = p.op("dve", lambda e, csl=csl: e.tensor_tensor(kfull[:, csl], kf, ksh, ALU.add))
        kv_ready.append(t6)
        t7 = p.op("act", lambda e, csl=csl: e.activation(qsq, kfull[:, csl], AF.Square), waits=[t6])
        m3 = p.op("pe", lambda e: e.matmul(C.pk(2, 1, TT), c["ones_col_bf"], qsq,
                                           start=True, stop=True),
                  waits=[t7] + C.war(2))
        C.wrote(2, m3)
        t8 = p.op("dve", lambda e, ti=ti: e.reduce_max(kstat[0:1, ti:ti + 1],
                                                       C.pk(2, 1, TT), axis=AX),
                  waits=C.raw(2))
        C.read(2, t8)
        ksq_toks.append(t8)
        tv = p.op("dve", lambda e: e.tensor_copy(vbf, C.pk(1)), waits=C.raw(1))
        C.read(1, tv)
        for j in range(4):
            kc = ti * 4 + j
            mtr = p.op("pe", lambda e, j=j: e.transpose(
                C.pkbf(3, 128, 128), vbf[:, j * 128:(j + 1) * 128], c["ident_bf"]),
                waits=[tv] + C.war(3) + C.tok["c_loads"])
            C.wrote(3, mtr)
            tve = p.op("dve", lambda e, kc=kc: e.tensor_copy(
                vtm[:, kc * 128:(kc + 1) * 128], C.pkbf(3, 128, 128)),
                waits=C.raw(3))
            C.read(3, tve)
            kv_ready.append(tve)
    tkm = p.op("dve", lambda e: e.reduce_max(kstat[0:1, 4:5], kstat[0:1, 0:4], axis=AX),
               waits=ksq_toks)
    tkm2a = p.op("act", lambda e: e.activation(kstat[0:1, 4:5], kstat[0:1, 4:5], AF.Sqrt),
                 waits=[tkm])
    tkm2 = p.op("dve", lambda e: e.tensor_scalar(kstat[0:1, 5:6], kstat[0:1, 4:5], -1.0,
                                                 None, ALU.mult), waits=[tkm2a])

    if not a1:
        ar.pop()
        return [tkm2]
    first_pe_kv_wait = kv_ready
    x1_stores = []
    ydma = [None, None]
    for qt in range(NTT):
        tx = p.dma("sync", x_tile.rearrange("p (c t) -> p c t", c=NC),
                   x_in[:, qt * TT:(qt + 1) * TT].rearrange("(c p) t -> p c t", p=128),
                   waits=prev_x_readers)
        txn = p.dma("sync", xn_tile.rearrange("p (c t) -> p c t", c=NC),
                    xn_sc[:, qt * TT:(qt + 1) * TT].rearrange("(c p) t -> p c t", p=128),
                    waits=[C.tok[("A_xnst", qt)], C.tok.get(("A_xn_used",))])
        xn_toks = [txn] * NC
        nkc = (qt + 1) * 4
        qsl = slice(qt * TT, (qt + 1) * TT)
        ocat_toks = [None] * H

        def qgen(h, slot):
            wb = h % 2
            qbank = 6 + slot
            twl = p.dma("sync", wstream[wb].rearrange("p (c n) -> p c n", c=NC),
                        C.dram["wq"][:, h * DH:(h + 1) * DH].rearrange("(c p) n -> p c n", p=128),
                        waits=[C.tok.get(("A_ws", wb))])
            twc = p.op("dve", lambda e, wb=wb: e.tensor_copy(wstream_bf[wb], wstream[wb]),
                       waits=[twl])
            mq = None
            for ci in range(NC):
                mq = p.op("pe", lambda e, ci=ci, wb=wb, qbank=qbank: e.matmul(
                    C.pk(qbank), wstream_bf[wb][:, ci * 128:(ci + 1) * 128], xnf(ci),
                    start=(ci == 0), stop=(ci == NC - 1)),
                    waits=[twc, xn_toks[ci]] + (C.war(qbank) if ci == 0 else [])
                          + (first_pe_kv_wait if (qt == 0 and h == 0 and ci == 0) else []))
            C.wrote(qbank, mq)
            C.tok[("A_ws", wb)] = mq
            t1 = p.op("dve", lambda e, qbank=qbank: e.tensor_copy(qf, C.pk(qbank)),
                      waits=C.raw(qbank))
            C.read(qbank, t1)
            p.op("dve", lambda e: e.tensor_copy(qsh[0:64, :], qf[64:128, :]))
            p.op("dve", lambda e: e.tensor_copy(qsh[64:128, :], qf[0:64, :]))
            p.op("dve", lambda e, qsl=qsl: e.tensor_tensor(qf, qf, c["cos_t"][:, qsl], ALU.mult))
            p.op("dve", lambda e, qsl=qsl: e.tensor_tensor(qsh, qsh, c["sinsgn_t"][:, qsl], ALU.mult))
            tq = p.op("dve", lambda e, slot=slot: e.tensor_tensor(qh2[slot], qf, qsh, ALU.add),
                      waits=[C.tok.get(("A_qh_pe", slot)), C.tok.get(("A_qh_act", slot))])
            t2 = p.op("act", lambda e, slot=slot: e.activation(qsq, qh2[slot], AF.Square),
                      waits=[tq, C.tok.get(("A_qsq_pe",))])
            C.tok[("A_qh_act", slot)] = t2
            m2 = p.op("pe", lambda e, qbank=qbank: e.matmul(C.pk(qbank, 1, TT),
                                                            c["ones_col_bf"], qsq,
                                                            start=True, stop=True),
                      waits=[t2] + C.war(qbank))
            C.wrote(qbank, m2)
            C.tok[("A_qsq_pe",)] = m2
            t3 = p.op("act", lambda e, slot=slot, qbank=qbank: e.activation(
                nqrow[slot], C.pk(qbank, 1, TT), AF.Sqrt), waits=C.raw(qbank))
            C.read(qbank, t3)
            t4 = p.op("dve", lambda e, slot=slot: e.tensor_scalar(
                nqrow[slot], nqrow[slot], kstat[0:1, 5:6], None, ALU.mult),
                waits=[t3, tkm2])
            t4b = p.op("dve", lambda e, slot=slot: e.tensor_copy(negbb2[slot], nqrow[slot]),
                       strict=[t4])
            return dict(tq=tq, t4b=t4b, slot=slot)

        qd = qgen(0, 0)
        for h in range(H):
            slot = qd["slot"]
            qh = qh2[slot]
            negbb = negbb2[slot]
            tq = qd["tq"]
            t4b = qd["t4b"]
            tl0 = p.op("dve", lambda e: e.memset(lacc, 0.0))
            stoks = []
            SBANKS = (0, 1, 2, 4)
            s_pe = []
            for kc in range(nkc):
                sb = SBANKS[kc % 4]
                war = C.war(sb) if kc < 4 else [stoks[kc - 4][1]]
                ms = p.op("pe", lambda e, kc=kc, sb=sb, qh=qh: e.matmul(
                    C.pk(sb), kfull[:, kc * 128:(kc + 1) * 128], qh,
                    start=True, stop=False),
                    waits=[tq] + war)
                ms2 = p.op("pe", lambda e, sb=sb, negbb=negbb: e.matmul(
                    C.pk(sb), c["ones_row_bf"], negbb, start=False, stop=True),
                    waits=[t4b])
                C.wrote(sb, ms2)
                qb0 = qt * 4
                tmask = None
                for j in range(4):
                    if kc == qb0 + j:
                        tmask = p.op("dve", lambda e, j=j, sb=sb: e.tensor_tensor(
                            C.pk(sb)[:, j * 128:(j + 1) * 128],
                            C.pk(sb)[:, j * 128:(j + 1) * 128],
                            c["causal"], ALU.add), waits=C.tok["c_loads"] + C.raw(sb))
                        C.read(sb, tmask)
                texp = p.op("act", lambda e, kc=kc, sb=sb: e.activation(
                    ptil[:, kc * TT:(kc + 1) * TT], C.pk(sb), AF.Exp, scale=SCALE),
                    waits=C.raw(sb) + [tmask, C.tok.get(("A_ptil_pe",)), C.tok.get(("A_ptil_dve",))])
                C.read(sb, texp)
                pready = texp
                for j in range(4):
                    if kc > qb0 + j:
                        pready = p.op("dve", lambda e, kc=kc, j=j: e.memset(
                            ptil[:, kc * TT + j * 128:kc * TT + (j + 1) * 128], 0),
                            waits=[texp])
                tla = p.op("dve", lambda e, kc=kc: e.tensor_tensor(
                    lacc, lacc, ptil[:, kc * TT:(kc + 1) * TT], ALU.add),
                    waits=[pready, tl0])
                stoks.append((pready, texp, tla))
            if h < H - 1:
                qd = qgen(h + 1, 1 - slot)
            mpv = None
            for kc in range(nkc):
                mpv = p.op("pe", lambda e, kc=kc: e.matmul(
                    C.pk(3), vtm[:, kc * 128:(kc + 1) * 128], ptil[:, kc * TT:(kc + 1) * TT],
                    start=(kc == 0), stop=(kc == nkc - 1)),
                    waits=[stoks[kc][0]] + (C.war(3) if kc == 0 else []))
            C.wrote(3, mpv)
            C.tok[("A_ptil_pe",)] = mpv
            C.tok[("A_qh_pe", slot)] = mpv
            C.tok[("A_ptil_dve",)] = stoks[-1][2]
            tlb = p.op("dve", lambda e: e.tensor_copy(laccb, lacc), waits=[stoks[-1][2]])
            ml = p.op("pe", lambda e: e.matmul(C.pk(5, 1, TT), c["ones_col_bf"], laccb,
                                               start=True, stop=True),
                      waits=[tlb] + C.war(5))
            C.wrote(5, ml)
            trl = p.op("dve", lambda e: e.tensor_copy(rl, C.pk(5, 1, TT)), waits=C.raw(5))
            C.read(5, trl)
            trl2 = p.op("dve", lambda e: e.reciprocal(rl, rl), strict=[trl])
            mrl = p.op("pe", lambda e: e.matmul(C.pk(5, 128, TT), c["ones_row_f"], rl,
                                                start=True, stop=True),
                      waits=[trl2] + C.war(5))
            C.wrote(5, mrl)
            trle = p.op("dve", lambda e: e.tensor_copy(rls, C.pk(5, 128, TT)), waits=C.raw(5))
            C.read(5, trle)
            to = p.op("dve", lambda e, h=h: e.tensor_tensor(
                ocat[:, h * TT:(h + 1) * TT], C.pk(3), rls, ALU.mult),
                waits=C.raw(3) + [trle, C.tok.get(("A_ocat_pe",))])
            C.read(3, to)
            ocat_toks[h] = to
            pump()
        my_last = None
        for oc in range(NC):
            wb = oc % 2
            twl = p.dma("sync", wstream[wb].rearrange("p (c n) -> p c n", c=NC),
                        C.dram["wo"][:, oc * 128:(oc + 1) * 128].rearrange("(c p) n -> p c n", p=128),
                        waits=[C.tok.get(("A_ws", wb))])
            twc = p.op("dve", lambda e, wb=wb: e.tensor_copy(wstream_bf[wb], wstream[wb]),
                       waits=[twl])
            my = None
            for ci in range(NC):
                my = p.op("pe", lambda e, ci=ci, wb=wb: e.matmul(
                    C.pk(6), wstream_bf[wb][:, ci * 128:(ci + 1) * 128],
                    ocat[:, ci * TT:(ci + 1) * TT],
                    start=(ci == 0), stop=(ci == NC - 1)),
                    waits=[twc, ocat_toks[ci]] + (C.war(6) if ci == 0 else []))
            C.wrote(6, my)
            C.tok[("A_ws", wb)] = my
            my_last = my
            ysl = oc % 2
            ty = p.op("dve", lambda e, oc=oc, ysl=ysl: e.tensor_tensor(
                youts[ysl], C.pk(6), xin(oc), ALU.add),
                waits=C.raw(6) + [ydma[ysl]])
            C.read(6, ty)
            tst = p.dma("act", x1_out[oc * 128:(oc + 1) * 128, qt * TT:(qt + 1) * TT],
                        youts[ysl], waits=[ty])
            ydma[ysl] = tst
            x1_stores.append(tst)
        C.tok[("A_ocat_pe",)] = my_last
        C.tok[("A_xn_used",)] = my_last
        prev_x_readers = [ty]
    ar.pop()
    return x1_stores



def _core_inputs(S, b, names):
    d = {}
    for n in names:
        if n == "x_fm":
            d[n] = S["x_fm"][b]
        else:
            d[n] = S[n]
    return d


def build_A(standalone=True, a1=True, debug=False):
    nc = bass.Bass("TRN2", target_bir_lowering=False)
    C = Ctx(nc)
    dbg_specs = []
    if debug:
        dbg_specs = [("dbg_k", (128, NT), BF16), ("dbg_vtm", (128, NT), BF16),
                     ("dbg_kstat", (1, 8), F32), ("dbg_cos", (128, NT), F32),
                     ("dbg_sin", (128, NT), F32), ("dbg_xn", (128, NC * TT), BF16),
                     ("dbg_qh", (128, TT), BF16), ("dbg_bbuf", (128, TT), F32),
                     ("dbg_ptil", (128, NC * TT), BF16), ("dbg_lacc", (128, TT), F32),
                     ("dbg_ocat", (128, H * TT), BF16)]
    declare_dram(C, [
        ("x_fm", (D, NT), F32), ("x1", (D, NT), F32),
        ("wq", (D, D), F32), ("wkv", (D, 256), F32), ("wo", (D, D), F32),
    ] + DRAM_CONST + dbg_specs, outputs=("x1",) + tuple(n for n, _, _ in dbg_specs))
    C.dram["xn_sc"] = nc.dram_tensor("xn_sc", [D, NT], BF16).ap()
    emit_prologue(C)
    stores = emit_phaseA(C, a1=a1)
    if debug:
        dmap = dict(dbg_k="kfull", dbg_vtm="vtm", dbg_kstat="kstat", dbg_xn="xn_tile",
                    dbg_ptil="ptil", dbg_lacc="lacc", dbg_ocat="ocat")
        for dname, cname in dmap.items():
            stores = stores + [C.p.dma("pool", C.dram[dname], C.dbg[cname], waits=stores[-1:])]
        stores = stores + [C.p.dma("pool", C.dram["dbg_cos"], C.c["cos_t"], waits=stores[-1:])]
        stores = stores + [C.p.dma("pool", C.dram["dbg_sin"], C.c["sinsgn_t"], waits=stores[-1:])]
    C.p.op("pool", lambda e: e.nop(), waits=stores)
    ctxs = C.p.realize()
    return nc, C


A_INPUT_NAMES = ["x_fm", "wq", "wkv", "wo", "gamma_t", "ident_bf", "ident_f",
                 "causal_bias", "invfreq2", "lut", "iota_row", "invfreq_row"]


def run_A(S, cores=8):
    nc, C = build_A()
    in_maps = [_core_inputs(S, b, A_INPUT_NAMES) for b in range(cores)]
    res = run_bass_kernel_spmd(nc, in_maps, core_ids=list(range(cores)))
    return [r["x1"] for r in res.results]





def emit_phaseB(C, xin_name="x1", xout_name="x2"):
    p, ar, c = C.p, C.ar, C.c
    x1_d = C.dram[xin_name]
    x2_d = C.dram[xout_name]
    TH = 1024
    ar.push()
    sc = alloc_ln_scratch(C, TT, double=False)
    acc = ar.alloc(NC * TH, F32, name="acc")
    xn2 = ar.alloc(NC * TH, BF16, name="xn2")
    gwtf = ar.alloc(NC * 32, F32)
    gwtb = ar.alloc(NC * 32, BF16)
    g_act = ar.alloc(TH, BF16, name="g_act")
    hbuf = [ar.alloc(6 * TH, BF16, name=f"h{i}") for i in range(2)]
    wgf = [ar.alloc(NC * 128, F32) for _ in range(2)]
    wgb = [ar.alloc(NC * 128, BF16) for _ in range(2)]
    wdf = ar.alloc(6 * TT, F32)
    wdb = [ar.alloc(6 * TT, BF16) for _ in range(2)]
    ce_s = [ar.alloc(TH, F32) for _ in range(2)]
    cerow = sc["strow"][0:1, 0:TH]
    combT = ar.alloc(8 * 128, F32, parts=32)
    gb = {}
    for nm, k in (("eexp", 32), ("r", 32), ("msk", 32), ("cu", 32), ("lg", 32),
                  ("mcol", 8), ("ccol", 8), ("acol", 8), ("dcol", 8)):
        gb[nm] = ar.alloc(k, F32)
    g1 = {nm: ar.alloc(1, F32) for nm in ("m1", "negm", "cS", "th", "Z", "rZ", "lnz")}

    g1row = ar.alloc(8, F32)
    C.dbg_b = dict(combT=combT, ce0=ce_s[0], h0=hbuf[0], xn2=xn2, g_act=g_act,
                   eexp=gb["eexp"], mcol=gb["mcol"], ccol=gb["ccol"], acol=gb["acol"],
                   cu=gb["cu"], g1row=g1row)
    C.dbg_g1 = (g1, g1row)
    tgw = p.dma("sync", gwtf.rearrange("p (c n) -> p c n", c=NC),
                C.dram["gwt"].rearrange("(c p) n -> p c n", p=128))
    tgwc = p.op("dve", lambda e: e.tensor_copy(gwtb, gwtf), waits=[tgw])

    BIG = 1e30
    x2_stores = []
    for hh in range(2):
        hsl = slice(hh * TH, (hh + 1) * TH)
        tacc = []
        for sub in range(2):
            t0 = hh * TH + sub * TT
            tl = p.dma("sync", acc.rearrange("p (c t) -> p c t", c=NC, t=TH)[:, :, sub * TT:(sub + 1) * TT],
                       x1_d[:, t0:t0 + TT].rearrange("(c p) t -> p c t", p=128),
                       waits=[C.tok.get(("B_acc_done", hh, sub))])
            tacc.append(tl)
        xn_toks = {}
        for sub in range(2):
            xin = lambda ci, sub=sub: acc[:, ci * TH + sub * TT: ci * TH + (sub + 1) * TT]
            xnf = lambda ci, sub=sub: xn2[:, ci * TH + sub * TT: ci * TH + (sub + 1) * TT]
            toks = emit_ln(C, xin, TT, xnf, sc, lambda ci, sub=sub: [tacc[sub]],
                           double_gamma=False)
            for ci in range(NC):
                xn_toks[(ci, sub)] = toks[ci]
        xn_all = [xn_toks[(NC - 1, 0)], xn_toks[(NC - 1, 1)]]
        tcomb = []
        for tb in range(8):
            mg = None
            for ci in range(NC):
                mg = p.op("pe", lambda e, ci=ci, tb=tb: e.matmul(
                    C.pk(4, 128, 32), xn2[:, ci * TH + tb * 128: ci * TH + (tb + 1) * 128],
                    gwtb[:, ci * 32:(ci + 1) * 32],
                    start=(ci == 0), stop=(ci == NC - 1)),
                    waits=[xn_toks[(ci, tb // 4)], tgwc] + (C.war(4) if ci == 0 else []))
            C.wrote(4, mg)
            tlg = p.op("dve", lambda e: e.tensor_copy(gb["lg"], C.pk(4, 128, 32)),
                       waits=C.raw(4))
            C.read(4, tlg)
            lastv = tlg
            def dv(fn, extra=()):
                nonlocal lastv
                lastv = p.op("dve", fn, waits=extra, strict=[lastv])
                return lastv
            dv(lambda e: e.reduce_max(g1["m1"], gb["lg"], axis=AX))
            tnm = dv(lambda e: e.tensor_scalar(g1["negm"], g1["m1"], -1.0, None, ALU.mult))
            tex = p.op("act", lambda e: e.activation(gb["eexp"], gb["lg"], AF.Exp,
                                                     bias=g1["negm"]), waits=[tnm])
            dv(lambda e: e.tensor_copy(gb["r"], gb["eexp"]), extra=[tex])
            for i in range(8):
                dv(lambda e: e.reduce_max(g1["m1"], gb["r"], axis=AX))
                dv(lambda e, i=i: e.tensor_copy(gb["mcol"][:, i:i + 1], g1["m1"]))
                if i == 0:
                    dv(lambda e: e.tensor_copy(gb["ccol"][:, 0:1], g1["m1"]))
                else:
                    dv(lambda e, i=i: e.tensor_tensor(
                        gb["ccol"][:, i:i + 1], gb["ccol"][:, i - 1:i], g1["m1"], ALU.add))
                if i < 7:
                    dv(lambda e: e.tensor_scalar(gb["msk"], gb["r"], g1["m1"], BIG,
                                                 ALU.is_ge, ALU.mult))
                    dv(lambda e: e.tensor_tensor(gb["r"], gb["r"], gb["msk"], ALU.subtract))
            dv(lambda e: e.tensor_scalar(g1["cS"], gb["ccol"][:, 7:8], 0.8, None, ALU.mult))
            dv(lambda e: e.memset(gb["acol"][:, 0:1], 1.0))
            dv(lambda e: e.tensor_scalar(gb["acol"][:, 1:8], gb["ccol"][:, 0:7],
                                         g1["cS"], 1.0, ALU.is_lt, ALU.mult))
            dv(lambda e: e.tensor_tensor(gb["dcol"][:, 0:7], gb["acol"][:, 0:7],
                                         gb["acol"][:, 1:8], ALU.subtract))
            dv(lambda e: e.tensor_copy(gb["dcol"][:, 7:8], gb["acol"][:, 7:8]))
            dv(lambda e: e.tensor_tensor(gb["dcol"], gb["mcol"], gb["dcol"], ALU.mult))
            dv(lambda e: e.tensor_reduce(g1["th"], gb["dcol"], axis=AX, op=ALU.add))
            dv(lambda e: e.tensor_scalar(gb["msk"], gb["eexp"], g1["th"], None, ALU.is_ge))
            dv(lambda e: e.tensor_tensor(gb["cu"], gb["eexp"], gb["msk"], ALU.mult))
            tz = dv(lambda e: e.tensor_reduce(g1["Z"], gb["cu"], axis=AX, op=ALU.add))
            tln = p.op("act", lambda e: e.activation(g1["lnz"], g1["Z"], AF.Ln),
                       waits=[tz], strict=[tex])
            trz = p.op("act", lambda e: e.activation(g1["rZ"], g1["lnz"], AF.Exp, scale=-1.0),
                       strict=[tln])
            tcu = dv(lambda e: e.tensor_scalar(gb["cu"], gb["cu"], g1["rZ"], None,
                                               ALU.mult), extra=[trz])
            mtr = p.op("pe", lambda e: e.transpose(C.pk(6, 32, 128), gb["cu"], c["ident_f"]),
                       waits=[tcu] + C.war(6) + C.tok["c_loads"])
            C.wrote(6, mtr)
            tct = p.op("dve", lambda e, tb=tb: e.tensor_copy(
                combT[:, tb * 128:(tb + 1) * 128], C.pk(6, 32, 128)), waits=C.raw(6))
            C.read(6, tct)
            tcomb.append(tct)
            if tb == 7:
                g1r = C.dbg_g1[1]
                for gi, nm in enumerate(("m1", "negm", "cS", "th", "Z", "rZ", "lnz")):
                    p.op("dve", lambda e, gi=gi, nm=nm: e.tensor_copy(
                        g1r[:, gi:gi + 1], g1[nm]), waits=[tct])
        for eg in range(NE // 2):
            htoks = [None, None]
            wd_toks = [None, None]
            for e2 in range(2):
                ex = eg * 2 + e2
                tce2 = None
                for hf in range(2):
                    m1_ = p.op("pe", lambda e, ex=ex, hf=hf: e.matmul(
                        C.pk(6, 1, 512), c["ident_f"][0:32, ex:ex + 1],
                        combT[:, hf * 512:(hf + 1) * 512], start=True, stop=True),
                        waits=tcomb + C.war(6))
                    C.wrote(6, m1_)
                    tcr = p.op("dve", lambda e, hf=hf: e.tensor_copy(
                        cerow[0:1, hf * 512:(hf + 1) * 512], C.pk(6, 1, 512)),
                        waits=C.raw(6))
                    C.read(6, tcr)
                    mb1 = p.op("pe", lambda e, hf=hf: e.matmul(
                        C.pk(7, 128, 512), c["ones_row_f"],
                        cerow[0:1, hf * 512:(hf + 1) * 512], start=True, stop=True),
                        waits=[tcr] + C.war(7))
                    C.wrote(7, mb1)
                    tce2 = p.op("dve", lambda e, e2=e2, hf=hf: e.tensor_copy(
                        ce_s[e2][:, hf * 512:(hf + 1) * 512], C.pk(7, 128, 512)),
                        waits=C.raw(7) + [C.tok.get(("B_ce_used", e2))])
                    C.read(7, tce2)
                for pr in range(6):
                    ldt = [None, None]
                    for gu, ib in ((0, pr), (1, pr + 6)):
                        wb = (ex * 12 + ib) % 2
                        twl = p.dma("sync", wgf[wb].rearrange("p (c n) -> p c n", c=NC),
                                    C.dram["wgu_t"][ex, :, ib * 128:(ib + 1) * 128]
                                    .rearrange("(c p) n -> p c n", p=128),
                                    waits=[C.tok.get(("B_wg_used", wb))])
                        twc = p.op("act", lambda e, wb=wb: e.activation(wgb[wb], wgf[wb], AF.Copy),
                                   waits=[twl])
                        mm = None
                        for sub in range(2):
                            bk = sub
                            for ci in range(NC):
                                mm = p.op("pe", lambda e, ci=ci, wb=wb, sub=sub, bk=bk: e.matmul(
                                    C.pk(bk), wgb[wb][:, ci * 128:(ci + 1) * 128],
                                    xn2[:, ci * TH + sub * TT: ci * TH + (sub + 1) * TT],
                                    start=(ci == 0), stop=(ci == NC - 1)),
                                    waits=[twc, xn_all[sub]] + (C.war(bk) if ci == 0 else []))
                            C.wrote(bk, mm)
                        C.tok[("B_wg_used", wb)] = mm
                        if gu == 0:
                            for sub in range(2):
                                tga = p.op("act", lambda e, sub=sub: e.activation(
                                    g_act[:, sub * TT:(sub + 1) * TT], C.pk(sub), AF.Silu),
                                    waits=C.raw(sub) + [C.tok.get(("B_gact_used", sub))])
                                C.read(sub, tga)
                                ldt[sub] = tga
                        else:
                            for sub in range(2):
                                th1 = p.op("dve", lambda e, sub=sub, pr=pr, e2=e2: e.tensor_tensor(
                                    hbuf[e2][:, pr * TH + sub * TT: pr * TH + (sub + 1) * TT],
                                    g_act[:, sub * TT:(sub + 1) * TT], C.pk(sub), ALU.mult),
                                    waits=C.raw(sub) + [ldt[sub], tce2,
                                                        C.tok.get(("B_h_used", e2))])
                                C.read(sub, th1)
                                C.tok[("B_gact_used", sub)] = th1
                                th2 = p.op("dve", lambda e, sub=sub, pr=pr, e2=e2: e.tensor_tensor(
                                    hbuf[e2][:, pr * TH + sub * TT: pr * TH + (sub + 1) * TT],
                                    hbuf[e2][:, pr * TH + sub * TT: pr * TH + (sub + 1) * TT],
                                    ce_s[e2][:, sub * TT:(sub + 1) * TT], ALU.mult))
                                htoks[e2] = th2
                C.tok[("B_ce_used", e2)] = htoks[e2]
            mm = None
            for ocq in range(4):
                for e2 in range(2):
                    twl = p.dma("act", wdf.rearrange("p (ic n) -> p ic n", ic=6),
                                C.dram["wd_t"][eg * 2 + e2, :, ocq * 512:(ocq + 1) * 512]
                                .rearrange("(ic p) n -> p ic n", p=128),
                                waits=[C.tok.get(("B_wdf_used",))])
                    twc = p.op("act", lambda e, e2=e2: e.activation(
                        wdb[e2], wdf, AF.Copy), waits=[twl, C.tok.get(("B_wdb_used", e2))])
                    C.tok[("B_wdf_used",)] = twc
                    C.tok[("B_wdb_cast", e2)] = twc
                for sub in range(2):
                    for oci in range(4):
                        oc = ocq * 4 + oci
                        bk = oci
                        mm = None
                        for e2 in range(2):
                            for ic in range(6):
                                mm = p.op("pe", lambda e, e2=e2, ic=ic, oci=oci, sub=sub, bk=bk: e.matmul(
                                    C.pk(bk), wdb[e2][:, ic * 512 + oci * 128: ic * 512 + (oci + 1) * 128],
                                    hbuf[e2][:, ic * TH + sub * TT: ic * TH + (sub + 1) * TT],
                                    start=(e2 == 0 and ic == 0), stop=(e2 == 1 and ic == 5)),
                                    waits=[htoks[e2], C.tok.get(("B_wdb_cast", e2))]
                                          + (C.war(bk) if (e2 == 0 and ic == 0) else []))
                            C.tok[("B_wdb_used", e2)] = mm
                        C.wrote(bk, mm)
                        tadd = p.op("dve", lambda e, oc=oc, sub=sub, bk=bk, hh=hh: e.tensor_tensor(
                            acc[:, oc * TH + sub * TT: oc * TH + (sub + 1) * TT],
                            acc[:, oc * TH + sub * TT: oc * TH + (sub + 1) * TT],
                            C.pk(bk), ALU.add), waits=C.raw(bk))
                        C.read(bk, tadd)
                        C.tok[("B_accadd", oc, sub)] = tadd
            for e2 in range(2):
                C.tok[("B_h_used", e2)] = mm
        for oc in range(NC):
            for sub in range(2):
                t0 = hh * TH + sub * TT
                tst = p.dma("act", x2_d[oc * 128:(oc + 1) * 128, t0:t0 + TT],
                            acc[:, oc * TH + sub * TT: oc * TH + (sub + 1) * TT],
                            waits=[C.tok[("B_accadd", oc, sub)]])
                C.tok[("B_acc_done", hh, sub)] = tst
                x2_stores.append(tst)
        C.tok[("B_acc_done", 1 - hh, 0)] = x2_stores[-1]
        C.tok[("B_acc_done", 1 - hh, 1)] = x2_stores[-1]
    ar.pop()
    return x2_stores


def build_B(debug=False):
    nc = bass.Bass("TRN2", target_bir_lowering=False)
    C = Ctx(nc)
    dbg = [("dbg_combT", (32, 1024), F32), ("dbg_ce0", (128, 1024), F32),
           ("dbg_eexp", (128, 32), F32), ("dbg_mcol", (128, 8), F32),
           ("dbg_ccol", (128, 8), F32), ("dbg_acol", (128, 8), F32),
           ("dbg_cu", (128, 32), F32), ("dbg_g1", (128, 8), F32),
           ("dbg_h0", (128, 6 * 1024), BF16), ("dbg_xn2", (128, NC * 1024), BF16),
           ("dbg_gact", (128, 1024), BF16)] if debug else []
    declare_dram(C, [
        ("x1", (D, NT), F32), ("x2", (D, NT), F32),
        ("gwt", (D, 32), F32), ("wgu_t", (NE, D, 2 * EI), F32),
        ("wd_t", (NE, EI, D), F32),
    ] + DRAM_CONST + dbg, outputs=("x2",) + tuple(n for n, _, _ in dbg))
    emit_prologue(C)
    stores = emit_phaseB(C)
    if debug:
        dm = dict(dbg_combT="combT", dbg_ce0="ce0", dbg_h0="h0", dbg_xn2="xn2",
                  dbg_gact="g_act", dbg_eexp="eexp", dbg_mcol="mcol", dbg_ccol="ccol",
                  dbg_acol="acol", dbg_cu="cu", dbg_g1="g1row")
        for dn, cn in dm.items():
            b = C.dbg_b.get(cn)
            if b is not None:
                stores = stores + [C.p.dma("pool", C.dram[dn], b, waits=stores[-1:])]
    C.p.op("pool", lambda e: e.nop(), waits=stores)
    C.p.realize()
    return nc, C


B_INPUT_NAMES = ["x1", "gwt", "wgu_t", "wd_t", "gamma_t", "ident_bf", "ident_f",
                 "causal_bias", "invfreq2", "lut", "iota_row", "invfreq_row"]





def emit_phaseC(C, xin_name="x2", xout_name="x3"):
    p, ar, c = C.p, C.ar, C.c
    x2_d = C.dram[xin_name]
    x3_d = C.dram[xout_name]
    ar.push()
    wT = {}
    for nm_ in ("q", "k", "v", "o"):
        wT[nm_] = ar.alloc(NT, BF16, parts=64)
    bas = {}
    for nm_ in ("q", "k", "v", "o"):
        bas[nm_] = ar.alloc(NC * HARM, BF16)
    ar.push()
    stg = [ar.alloc(NT, F32, parts=64) for _ in range(2)]
    halfpi64 = ar.alloc(1, F32, parts=64)
    tpi = p.op("dve", lambda e: e.memset(halfpi64, float(np.pi / 2)))
    tw_toks = {}
    for nm in ("q", "k", "v", "o"):
        ta = p.dma("sync", stg[0], C.dram[f"amp_{nm}_T"], waits=[tw_toks.get(("s", 0))])
        tp2 = p.dma("sync", stg[1], C.dram[f"phase_{nm}_T"], waits=[tw_toks.get(("s", 1))])
        tc_ = p.op("act", lambda e: e.activation(stg[1], stg[1], AF.Sin, bias=halfpi64[:, 0:1]),
                   waits=[tp2, tpi])
        tw = p.op("dve", lambda e, nm=nm: e.tensor_tensor(wT[nm], stg[0], stg[1], ALU.mult),
                  waits=[ta, tc_])
        tw_toks[("s", 0)] = tw
        tw_toks[("s", 1)] = tw
        tw_toks[nm] = tw
    bstg = ar.alloc(NC * HARM, F32)
    tb_prev = None
    for nm in ("q", "k", "v", "o"):
        tl = p.dma("sync", bstg.rearrange("p (c n) -> p c n", c=NC),
                   C.dram[f"basis_{nm}_T"].rearrange("(c p) n -> p c n", p=128),
                   waits=[tb_prev])
        tb_prev = p.op("dve", lambda e, nm=nm: e.tensor_copy(bas[nm], bstg), waits=[tl])
        tw_toks[("b", nm)] = tb_prev
    ar.pop()
    res = {nm: ar.alloc(NT, BF16, parts=64) for nm in ("q", "k", "v")}
    res_o = ar.alloc(NT, F32, parts=64)
    res_ob = ar.alloc(NT, BF16, parts=64)
    sc = alloc_ln_scratch(C, TT, double=False)
    x2t = ar.alloc(NC * TT, F32)
    xn3 = ar.alloc(NC * TT, BF16)
    prev_x = []
    res_toks = []
    for ti in range(NTT):
        tx = p.dma("sync", x2t.rearrange("p (c t) -> p c t", c=NC),
                   x2_d[:, ti * TT:(ti + 1) * TT].rearrange("(c p) t -> p c t", p=128),
                   waits=prev_x)
        xin = lambda ci: x2t[:, ci * TT:(ci + 1) * TT]
        xnf = lambda ci: xn3[:, ci * TT:(ci + 1) * TT]
        xt = emit_ln(C, xin, TT, xnf, sc, lambda ci: [tx], double_gamma=False)
        prev_x = [xt[-1]]
        for bi, nm in enumerate(("q", "k", "v")):
            bk = bi
            mr = None
            for ci in range(NC):
                mr = p.op("pe", lambda e, ci=ci, nm=nm, bk=bk: e.matmul(
                    C.pk(bk, 64, TT), bas[nm][:, ci * HARM:(ci + 1) * HARM], xnf(ci),
                    start=(ci == 0), stop=(ci == NC - 1)),
                    waits=[xt[ci], tw_toks[("b", nm)]] + (C.war(bk) if ci == 0 else []))
            C.wrote(bk, mr)
            te = p.op("dve", lambda e, nm=nm, ti=ti, bk=bk: e.tensor_copy(
                res[nm][:, ti * TT:(ti + 1) * TT], C.pk(bk, 64, TT)), waits=C.raw(bk))
            C.read(bk, te)
            res_toks.append(te)
    qbuf2 = [ar.alloc(NT, BF16) for _ in range(2)]
    kbuf2 = [ar.alloc(NT, BF16) for _ in range(2)]
    vbuf = ar.alloc(TT, BF16)
    vtm2 = [ar.alloc(NT, BF16) for _ in range(2)]
    nq2 = [ar.alloc(NT, F32, parts=1) for _ in range(2)]
    kst = ar.alloc(16, F32, parts=1)
    qsq = ar.alloc(TT, BF16)
    bbuf = ar.alloc(TT, F32)
    sxb = ar.alloc(TT, F32)
    ptil = ar.alloc(NC * TT, BF16)
    lacc = ar.alloc(TT, F32)
    laccb = ar.alloc(TT, BF16)
    rl = ar.alloc(TT, F32, parts=1)
    rls = ar.alloc(TT, F32)
    obuf = ar.alloc(TT, BF16)
    negbb = ar.alloc(TT, BF16, parts=1)
    tro = p.op("dve", lambda e: e.memset(res_o, 0.0))
    C.tok[("C_resob",)] = tro

    def gen(h, slot):
        hs = slice(h * DH, (h + 1) * DH)
        kbuf, qbuf, vtm, nq = kbuf2[slot], qbuf2[slot], vtm2[slot], nq2[slot]
        gen_toks = {}
        for src_nm, dstb in (("k", kbuf), ("q", qbuf)):
            for tb in range(NTT):
                mg = p.op("pe", lambda e, src_nm=src_nm, tb=tb, hs=hs: e.matmul(
                    C.pk(2), wT[src_nm][:, hs], res[src_nm][:, tb * TT:(tb + 1) * TT],
                    start=True, stop=True),
                    waits=res_toks + [tw_toks[src_nm]] + C.war(2))
                C.wrote(2, mg)
                te = p.op("dve", lambda e, dstb=dstb, tb=tb: e.tensor_copy(
                    dstb[:, tb * TT:(tb + 1) * TT], C.pk(2)),
                    waits=C.raw(2) + [C.tok.get(("C_kq_used", slot))])
                pass
                C.read(2, te)
                gen_toks[(src_nm, tb)] = te
        for tb in range(NTT):
            mg = p.op("pe", lambda e, tb=tb, hs=hs: e.matmul(
                C.pk(2), wT["v"][:, hs], res["v"][:, tb * TT:(tb + 1) * TT],
                start=True, stop=True),
                waits=res_toks + [tw_toks["v"]] + C.war(2))
            C.wrote(2, mg)
            tv = p.op("dve", lambda e, tb=tb: e.tensor_copy(vbuf, C.pk(2)),
                      waits=C.raw(2) + [C.tok.get(("C_vb_used",))])
            C.read(2, tv)
            tvt_last = None
            for j in range(4):
                kc = tb * 4 + j
                mtr = p.op("pe", lambda e, j=j: e.transpose(
                    C.pkbf(2, 128, 128), vbuf[:, j * 128:(j + 1) * 128], c["ident_bf"]),
                    waits=[tv] + C.war(2) + C.tok["c_loads"])
                C.wrote(2, mtr)
                tvt = p.op("dve", lambda e, kc=kc, vtm=vtm: e.tensor_copy(
                    vtm[:, kc * 128:(kc + 1) * 128], C.pkbf(2, 128, 128)),
                    waits=C.raw(2) + [C.tok.get(("C_kq_used", slot))])
                C.read(2, tvt)
                tvt_last = tvt
            C.tok[("C_vb_used",)] = tvt_last
            gen_toks[("v", tb)] = tvt_last
        ks0 = slot * 8
        nk_toks = []
        for tb in range(NTT):
            t7 = p.op("act", lambda e, tb=tb, kbuf=kbuf: e.activation(
                qsq, kbuf[:, tb * TT:(tb + 1) * TT], AF.Square),
                waits=[gen_toks[("k", tb)], C.tok.get(("C_qsq_pe",))])
            m3 = p.op("pe", lambda e: e.matmul(C.pk(2, 1, TT), c["ones_col_bf"], qsq,
                                               start=True, stop=True),
                      waits=[t7] + C.war(2) + C.tok["c_dve"])
            C.wrote(2, m3)
            C.tok[("C_qsq_pe",)] = m3
            t8 = p.op("dve", lambda e, tb=tb, ks0=ks0: e.reduce_max(
                kst[0:1, ks0 + tb:ks0 + tb + 1], C.pk(2, 1, TT), axis=AX),
                waits=C.raw(2), strict=[C.tok.get(("C_kst",))])
            C.read(2, t8)
            C.tok[("C_kst",)] = t8
            nk_toks.append(t8)
        tkm = p.op("dve", lambda e, ks0=ks0: e.reduce_max(
            kst[0:1, ks0 + 4:ks0 + 5], kst[0:1, ks0:ks0 + 4], axis=AX), strict=nk_toks[-1:])
        tkm2a = p.op("act", lambda e, ks0=ks0: e.activation(
            kst[0:1, ks0 + 4:ks0 + 5], kst[0:1, ks0 + 4:ks0 + 5], AF.Sqrt), strict=[tkm])
        tkm2 = p.op("dve", lambda e, ks0=ks0: e.tensor_scalar(
            kst[0:1, ks0 + 5:ks0 + 6], kst[0:1, ks0 + 4:ks0 + 5], -1.0,
            None, ALU.mult), waits=[tkm2a])
        nq_toks = []
        for tb in range(NTT):
            t7 = p.op("act", lambda e, tb=tb, qbuf=qbuf: e.activation(
                qsq, qbuf[:, tb * TT:(tb + 1) * TT], AF.Square),
                waits=[gen_toks[("q", tb)], C.tok.get(("C_qsq_pe",))])
            m3 = p.op("pe", lambda e: e.matmul(C.pk(2, 1, TT), c["ones_col_bf"], qsq,
                                               start=True, stop=True),
                      waits=[t7] + C.war(2))
            C.wrote(2, m3)
            C.tok[("C_qsq_pe",)] = m3
            t9 = p.op("act", lambda e, tb=tb, nq=nq: e.activation(
                nq[0:1, tb * TT:(tb + 1) * TT], C.pk(2, 1, TT), AF.Sqrt), waits=C.raw(2))
            C.read(2, t9)
            nq_toks.append(t9)
        return dict(gen_toks=gen_toks, nq_toks=nq_toks, tkm2=tkm2, slot=slot, ks0=ks0)

    gd = gen(0, 0)
    for h in range(H):
        slot = gd["slot"]
        kbuf, qbuf, vtm, nq = kbuf2[slot], qbuf2[slot], vtm2[slot], nq2[slot]
        gen_toks, nq_toks, tkm2, ks0 = gd["gen_toks"], gd["nq_toks"], gd["tkm2"], gd["ks0"]
        for qt in range(NTT):
            qsl = slice(qt * TT, (qt + 1) * TT)
            if qt == 1 and h < H - 1:
                gd = gen(h + 1, 1 - slot)
            t4 = p.op("dve", lambda e, qt=qt, nq=nq, ks0=ks0: e.tensor_scalar(
                sc["st"][0], nq[0:1, qt * TT:(qt + 1) * TT], kst[0:1, ks0 + 5:ks0 + 6],
                None, ALU.mult),
                waits=[nq_toks[qt], tkm2, C.tok.get(("C_ptil_pe",))])
            t4b = p.op("dve", lambda e: e.tensor_copy(negbb, sc["st"][0]), strict=[t4])
            tl0 = p.op("dve", lambda e: e.memset(lacc, 0.0))
            stoks = []
            SBANKS = (0, 1, 6, 7)
            for kc in range(NC):
                sb = SBANKS[kc % 4]
                war = C.war(sb) if kc < 4 else [stoks[kc - 4][1]]
                ms = p.op("pe", lambda e, kc=kc, sb=sb, qsl=qsl, kbuf=kbuf, qbuf=qbuf: e.matmul(
                    C.pk(sb), kbuf[:, kc * 128:(kc + 1) * 128], qbuf[:, qsl],
                    start=True, stop=False),
                    waits=[gen_toks[("k", kc // 4)], gen_toks[("q", qt)]] + war)
                ms2 = p.op("pe", lambda e, sb=sb: e.matmul(
                    C.pk(sb), c["ones_row_bf"], negbb, start=False, stop=True),
                    waits=[t4b])
                C.wrote(sb, ms2)

                texp = p.op("act", lambda e, kc=kc, sb=sb: e.activation(
                    ptil[:, kc * TT:(kc + 1) * TT], C.pk(sb), AF.Exp, scale=SCALE),
                    waits=C.raw(sb) + [C.tok.get(("C_ptil_pe",)), C.tok.get(("C_ptil_dve",))])
                C.read(sb, texp)
                tla = p.op("dve", lambda e, kc=kc: e.tensor_tensor(
                    lacc, lacc, ptil[:, kc * TT:(kc + 1) * TT], ALU.add),
                    waits=[texp, tl0])
                stoks.append((texp, texp, tla))
            mpv = None
            for kc in range(NC):
                mpv = p.op("pe", lambda e, kc=kc, vtm=vtm: e.matmul(
                    C.pk(3), vtm[:, kc * 128:(kc + 1) * 128], ptil[:, kc * TT:(kc + 1) * TT],
                    start=(kc == 0), stop=(kc == NC - 1)),
                    waits=[stoks[kc][1], gen_toks[("v", kc // 4)]] + (C.war(3) if kc == 0 else []))
            C.wrote(3, mpv)
            C.tok[("C_ptil_pe",)] = mpv
            C.tok[("C_ptil_dve",)] = stoks[-1][2]
            tlb = p.op("dve", lambda e: e.tensor_copy(laccb, lacc), waits=[stoks[-1][2]])
            ml = p.op("pe", lambda e: e.matmul(C.pk(4, 1, TT), c["ones_col_bf"], laccb,
                                               start=True, stop=True),
                      waits=[tlb] + C.war(4))
            C.wrote(4, ml)
            trl = p.op("dve", lambda e: e.tensor_copy(rl, C.pk(4, 1, TT)), waits=C.raw(4))
            C.read(4, trl)
            trl2 = p.op("dve", lambda e: e.reciprocal(rl, rl), strict=[trl])
            mrl = p.op("pe", lambda e: e.matmul(C.pk(4, 128, TT), c["ones_row_f"], rl,
                                                start=True, stop=True),
                      waits=[trl2] + C.war(4))
            C.wrote(4, mrl)
            trle = p.op("dve", lambda e: e.tensor_copy(rls, C.pk(4, 128, TT)), waits=C.raw(4))
            C.read(4, trle)
            to = p.op("dve", lambda e: e.tensor_tensor(obuf, C.pk(3), rls, ALU.mult),
                      waits=C.raw(3) + [trle, C.tok.get(("C_obuf_pe",))])
            C.read(3, to)
            mo = p.op("pe", lambda e, h=h: e.matmul(
                C.pk(5, 64, TT), bas["o"][:, h * HARM:(h + 1) * HARM], obuf,
                start=True, stop=True),
                waits=[to, tw_toks[("b", "o")]] + C.war(5))
            C.wrote(5, mo)
            C.tok[("C_obuf_pe",)] = mo
            ta = p.op("dve", lambda e, qt=qt: e.tensor_tensor(
                res_o[:, qt * TT:(qt + 1) * TT], res_o[:, qt * TT:(qt + 1) * TT],
                C.pk(5, 64, TT), ALU.add), waits=C.raw(5) + [C.tok[("C_resob",)]])
            C.read(5, ta)
            C.tok[("C_resob",)] = ta
            if qt == NTT - 1:
                C.tok[("C_kq_used", slot)] = mpv
    x2c = [ar.alloc(TT, F32) for _ in range(2)]
    yout = [ar.alloc(TT, F32) for _ in range(2)]
    tob = []
    for tb in range(NTT):
        tob.append(p.op("dve", lambda e, tb=tb: e.tensor_copy(
            res_ob[:, tb * TT:(tb + 1) * TT], res_o[:, tb * TT:(tb + 1) * TT]),
            waits=[C.tok[("C_resob",)]]))
    x3_stores = []
    ldma = [None, None]
    sdma = [None, None]
    for oc in range(NC):
        for tb in range(NTT):
            sl_ = (oc * NTT + tb) % 2
            tl = p.dma("sync", x2c[sl_],
                       x2_d[oc * 128:(oc + 1) * 128, tb * TT:(tb + 1) * TT],
                       waits=[ldma[sl_]])
            my = p.op("pe", lambda e, oc=oc, tb=tb: e.matmul(
                C.pk(6), wT["o"][:, oc * 128:(oc + 1) * 128], res_ob[:, tb * TT:(tb + 1) * TT],
                start=True, stop=True),
                waits=[tob[tb], tw_toks["o"]] + C.war(6))
            C.wrote(6, my)
            ty = p.op("dve", lambda e, sl_=sl_: e.tensor_tensor(
                yout[sl_], C.pk(6), x2c[sl_], ALU.add),
                waits=C.raw(6) + [tl, sdma[sl_]])
            C.read(6, ty)
            ldma[sl_] = ty
            tst = p.dma("act", x3_d[oc * 128:(oc + 1) * 128, tb * TT:(tb + 1) * TT],
                        yout[sl_], waits=[ty])
            sdma[sl_] = tst
            x3_stores.append(tst)
    ar.pop()
    return x3_stores


def build_C():
    nc = bass.Bass("TRN2", target_bir_lowering=False)
    C = Ctx(nc)
    specs = [("x2", (D, NT), F32), ("x3", (D, NT), F32)]
    for nm in ("q", "k", "v", "o"):
        specs += [(f"basis_{nm}_T", (D, HARM), F32), (f"phase_{nm}_T", (HARM, D), F32),
                  (f"amp_{nm}_T", (HARM, D), F32)]
    declare_dram(C, specs + DRAM_CONST, outputs=("x3",))
    emit_prologue(C)
    stores = emit_phaseC(C)
    C.p.op("pool", lambda e: e.nop(), waits=stores)
    C.p.realize()
    return nc, C


C_INPUT_NAMES = (["x2"] + [f"{w}_{nm}_T" for nm in ("q", "k", "v", "o")
                           for w in ("basis", "phase", "amp")] +
                 ["gamma_t", "ident_bf", "ident_f", "causal_bias", "invfreq2", "lut",
                  "iota_row", "invfreq_row"])



NTILE_ALL = GOB * NC * 2 + NC * GOB
NTILE_CORE = 260


def stage_gather_shards(S):
    allt = np.concatenate([S["gidx"].reshape(-1, 128, 128),
                           S["uidx"].reshape(-1, 128, 128),
                           S["didx"].reshape(-1, 128, 128)], axis=0)
    pad = NTILE_CORE * 8 - allt.shape[0]
    allt = np.concatenate([allt, np.zeros((pad, 128, 128), np.int32)], axis=0)
    return [np.ascontiguousarray(allt[b * NTILE_CORE:(b + 1) * NTILE_CORE]) for b in range(8)]


def reassemble_gather(outs):
    allw = np.concatenate(outs, axis=0)[:NTILE_ALL]
    n1 = GOB * NC
    wg = allw[:n1].reshape(GOB, NC, 128, 128)
    wu = allw[n1:2 * n1].reshape(GOB, NC, 128, 128)
    wd = allw[2 * n1:].reshape(NC, GOB, 128, 128)
    return wg, wu, wd


def emit_gather_gen(C, idx_name, out_name, ntiles):
    p, ar, c = C.p, C.ar, C.c
    ibuf = [ar.alloc(4 * 128, I32) for _ in range(2)]
    wbuf = [ar.alloc(4 * 128, BF16) for _ in range(2)]
    term = ar.alloc(4 * 128, BF16)
    idx_d = C.dram[idx_name]
    out_d = C.dram[out_name]
    ngroups = ntiles // 4
    sts = []
    ld = [None, None]
    st = [None, None]
    for g in range(ngroups):
        sl_ = g % 2
        tl = p.dma("sync", ibuf[sl_].rearrange("p (k n) -> p k n", k=4),
                   idx_d[g * 4:(g + 1) * 4].rearrange("k p n -> p k n"), waits=[ld[sl_]])
        tacc = p.op("dve", lambda e, sl_=sl_: e.tensor_scalar(
            wbuf[sl_], ibuf[sl_], 0, c["lutcol"][:, 0:1], ALU.is_equal, ALU.mult),
            waits=[tl, C.tok["lutcol"], st[sl_]])
        for v in range(1, 16):
            p.op("dve", lambda e, v=v, sl_=sl_: e.tensor_scalar(
                term, ibuf[sl_], v, c["lutcol"][:, v:v + 1], ALU.is_equal, ALU.mult))
            tacc = p.op("dve", lambda e, sl_=sl_: e.tensor_tensor(
                wbuf[sl_], wbuf[sl_], term, ALU.add))
        ld[sl_] = tacc
        tso = p.dma("act", out_d[g * 4:(g + 1) * 4].rearrange("k p n -> p k n"),
                    wbuf[sl_].rearrange("p (k n) -> p k n", k=4), waits=[tacc])
        st[sl_] = tso
        sts.append(tso)
        C.tok["gather_stores"] = sts
        yield g
    while True:
        yield None


def emit_gather(C, idx_name, out_name, ntiles, groups=None):
    p, ar, c = C.p, C.ar, C.c
    ar.push()
    ibuf = [ar.alloc(4 * 128, I32) for _ in range(2)]
    wbuf = [ar.alloc(4 * 128, BF16) for _ in range(2)]
    term = ar.alloc(4 * 128, BF16)
    idx_d = C.dram[idx_name]
    out_d = C.dram[out_name]
    ngroups = ntiles // 4
    sts = []
    ld = [None, None]
    st = [None, None]
    for g in range(ngroups):
        sl_ = g % 2
        tl = p.dma("sync", ibuf[sl_].rearrange("p (k n) -> p k n", k=4),
                   idx_d[g * 4:(g + 1) * 4].rearrange("k p n -> p k n"), waits=[ld[sl_]])
        tacc = p.op("dve", lambda e, sl_=sl_: e.tensor_scalar(
            wbuf[sl_], ibuf[sl_], 0, c["lutcol"][:, 0:1], ALU.is_equal, ALU.mult),
            waits=[tl, C.tok["lutcol"], st[sl_]])
        for v in range(1, 16):
            tt_ = p.op("dve", lambda e, v=v, sl_=sl_: e.tensor_scalar(
                term, ibuf[sl_], v, c["lutcol"][:, v:v + 1], ALU.is_equal, ALU.mult))
            tacc = p.op("dve", lambda e, sl_=sl_: e.tensor_tensor(
                wbuf[sl_], wbuf[sl_], term, ALU.add))
        ld[sl_] = tacc
        tso = p.dma("act", out_d[g * 4:(g + 1) * 4].rearrange("k p n -> p k n"),
                    wbuf[sl_].rearrange("p (k n) -> p k n", k=4), waits=[tacc])
        st[sl_] = tso
        sts.append(tso)
    ar.pop()
    return sts


def build_G():
    nc = bass.Bass("TRN2", target_bir_lowering=False)
    C = Ctx(nc)
    declare_dram(C, [
        ("idx_shard", (NTILE_CORE, 128, 128), I32),
        ("w_shard", (NTILE_CORE, 128, 128), BF16),
    ] + DRAM_CONST, outputs=("w_shard",))
    emit_prologue(C)
    stores = emit_gather(C, "idx_shard", "w_shard", NTILE_CORE)
    C.p.op("pool", lambda e: e.nop(), waits=stores)
    C.p.realize()
    return nc, C



def emit_phaseD(C, xin_name="x3", out_name="out", wsrc=None):
    p, ar, c = C.p, C.ar, C.c
    x3_d = C.dram[xin_name]
    out_d = C.dram[out_name]
    if wsrc is None:
        def wsrc(kind, blk):
            t = {"g": "wg_sc", "u": "wu_sc", "d": "wd_sc"}[kind]
            return C.dram[t][blk].rearrange("k p n -> p k n")
    ar.push()
    sc = alloc_ln_scratch(C, TT, double=False)
    x3q = ar.alloc(NC * TT, F32, name="x3q")
    xn4 = ar.alloc(NC * TT, BF16)
    hq = ar.alloc(GOB * TT, BF16, name="hq")
    wgub = [ar.alloc(NC * 128, BF16) for _ in range(2)]
    wub = [ar.alloc(NC * 128, BF16) for _ in range(2)]
    wdb = [ar.alloc(GOB * 128, BF16) for _ in range(2)]
    yout = [ar.alloc(TT, F32) for _ in range(2)]
    out_stores = []
    prev_x = []
    sdma = [None, None]
    for tq in range(NTT):
        tx = p.dma("sync", x3q.rearrange("p (c t) -> p c t", c=NC),
                   x3_d[:, tq * TT:(tq + 1) * TT].rearrange("(c p) t -> p c t", p=128),
                   waits=prev_x)
        xin = lambda ci: x3q[:, ci * TT:(ci + 1) * TT]
        xnf = lambda ci: xn4[:, ci * TT:(ci + 1) * TT]
        xt = emit_ln(C, xin, TT, xnf, sc, lambda ci: [tx], double_gamma=False)
        for ob in range(GOB):
            wb = ob % 2
            tg = p.dma("sync", wgub[wb].rearrange("p (k n) -> p k n", k=NC),
                       wsrc("g", ob),
                       waits=[C.tok.get(("D_wg_used", wb)), C.tok.get("wall_ready")])
            tu = p.dma("sync", wub[wb].rearrange("p (k n) -> p k n", k=NC),
                       wsrc("u", ob),
                       waits=[C.tok.get(("D_wu_used", wb))])
            mgx = mux = None
            for ci in range(NC):
                mgx = p.op("pe", lambda e, ci=ci, wb=wb: e.matmul(
                    C.pk(0), wgub[wb][:, ci * 128:(ci + 1) * 128], xnf(ci),
                    start=(ci == 0), stop=(ci == NC - 1)),
                    waits=[tg, xt[ci]] + (C.war(0) if ci == 0 else []))
                mux = p.op("pe", lambda e, ci=ci, wb=wb: e.matmul(
                    C.pk(1), wub[wb][:, ci * 128:(ci + 1) * 128], xnf(ci),
                    start=(ci == 0), stop=(ci == NC - 1)),
                    waits=[tu] + (C.war(1) if ci == 0 else []))
            C.wrote(0, mgx)
            C.wrote(1, mux)
            C.tok[("D_wg_used", wb)] = mgx
            C.tok[("D_wu_used", wb)] = mux
            tga = p.op("act", lambda e: e.activation(sc["tmpf"], C.pk(0), AF.Silu),
                       waits=C.raw(0) + [C.tok.get(("D_tmpf_used",))])
            C.read(0, tga)
            th = p.op("dve", lambda e, ob=ob: e.tensor_tensor(
                hq[:, ob * TT:(ob + 1) * TT], sc["tmpf"], C.pk(1), ALU.mult),
                waits=C.raw(1) + [tga, C.tok.get(("D_hq_used", ob))])
            C.read(1, th)
            C.tok[("D_tmpf_used",)] = th
            C.tok[("D_h", ob)] = th
        for oc in range(NC):
            wb = oc % 2
            td = p.dma("sync", wdb[wb].rearrange("p (k n) -> p k n", k=GOB),
                       wsrc("d", oc),
                       waits=[C.tok.get(("D_wd_used", wb))])
            my = None
            for ic in range(GOB):
                my = p.op("pe", lambda e, ic=ic, wb=wb: e.matmul(
                    C.pk(2), wdb[wb][:, ic * 128:(ic + 1) * 128], hq[:, ic * TT:(ic + 1) * TT],
                    start=(ic == 0), stop=(ic == GOB - 1)),
                    waits=[td, C.tok[("D_h", ic)]] + (C.war(2) if ic == 0 else []))
            C.wrote(2, my)
            C.tok[("D_wd_used", wb)] = my
            for ob in range(GOB):
                C.tok[("D_hq_used", ob)] = my
            ysl = oc % 2
            ty = p.op("dve", lambda e, oc=oc, ysl=ysl: e.tensor_tensor(
                yout[ysl], C.pk(2), xin(oc), ALU.add),
                waits=C.raw(2) + [sdma[ysl]])
            C.read(2, ty)
            tst = p.dma("act", out_d[oc * 128:(oc + 1) * 128, tq * TT:(tq + 1) * TT],
                        yout[ysl], waits=[ty])
            sdma[ysl] = tst
            out_stores.append(tst)
        prev_x = [ty]
    ar.pop()
    return out_stores


def build_D():
    nc = bass.Bass("TRN2", target_bir_lowering=False)
    C = Ctx(nc)
    declare_dram(C, [
        ("x3", (D, NT), F32), ("out", (D, NT), F32),
        ("wg_sc", (GOB, NC, 128, 128), BF16), ("wu_sc", (GOB, NC, 128, 128), BF16),
        ("wd_sc", (NC, GOB, 128, 128), BF16),
    ] + DRAM_CONST, outputs=("out",))
    emit_prologue(C)
    stores = emit_phaseD(C)
    C.p.op("pool", lambda e: e.nop(), waits=stores)
    C.p.realize()
    return nc, C


D_INPUT_NAMES = ["x3", "wg_sc", "wu_sc", "wd_sc", "gamma_t", "ident_bf", "ident_f",
                 "causal_bias", "invfreq2", "lut", "iota_row", "invfreq_row"]


CONST_NAMES = ["gamma_t", "ident_bf", "ident_f", "causal_bias", "invfreq2", "lut",
               "iota_row", "invfreq_row"]



def build_fused():
    nc = bass.Bass("TRN2", target_bir_lowering=False)
    C = Ctx(nc)
    specs = [
        ("x_fm", (D, NT), F32), ("out", (D, NT), F32),
        ("wq", (D, D), F32), ("wkv", (D, 256), F32), ("wo", (D, D), F32),
        ("gwt", (D, 32), F32), ("wgu_t", (NE, D, 2 * EI), F32), ("wd_t", (NE, EI, D), F32),
        ("idx_shard", (NTILE_CORE, 128, 128), I32),
    ]
    for nm in ("q", "k", "v", "o"):
        specs += [(f"basis_{nm}_T", (D, HARM), F32), (f"phase_{nm}_T", (HARM, D), F32),
                  (f"amp_{nm}_T", (HARM, D), F32)]
    declare_dram(C, specs + DRAM_CONST, outputs=("out",))
    sc_a = nc.dram_tensor("sc_a", [D, NT], F32)
    sc_b = nc.dram_tensor("sc_b", [D, NT], F32)
    C.dram["xn_sc"] = nc.dram_tensor("xn_sc", [D, NT], BF16).ap()
    wall_in = nc.dram_tensor("wall_in", [NTILE_CORE, 128, 128], BF16)
    wall_out = nc.dram_tensor("wall_out", [8 * NTILE_CORE * 128, 128], BF16, addr_space="Shared")
    C.dram["sc_a"] = sc_a.ap()
    C.dram["sc_b"] = sc_b.ap()
    C.dram["wall_in"] = wall_in.ap()
    wall_out_ap = wall_out.ap()

    emit_prologue(C)
    p = C.p
    gg = emit_gather_gen(C, "idx_shard", "wall_in", NTILE_CORE)

    def pump():
        next(gg, None)

    emit_phaseA(C, xin_name="x_fm", xout_name="sc_a", pump=pump)
    for _ in range(4):
        pump()
    p.fence()
    emit_phaseB(C, xin_name="sc_a", xout_name="sc_b")
    tcc = p.op("pool", lambda e: e.collective_compute(
        "AllGather", ALU.bypass, replica_groups=[list(range(8))],
        ins=[wall_in.ap().rearrange("t p n -> (t p) n").opt()], outs=[wall_out.ap().opt()]),
        waits=C.tok["gather_stores"])
    C.tok["wall_ready"] = tcc
    p.fence()
    emit_phaseC(C, xin_name="sc_b", xout_name="sc_a")
    p.fence()

    def wsrc(kind, blk):
        base = {"g": 0, "u": GOB * NC, "d": 2 * GOB * NC}[kind]
        n = NC if kind in ("g", "u") else GOB
        t0 = base + blk * n
        return wall_out_ap[t0 * 128:(t0 + n) * 128, :].rearrange("(k p) n -> p k n", p=128)

    stores = emit_phaseD(C, xin_name="sc_a", out_name="out", wsrc=wsrc)
    p.op("pool", lambda e: e.nop(), waits=stores)
    C.p.realize()
    return nc, C


FUSED_INPUT_NAMES = (["x_fm", "wq", "wkv", "wo", "gwt", "wgu_t", "wd_t", "idx_shard"] +
                     [f"{w}_{nm}_T" for nm in ("q", "k", "v", "o")
                      for w in ("basis", "phase", "amp")] + CONST_NAMES)


_BUILD_CACHE = {}


def _get(name, fn):
    if name not in _BUILD_CACHE:
        _BUILD_CACHE[name] = fn()
    return _BUILD_CACHE[name]


def kernel(**inputs):
    if FUSED:
        return kernel_fused(**inputs)
    return kernel_unfused(**inputs)


FUSED = True


def kernel_fused(**inputs):
    S = stage_inputs(inputs)
    shards = stage_gather_shards(S)
    nc, _ = _get("F", build_fused)
    in_maps = []
    for b in range(8):
        d = {n: S[n] for n in FUSED_INPUT_NAMES if n not in ("x_fm", "idx_shard")}
        d["x_fm"] = S["x_fm"][b]
        d["idx_shard"] = shards[b]
        in_maps.append(d)
    res = run_bass_kernel_spmd(nc, in_maps, core_ids=list(range(8)))
    return np.stack([np.ascontiguousarray(r["out"].T) for r in res.results])


def kernel_unfused(**inputs):
    S = stage_inputs(inputs)
    consts = {n: S[n] for n in CONST_NAMES}
    cores = list(range(8))

    shards = stage_gather_shards(S)
    nc_g, _ = _get("G", build_G)
    res = run_bass_kernel_spmd(nc_g, [dict(consts, idx_shard=shards[b]) for b in cores],
                               core_ids=cores)
    wg, wu, wd = reassemble_gather([r["w_shard"] for r in res.results])

    nc_a, _ = _get("A", build_A)
    amap = dict(consts, wq=S["wq"], wkv=S["wkv"], wo=S["wo"])
    res = run_bass_kernel_spmd(nc_a, [dict(amap, x_fm=S["x_fm"][b]) for b in cores],
                               core_ids=cores)
    x1 = [r["x1"] for r in res.results]

    nc_b, _ = _get("B", build_B)
    bmap = dict(consts, gwt=S["gwt"], wgu_t=S["wgu_t"], wd_t=S["wd_t"])
    res = run_bass_kernel_spmd(nc_b, [dict(bmap, x1=x1[b]) for b in cores],
                               core_ids=cores)
    x2 = [r["x2"] for r in res.results]

    nc_c, _ = _get("C", build_C)
    cmap = dict(consts)
    for nm in ("q", "k", "v", "o"):
        for w in ("basis", "phase", "amp"):
            cmap[f"{w}_{nm}_T"] = S[f"{w}_{nm}_T"]
    res = run_bass_kernel_spmd(nc_c, [dict(cmap, x2=x2[b]) for b in cores],
                               core_ids=cores)
    x3 = [r["x3"] for r in res.results]

    nc_d, _ = _get("D", build_D)
    dmap = dict(consts, wg_sc=wg, wu_sc=wu, wd_sc=wd)
    res = run_bass_kernel_spmd(nc_d, [dict(dmap, x3=x3[b]) for b in cores],
                               core_ids=cores)
    return np.stack([np.ascontiguousarray(r["out"].T) for r in res.results])


if __name__ == "__main__":
    pass
